# version 03
# baseline (speedup 1.0000x reference)
import contextlib
import numpy as np

import concourse.bass as bass
import concourse.mybir as mybir
from concourse.bass_utils import run_bass_kernel_spmd

F32 = mybir.dt.float32
BF16 = mybir.dt.bfloat16
AF = mybir.ActivationFunctionType
ALU = mybir.AluOpType
AX = mybir.AxisListType

B, T, D = 4, 4096, 1024
NCORE = 8
TPC = T * B // NCORE
ND = D // 128
LN_EPS = 1e-5
SHIFT = 16.0
PAIRS = [[0, 1], [2, 3], [4, 5], [6, 7]]
ALL8 = [[0, 1, 2, 3, 4, 5, 6, 7]]

_PALETTE = ["p%02d" % i for i in range(40)]


class CSem:
    def __init__(self, handle):
        self.h = handle
        self.n = 0

    def inc(self, ins, k=1):
        ins.then_inc(self.h, k)
        self.n += k
        return self.n

    def wait(self, eng, n=None):
        eng.wait_ge(self.h, self.n if n is None else n)


class Core:
    def __init__(self, scale=1):
        self.nc = bass.Bass(target_bir_lowering=False)
        self.es = contextlib.ExitStack()
        self.scale = scale
        self.tpc = TPC // scale
        self.tf = T // scale
        self.nt = self.tpc // 128
        names = ["dG", "dG2", "dW", "dM0", "dM1", "dR0", "dR1", "cc", "cV", "cS", "cT", "cG"] + _PALETTE
        self.sems = {n: CSem(self.es.enter_context(self.nc.semaphore(n)))
                     for n in names}
        self.const = {}

    def sem(self, name):
        return self.sems[name]

    def stage(self, k):
        return [self.sems[_PALETTE[i]] for i in range(k)]

    def sbuf(self, name, shape, dtype):
        return self.es.enter_context(self.nc.sbuf_tensor(name, shape, dtype))



FCONSTS = [LN_EPS, -SHIFT, 1.0 / D, -0.5, 1.0 / 32.0, -1.0 / D]


def build_consts(C, fvals=FCONSTS):
    nc = C.nc
    ident_in = nc.dram_tensor("c_ident", [128, 128], BF16, kind="ExternalInput").ap()
    identf_in = nc.dram_tensor("c_identf", [128, 128], F32, kind="ExternalInput").ap()
    C.const["ident"] = C.sbuf("ident", [128, 128], BF16)
    C.const["identf"] = C.sbuf("identf", [128, 128], F32)
    C.const["ones_row"] = C.sbuf("ones_row", [1, 128], BF16)
    C.const["ones_row_f"] = C.sbuf("ones_row_f", [1, 128], F32)
    fpool = C.sbuf("fpool", [128, max(1, len(fvals))], F32)
    dG = C.sem("dG")
    sG = C.sems[_PALETTE[-1]]
    with nc.Block() as blk:
        @blk.gpsimd
        def _(g):
            g.dma_start(out=C.const["ident"][:], in_=ident_in[:]).then_inc(dG.h, 16)
            g.dma_start(out=C.const["identf"][:], in_=identf_in[:]).then_inc(dG.h, 16)
            dG.n += 32
            dG.wait(g)
            g.memset(C.const["ones_row"][:], 1.0)
            ins = g.memset(C.const["ones_row_f"][:], 1.0)
            for i, v in enumerate(fvals):
                ins = g.memset(fpool[:, i:i + 1], float(v))
            sG.inc(ins)
    for i, v in enumerate(fvals):
        nc.const_aps.register(F32, float(v), fpool[:, i:i + 1])
    C.const_ready = (sG, sG.n)


def wait_consts(C, eng):
    s, n = C.const_ready
    s.wait(eng, n)



def ln_to_xt(C, x_src, xt, tag, affine=None, y_dram=None, wait_first=None, row0=0, nrows=None):
    assert not (affine is not None and y_dram is not None)
    nc = C.nc
    nt = (nrows // 128) if nrows is not None else C.nt
    ident = C.const["ident"]
    sLD0, sLD1, sNM, sRS, sY, sTP, sEV, sYD0, sYD1 = C.stage(9)
    sLD = [sLD0, sLD1]
    sYD = [sYD0, sYD1]
    bNM, bRS, bY, bTP, bEV = (s.n for s in (sNM, sRS, sY, sTP, sEV))
    bLD = [sLD0.n, sLD1.n]
    bYD = [sYD0.n, sYD1.n]

    with contextlib.ExitStack() as bes:
        xtile = [bes.enter_context(nc.sbuf_tensor(f"lnx{tag}{i}", [128, D], F32)) for i in range(2)]
        ytile = [bes.enter_context(nc.sbuf_tensor(f"lny{tag}{i}", [128, D], BF16)) for i in range(2)] if xt is not None else None
        yf32 = [bes.enter_context(nc.sbuf_tensor(f"lnyf{tag}{i}", [128, D], F32)) for i in range(2)] if y_dram is not None else None
        stats = bes.enter_context(nc.sbuf_tensor(f"lnst{tag}", [128, 4], F32))
        scrap = bes.enter_context(nc.sbuf_tensor(f"lnsc{tag}", [128, D], BF16))
        tp_ps = [bes.enter_context(nc.psum_tensor(f"lntp{tag}{i}", [128, 512], BF16)) for i in range(2)] if xt is not None else None
        blk = bes.enter_context(nc.Block())

        def _ystore(sync, i):
            sY.wait(sync, bY + i + 1)
            sync.dma_start(out=y_dram[row0 + i * 128:row0 + (i + 1) * 128, :],
                           in_=yf32[i % 2][:]).then_inc(sYD[i % 2].h, 16)
            sYD[i % 2].n += 16

        @blk.sync
        def _(sync):
            if wait_first is not None:
                for wf in (wait_first if isinstance(wait_first, list) else [wait_first]):
                    wf[0].wait(sync, wf[1])
            for i in range(nt):
                if i >= 2:
                    sY.wait(sync, bY + i - 1)
                sync.dma_start(out=xtile[i % 2][:],
                               in_=x_src[row0 + i * 128:row0 + (i + 1) * 128, :]).then_inc(sLD[i % 2].h, 16)
                sLD[i % 2].n += 16
                if y_dram is not None and i >= 2:
                    _ystore(sync, i - 2)
            if y_dram is not None:
                for i in range(max(0, nt - 2), nt):
                    _ystore(sync, i)
                sYD0.wait(sync)
                sYD1.wait(sync)

        cV = C.sem("cV")

        @blk.vector
        def _(v):
            for i in range(nt):
                sLD[i % 2].wait(v, bLD[i % 2] + (i // 2 + 1) * 16)
                xti = xtile[i % 2]
                cV.inc(v.reduce_sum(stats[:, 0:1], xti[:], axis=AX.X))
                cV.wait(v)
                sNM.inc(v.tensor_scalar(stats[:, 1:2], stats[:, 0:1], -1.0 / D, None, ALU.mult))
                sNM.wait(v)
                sRS.wait(v, bRS + i + 1)
                if xt is not None and i >= 2:
                    sTP.wait(v, bTP + (i - 2) * 2 + 2)
                if y_dram is not None and i >= 2:
                    sYD[i % 2].wait(v, bYD[i % 2] + (i // 2) * 16)
                if y_dram is not None:
                    ins = v.tensor_scalar(yf32[i % 2][:], xti[:], stats[:, 1:2],
                                          stats[:, 3:4], ALU.add, ALU.mult)
                    if xt is not None:
                        cV.inc(ins)
                        cV.wait(v)
                        ins = v.tensor_copy(ytile[i % 2][:], yf32[i % 2][:])
                else:
                    yi = ytile[i % 2]
                    ins = v.tensor_scalar(yi[:], xti[:], stats[:, 1:2], stats[:, 3:4],
                                          ALU.add, ALU.mult)
                    if affine is not None:
                        wb, bb = affine
                        cV.inc(ins)
                        cV.wait(v)
                        cV.inc(v.tensor_tensor(yi[:], yi[:], wb[:], ALU.mult))
                        cV.wait(v)
                        ins = v.tensor_tensor(yi[:], yi[:], bb[:], ALU.add)
                sY.inc(ins)
                if xt is not None and i >= 1:
                    _ln_evac(v, xt, tp_ps, i - 1, bTP, sTP, sEV)
            if xt is not None:
                _ln_evac(v, xt, tp_ps, nt - 1, bTP, sTP, sEV)

        cS = C.sem("cS")

        @blk.scalar
        def _(s):
            for i in range(nt):
                sNM.wait(s, bNM + i + 1)
                if i >= 1:
                    sY.wait(s, bY + i)
                cS.inc(s.activation(scrap[:], xtile[i % 2][:], AF.Square, bias=stats[:, 1:2],
                                    accum_out=stats[:, 2:3]))
                cS.wait(s)
                cS.inc(s.activation(stats[:, 3:4], stats[:, 2:3], AF.Ln, bias=LN_EPS, scale=1.0 / D))
                cS.wait(s)
                sRS.inc(s.activation(stats[:, 3:4], stats[:, 3:4], AF.Exp, scale=-0.5))

        if xt is not None:
            @blk.tensor
            def _(t):
                wait_consts(C, t)
                for i in range(nt):
                    sY.wait(t, bY + i + 1)
                    for g in range(2):
                        if i >= 1:
                            sEV.wait(t, bEV + (i - 1) * 2 + g + 1)
                        ins = None
                        for c in range(4 * g, 4 * g + 4):
                            ins = t.transpose(tp_ps[g][:, (c % 4) * 128:(c % 4) * 128 + 128],
                                              ytile[i % 2][:, c * 128:(c + 1) * 128], ident[:])
                        sTP.inc(ins)
    if xt is not None:
        return (sEV, sEV.n)
    return [(sYD0, sYD0.n), (sYD1, sYD1.n)]


def _ln_evac(v, xt, tp_ps, i, bTP, sTP, sEV):
    for g in range(2):
        sTP.wait(v, bTP + i * 2 + g + 1)
        dst = xt[:, 4 * g:4 * g + 4, i * 128:(i + 1) * 128]
        sEV.inc(v.tensor_copy(dst, tp_ps[g][:].rearrange("p (c n) -> p c n", c=4)))



def l0_attn(C, x_in, x_out, W, kv_src, kv_all, wait_first=None):
    nc = C.nc
    tpc, tf, nt = C.tpc, C.tf, C.nt
    H, HD = 16, 64
    NKC = tf // 128
    QW = min(getattr(C, "qw_cap", 1024), tpc)
    NQH = tpc // QW
    sCC = C.sem("cc")
    kv_src2d = kv_src.rearrange("(a b) -> a b", b=D)
    kv_all2d = kv_all.rearrange("(a b) -> a b", b=D)

    with contextlib.ExitStack() as es:
        wout = es.enter_context(nc.sbuf_tensor("l0_wout", [128, ND, D], BF16))
        bout = es.enter_context(nc.sbuf_tensor("l0_bout", [1, D], BF16))
        qT = es.enter_context(nc.sbuf_tensor("l0_qT", [128, ND, tpc], BF16))
        es_qkv = contextlib.ExitStack()
        wqkv = es_qkv.enter_context(nc.sbuf_tensor("l0_wqkv", [128, ND, 3 * D], BF16))
        sW = C.sem("dG")
        with nc.Block() as blk:
            @blk.gpsimd
            def _(g):
                for c in range(ND):
                    g.dma_start(out=wqkv[:, c, :], in_=W["w_qkv"][c * 128:(c + 1) * 128, :]).then_inc(sW.h, 16)
                    g.dma_start(out=wout[:, c, :], in_=W["w_out"][c * 128:(c + 1) * 128, :]).then_inc(sW.h, 16)
                g.dma_start(out=bout[:], in_=W["b_out"][None, :]).then_inc(sW.h, 16)
        sW.n += (2 * ND + 1) * 16
        wready = sW.n

        xt = es_qkv.enter_context(nc.sbuf_tensor("l0_xt", [128, ND, tpc], BF16))
        xt_sem, xt_n = ln_to_xt(C, x_in, xt, "l0a", wait_first=wait_first)

        NH2 = tpc // QW
        NQK = 2 * ND * NH2
        NSTG = ND * NH2 + nt
        with contextlib.ExitStack() as es2:
            ps_qk = [es2.enter_context(nc.psum_tensor(f"l0pqk{i}", [128, QW], F32)) for i in range(2)]
            ps_v = [es2.enter_context(nc.psum_tensor(f"l0pv{i}", [128, min(D, 1024)], F32)) for i in range(2)]
            stage = [es2.enter_context(nc.sbuf_tensor(f"l0st{i}", [128, max(QW, D)], BF16)) for i in range(2)]
            blk = es2.enter_context(nc.Block())
            sMM, sEVq, sSTa, sSTb = C.stage(4)
            sST = [sSTa, sSTb]
            bMM, bEVq = sMM.n, sEVq.n
            bST = [sSTa.n, sSTb.n]

            @blk.tensor
            def _(t):
                sW.wait(t, wready)
                xt_sem.wait(t, xt_n)
                for st in range(NQK):
                    cc, hf = st // NH2, st % NH2
                    if st >= 2:
                        sEVq.wait(t, bEVq + st - 1)
                    pb = ps_qk[st % 2]
                    ins = None
                    for dc in range(ND):
                        for j in range(QW // 512) or [0]:
                            jw = min(512, QW)
                            ins = t.matmul(pb[:, j * 512:j * 512 + jw],
                                           wqkv[:, dc, cc * 128:(cc + 1) * 128],
                                           xt[:, dc, hf * QW + j * 512:hf * QW + j * 512 + jw],
                                           start=(dc == 0), stop=(dc == ND - 1))
                    sMM.inc(ins)
                for i in range(nt):
                    if NQK + i >= 2:
                        sEVq.wait(t, bEVq + NQK + i - 1)
                    pb = ps_v[i % 2]
                    ins = None
                    for dc in range(ND):
                        for j in range(D // 512):
                            ins = t.matmul(pb[:, j * 512:(j + 1) * 512],
                                           xt[:, dc, i * 128:(i + 1) * 128],
                                           wqkv[:, dc, 2 * D + j * 512:2 * D + (j + 1) * 512],
                                           start=(dc == 0), stop=(dc == ND - 1))
                    sMM.inc(ins)

            @blk.scalar
            def _(s):
                for st in range(NQK):
                    cc, hf = st // NH2, st % NH2
                    sMM.wait(s, bMM + st + 1)
                    if cc < ND:
                        ins = s.activation(qT[:, cc, hf * QW:(hf + 1) * QW],
                                           ps_qk[st % 2][:], AF.Copy)
                    else:
                        sidx = st - ND * NH2
                        if sidx >= 2:
                            q = sidx % 2
                            sST[q].wait(s, bST[q] + (sidx // 2) * 16)
                        ins = s.activation(stage[st % 2][:, :QW], ps_qk[st % 2][:], AF.Copy)
                    sEVq.inc(ins)
                for i in range(nt):
                    sMM.wait(s, bMM + NQK + i + 1)
                    sidx = ND * NH2 + i
                    if sidx >= 2:
                        q = sidx % 2
                        sST[q].wait(s, bST[q] + (sidx // 2) * 16)
                    ins = s.activation(stage[i % 2][:, :D], ps_v[i % 2][:], AF.Copy)
                    sEVq.inc(ins)

            @blk.sync
            def _(sync):
                kT_view = kv_src[0:D * tpc].rearrange("(p t) -> p t", t=tpc)
                v_view = kv_src[D * tpc:2 * D * tpc].rearrange("(p t) -> p t", t=D)
                for st in range(ND * NH2, NQK):
                    cc, hf = st // NH2, st % NH2
                    kc = cc - ND
                    sidx = st - ND * NH2
                    sEVq.wait(sync, bEVq + st + 1)
                    sync.dma_start(out=kT_view[kc * 128:(kc + 1) * 128, hf * QW:(hf + 1) * QW],
                                   in_=stage[st % 2][:, :QW]).then_inc(sST[sidx % 2].h, 16)
                    sST[sidx % 2].n += 16
                for i in range(nt):
                    sEVq.wait(sync, bEVq + NQK + i + 1)
                    sidx = ND * NH2 + i
                    sync.dma_start(out=v_view[i * 128:(i + 1) * 128, :],
                                   in_=stage[i % 2][:, :D]).then_inc(sST[sidx % 2].h, 16)
                    sST[sidx % 2].n += 16
        kv_written = [(sSTa, sSTa.n), (sSTb, sSTb.n)]

        es_qkv.close()

        with nc.Block() as blk:
            @blk.gpsimd
            def _(g):
                for s_, n_ in kv_written:
                    s_.wait(g, n_)
                sCC.inc(g.collective_compute("AllGather", ALU.bypass,
                                             replica_groups=PAIRS,
                                             ins=[kv_src2d], outs=[kv_all2d]))
        kv_ready = sCC.n

        v65 = es.enter_context(nc.sbuf_tensor("l0_v65", [128, tf // 128, H, HD + 1], BF16))
        attnT = es.enter_context(nc.sbuf_tensor("l0_attnT", [128, ND, tpc], BF16))
        NVT = tf // 128
        half_t = NVT // 2
        with contextlib.ExitStack() as es2:
            vstage = [es2.enter_context(nc.sbuf_tensor(f"l0vst{i}", [128, D], BF16)) for i in range(2)]
            kth = es2.enter_context(nc.sbuf_tensor("l0kth", [128, tf], BF16))
            attnT64 = es2.enter_context(nc.sbuf_tensor("l0at64", [64, 2, tpc], BF16))
            probs = [es2.enter_context(nc.sbuf_tensor(f"l0pr{i}", [128, QW], BF16)) for i in range(2)]
            usb = es2.enter_context(nc.sbuf_tensor("l0usb", [64, QW], F32))
            rrow = es2.enter_context(nc.sbuf_tensor("l0rr", [1, QW], F32))
            ps_sc = [es2.enter_context(nc.psum_tensor(f"l0psc{i}", [128, QW], F32)) for i in range(2)]
            ps_u = es2.enter_context(nc.psum_tensor("l0psu", [65, QW], F32))
            ps_rb = es2.enter_context(nc.psum_tensor("l0psrb", [64, QW], F32))
            blk = es2.enter_context(nc.Block())
            (sVL0, sVL1, sVC, sVO, sKL0, sKL1, sSC, sEX, sAV, sRC, sRB, sUE, sAT) = C.stage(13)
            sMG0, sMG1 = C.sem("dM0"), C.sem("dM1")
            sMG = [sMG0, sMG1]
            sVL = [sVL0, sVL1]
            sKL = [sKL0, sKL1]
            (bVC, bVO, bSC, bEX, bAV, bRC, bRB, bUE, bAT) = (
                s.n for s in (sVC, sVO, sSC, sEX, sAV, sRC, sRB, sUE, sAT))
            bMG = [sMG0.n, sMG1.n]
            bVL = [sVL0.n, sVL1.n]
            bKL = [sKL0.n, sKL1.n]

            @blk.sync
            def _(sync):
                sCC.wait(sync, kv_ready)
                for i in range(NVT):
                    if i >= 2:
                        sVC.wait(sync, bVC + i - 1)
                    r, ii = (0, i) if i < half_t else (1, i - half_t)
                    src = kv_all[r * 2 * tpc * D + D * tpc + ii * 128 * D:
                                 r * 2 * tpc * D + D * tpc + (ii + 1) * 128 * D]
                    sync.dma_start(out=vstage[i % 2][:],
                                   in_=src.rearrange("(p t) -> p t", t=D)).then_inc(sVL[i % 2].h, 16)
                    sVL[i % 2].n += 16
                for h in range(H):
                    if h >= 2:
                        sSC.wait(sync, bSC + (h - 1) * NQH * NKC)
                    for r in range(2):
                        src = kv_all[r * 2 * tpc * D + h * HD * tpc:
                                     r * 2 * tpc * D + (h + 1) * HD * tpc]
                        sync.dma_start(out=kth[(h % 2) * 64:(h % 2) * 64 + 64, r * tpc:(r + 1) * tpc],
                                       in_=src.rearrange("(p t) -> p t", t=tpc)).then_inc(sKL[h % 2].h, 16)
                        sKL[h % 2].n += 16

            @blk.gpsimd
            def _(g):
                sVO.inc(g.memset(v65[:, :, :, HD:HD + 1], 1.0))
                for h in range(16):
                    sAT.wait(g, bAT + (h + 1) * NQH)
                    g.dma_start(out=attnT[(h % 2) * 64:(h % 2) * 64 + 64, h // 2, :],
                                in_=attnT64[:, h % 2, :]).then_inc(sMG[h % 2].h, 16)
                    sMG[h % 2].n += 16
                sMG[0].wait(g)
                sMG[1].wait(g)

            @blk.vector
            def _(v):
                for i in range(NVT):
                    sVL[i % 2].wait(v, bVL[i % 2] + (i // 2 + 1) * 16)
                    sVC.inc(v.tensor_copy(v65[:, i, :, 0:HD],
                                          vstage[i % 2][:].rearrange("p (h e) -> p h e", h=H)))
                for h in range(H):
                    for qh in range(NQH):
                        hq = h * NQH + qh
                        sAV.wait(v, bAV + (hq + 1) * NKC)
                        if hq >= 1:
                            sRB.wait(v, bRB + hq)
                        sRC.inc(v.reciprocal(rrow[:], ps_u[64:65, :]))
                        sUE.wait(v, bUE + hq + 1)
                        sRB.wait(v, bRB + hq + 1)
                        if qh == 0 and h >= 2:
                            sMG[h % 2].wait(v, bMG[h % 2] + (h // 2) * 16)
                        dst = attnT64[:, h % 2, qh * QW:(qh + 1) * QW]
                        sAT.inc(v.tensor_tensor(dst, usb[:], ps_rb[:], ALU.mult))

            @blk.tensor
            def _(t):
                sVC.wait(t, bVC + NVT)
                sVO.wait(t, bVO + 1)
                step = 0
                for h in range(H):
                    sKL[h % 2].wait(t, bKL[h % 2] + (h // 2 + 1) * 32)
                    qrows = qT[(h % 2) * 64:(h % 2) * 64 + 64, h // 2, :]
                    for qh in range(NQH):
                        hq = h * NQH + qh
                        for kc in range(NKC):
                            if step >= 2:
                                sEX.wait(t, bEX + step - 1)
                            pb = ps_sc[step % 2]
                            ins = None
                            for j in range(max(1, QW // 512)):
                                jw = min(512, QW)
                                ins = t.matmul(pb[:, j * 512:j * 512 + jw],
                                               kth[(h % 2) * 64:(h % 2) * 64 + 64, kc * 128:(kc + 1) * 128],
                                               qrows[:, qh * QW + j * 512:qh * QW + j * 512 + jw],
                                               start=True, stop=True)
                            sSC.inc(ins)
                            sEX.wait(t, bEX + step + 1)
                            if kc == 0 and hq >= 1:
                                sRC.wait(t, bRC + hq)
                                sUE.wait(t, bUE + hq)
                            ins = None
                            for j in range(max(1, QW // 512)):
                                jw = min(512, QW)
                                ins = t.matmul(ps_u[:, j * 512:j * 512 + jw],
                                               v65[:, kc, h, :],
                                               probs[step % 2][:, j * 512:j * 512 + jw],
                                               start=(kc == 0), stop=(kc == NKC - 1))
                            sAV.inc(ins)
                            step += 1
                        sRC.wait(t, bRC + hq + 1)
                        if hq >= 1:
                            sAT.wait(t, bAT + hq)
                        ins = None
                        for j in range(max(1, QW // 512)):
                            jw = min(512, QW)
                            ins = t.matmul(ps_rb[:, j * 512:j * 512 + jw],
                                           C.const["ones_row_f"][:, 0:64],
                                           rrow[:, j * 512:j * 512 + jw],
                                           start=True, stop=True)
                        sRB.inc(ins)

            @blk.scalar
            def _(s):
                step = 0
                for h in range(H):
                    for qh in range(NQH):
                        hq = h * NQH + qh
                        for kc in range(NKC):
                            sSC.wait(s, bSC + step + 1)
                            if step >= 2:
                                sAV.wait(s, bAV + step - 1)
                            sEX.inc(s.activation(probs[step % 2][:], ps_sc[step % 2][:],
                                                 AF.Exp, bias=-SHIFT, scale=1.0 / 32.0))
                            step += 1
                        sAV.wait(s, bAV + step)
                        if hq >= 1:
                            sAT.wait(s, bAT + hq)
                        sUE.inc(s.activation(usb[:], ps_u[0:64, :], AF.Copy))
        attn_done = [(sMG0, sMG0.n), (sMG1, sMG1.n)]

        with contextlib.ExitStack() as es2:
            ps_o = [es2.enter_context(nc.psum_tensor(f"l0po{i}", [128, D], F32)) for i in range(2)]
            xres = [es2.enter_context(nc.sbuf_tensor(f"l0xr{i}", [128, D], F32)) for i in range(2)]
            onew = [es2.enter_context(nc.sbuf_tensor(f"l0on{i}", [128, D], F32)) for i in range(2)]
            blk = es2.enter_context(nc.Block())
            sXL0, sXL1, sOM, sOA, sOS0, sOS1 = C.stage(6)
            sXL = [sXL0, sXL1]
            sOS = [sOS0, sOS1]
            bOM, bOA = sOM.n, sOA.n
            bXL = [sXL0.n, sXL1.n]
            bOS = [sOS0.n, sOS1.n]

            def _store(sync, i):
                sOA.wait(sync, bOA + i + 1)
                sync.dma_start(out=x_out[i * 128:(i + 1) * 128, :],
                               in_=onew[i % 2][:]).then_inc(sOS[i % 2].h, 16)
                sOS[i % 2].n += 16

            @blk.sync
            def _(sync):
                for i in range(nt):
                    if i >= 2:
                        sOA.wait(sync, bOA + i - 1)
                    sync.dma_start(out=xres[i % 2][:],
                                   in_=x_in[i * 128:(i + 1) * 128, :]).then_inc(sXL[i % 2].h, 16)
                    sXL[i % 2].n += 16
                    if i >= 2:
                        _store(sync, i - 2)
                for i in range(max(0, nt - 2), nt):
                    _store(sync, i)
                sOS[0].wait(sync)
                sOS[1].wait(sync)

            @blk.tensor
            def _(t):
                for s_, n_ in attn_done:
                    s_.wait(t, n_)
                for i in range(nt):
                    if i >= 2:
                        sOA.wait(t, bOA + i - 1)
                    pb = ps_o[i % 2]
                    for dc in range(ND):
                        for j in range(D // 512):
                            t.matmul(pb[:, j * 512:(j + 1) * 512],
                                     attnT[:, dc, i * 128:(i + 1) * 128],
                                     wout[:, dc, j * 512:(j + 1) * 512],
                                     start=(dc == 0), stop=False)
                    ins = None
                    for j in range(D // 512):
                        ins = t.matmul(pb[:, j * 512:(j + 1) * 512],
                                       C.const["ones_row"][:, :],
                                       bout[:, j * 512:(j + 1) * 512],
                                       start=False, stop=True)
                    sOM.inc(ins)

            @blk.vector
            def _(v):
                for i in range(nt):
                    sOM.wait(v, bOM + i + 1)
                    sXL[i % 2].wait(v, bXL[i % 2] + (i // 2 + 1) * 16)
                    if i >= 2:
                        sOS[i % 2].wait(v, bOS[i % 2] + (i // 2) * 16)
                    sOA.inc(v.tensor_tensor(onew[i % 2][:], xres[i % 2][:],
                                            ps_o[i % 2][:], ALU.add))
        return [(sOS0, sOS0.n), (sOS1, sOS1.n)]





def moe_ffn(C, x_in, x_out, cfg, wait_first=None):
    nc = C.nc
    tpc = C.tpc
    E = 8
    TH = tpc // 2
    nt2 = TH // 128
    NW = max(1, TH // 512)
    JW = min(512, TH)
    dG = C.sem("dG")
    cV, cS = C.sem("cV"), C.sem("cS")
    done_marks = []
    GC = 0.7978845608028654
    GK = 0.044715

    def nh1(br):
        n = br["w1"].shape[1] // 128
        if br.get("w1u") is not None:
            n += br["w1u"].shape[1] // 128
        return n

    def nhh(br):
        return br["w1"].shape[1] // 128 if br["act"] == "glu_silu" else nh1(br)

    with contextlib.ExitStack() as es:
        wr_sb = es.enter_context(nc.sbuf_tensor(f"{cfg['tag']}_wr", [128, ND, E], BF16))
        rb_sb = es.enter_context(nc.sbuf_tensor(f"{cfg['tag']}_rb", [1, E], BF16))
        kb = cfg.get("kb") or 1
        b2_sb = es.enter_context(nc.sbuf_tensor(f"{cfg['tag']}_b2", [max(kb, 1), D], BF16))
        with nc.Block() as blk:
            @blk.gpsimd
            def _(g):
                for dc in range(ND):
                    g.dma_start(out=wr_sb[:, dc, :],
                                in_=cfg["router_w"][dc * 128:(dc + 1) * 128, :]).then_inc(dG.h, 16)
                    dG.n += 16
                if cfg.get("router_b") is not None:
                    g.dma_start(out=rb_sb[:], in_=cfg["router_b"][None, :]).then_inc(dG.h, 16)
                    dG.n += 16
                if cfg.get("b2all") is not None:
                    g.dma_start(out=b2_sb[0:kb, :], in_=cfg["b2all"]).then_inc(dG.h, 16)
                    dG.n += 16
        consts_ready = dG.n

        for th in range(2):
            row0 = th * TH
            with contextlib.ExitStack() as eh:
                xt = eh.enter_context(nc.sbuf_tensor(f"{cfg['tag']}_xt{th}", [128, ND, TH], BF16))
                xt_sem, xt_n = ln_to_xt(C, x_in, xt, f"{cfg['tag']}{th}", row0=row0, nrows=TH,
                                        wait_first=wait_first if th == 0 else None)

                gT9 = eh.enter_context(nc.sbuf_tensor(f"{cfg['tag']}_gt{th}", [9, TH], BF16))
                ones_th = eh.enter_context(nc.sbuf_tensor(f"{cfg['tag']}_o1{th}", [1, TH], BF16))
                with contextlib.ExitStack() as es2:
                    stx = es2.enter_context(nc.sbuf_tensor(f"{cfg['tag']}_stx{th}", [128, 32], F32))
                    lg_sb = [es2.enter_context(nc.sbuf_tensor(f"{cfg['tag']}_lg{th}{i}", [128, E], F32)) for i in range(2)]
                    e_sb = [es2.enter_context(nc.sbuf_tensor(f"{cfg['tag']}_es{th}{i}", [128, E], F32)) for i in range(2)]
                    gbf = [es2.enter_context(nc.sbuf_tensor(f"{cfg['tag']}_gbf{th}{i}", [128, E], BF16)) for i in range(2)]
                    ps_lg = [es2.enter_context(nc.psum_tensor(f"{cfg['tag']}_plg{th}{i}", [128, E], F32)) for i in range(2)]
                    ps_gt = es2.enter_context(nc.psum_tensor(f"{cfg['tag']}_pgt{th}", [8, TH], BF16))
                    blk = es2.enter_context(nc.Block())
                    sLG, sLC, sNM, sEX, sGB, sGT = C.stage(6)
                    sOD = C.sem("dG2")
                    bLG, bLC, bNM, bEX, bGB, bGT = (s.n for s in (sLG, sLC, sNM, sEX, sGB, sGT))

                    @blk.tensor
                    def _(t):
                        dG.wait(t, consts_ready)
                        xt_sem.wait(t, xt_n)
                        for i in range(nt2):
                            if i >= 2:
                                sLC.wait(t, bLC + i - 1)
                            pb = ps_lg[i % 2]
                            ins = None
                            for dc in range(ND):
                                ins = t.matmul(pb[:], xt[:, dc, i * 128:(i + 1) * 128],
                                               wr_sb[:, dc, :], start=(dc == 0),
                                               stop=(dc == ND - 1 and cfg.get("router_b") is None))
                            if cfg.get("router_b") is not None:
                                ins = t.matmul(pb[:], C.const["ones_row"][:, :],
                                               rb_sb[:], start=False, stop=True)
                            sLG.inc(ins)
                            sGB.wait(t, bGB + i + 1)
                            sGT.inc(t.transpose(ps_gt[:, i * 128:(i + 1) * 128], gbf[i % 2][:],
                                                C.const["ident"][:]))

                    @blk.scalar
                    def _(s):
                        for i in range(nt2):
                            p = i % 2
                            o = p * 16
                            sLG.wait(s, bLG + i + 1)
                            if i >= 2:
                                sEX.wait(s, bEX + i - 1)
                                sGB.wait(s, bGB + i - 1)
                            s.activation(lg_sb[p][:], ps_lg[p][:], AF.Copy).then_inc(sLC.h, 1)
                            sLC.n += 1
                            sNM.wait(s, bNM + i + 1)
                            sLC.wait(s)
                            s.activation(e_sb[p][:], lg_sb[p][:], AF.Exp,
                                         bias=stx[:, o + 0:o + 1],
                                         accum_out=stx[:, o + 1:o + 2]).then_inc(sEX.h, 1)
                            sEX.n += 1

                    @blk.vector
                    def _(v):
                        for i in range(nt2):
                            p = i % 2
                            o = p * 16
                            sLC.wait(v, bLC + i + 1)
                            if i >= 2:
                                sGB.wait(v, bGB + i - 1)
                            cV.inc(v.reduce_max(stx[:, o + 0:o + 1], lg_sb[p][:], axis=AX.X, negate=True))
                            cV.wait(v)
                            v.engine_nop().then_inc(sNM.h, 1)
                            sNM.n += 1
                            sEX.wait(v, bEX + i + 1)
                            cV.inc(v.max(stx[:, o + 2:o + 10], e_sb[p][:]))
                            cV.wait(v)
                            if cfg.get("l3_denom"):
                                cV.inc(v.tensor_scalar(stx[:, o + 10:o + 11], stx[:, o + 1:o + 2],
                                                       1e-6, None, ALU.mult))
                                cV.wait(v)
                                cV.inc(v.tensor_tensor(stx[:, o + 10:o + 11], stx[:, o + 10:o + 11],
                                                       stx[:, o + 2:o + 3], ALU.add))
                                cV.wait(v)
                                cV.inc(v.tensor_tensor(stx[:, o + 10:o + 11], stx[:, o + 10:o + 11],
                                                       stx[:, o + 3:o + 4], ALU.add))
                            else:
                                cV.inc(v.tensor_tensor(stx[:, o + 10:o + 11], stx[:, o + 2:o + 3],
                                                       stx[:, o + 3:o + 4], ALU.add))
                            cV.wait(v)
                            cV.inc(v.reciprocal(stx[:, o + 11:o + 12], stx[:, o + 10:o + 11]))
                            cV.wait(v)
                            cV.inc(v.tensor_tensor(stx[:, o + 12:o + 13], stx[:, o + 2:o + 3],
                                                   stx[:, o + 11:o + 12], ALU.mult))
                            cV.wait(v)
                            cV.inc(v.tensor_tensor(stx[:, o + 13:o + 14], stx[:, o + 3:o + 4],
                                                   stx[:, o + 11:o + 12], ALU.mult))
                            cV.wait(v)
                            cV.inc(v.tensor_scalar(lg_sb[p][:], e_sb[p][:], stx[:, o + 2:o + 3],
                                                   stx[:, o + 12:o + 13], ALU.is_equal, ALU.mult))
                            cV.wait(v)
                            cV.inc(v.tensor_scalar(e_sb[p][:], e_sb[p][:], stx[:, o + 3:o + 4],
                                                   stx[:, o + 13:o + 14], ALU.is_equal, ALU.mult))
                            cV.wait(v)
                            if i >= 2:
                                sGT.wait(v, bGT + i - 1)
                            cV.inc(v.tensor_tensor(gbf[p][:], lg_sb[p][:], e_sb[p][:], ALU.add))
                            cV.wait(v)
                            v.engine_nop().then_inc(sGB.h, 1)
                            sGB.n += 1
                        sGT.wait(v, bGT + nt2)
                        cV.inc(v.tensor_copy(gT9[0:8, :], ps_gt[:]))
                        cV.wait(v)
                        v.engine_nop().then_inc(sGB.h, 1)
                        sGB.n += 1

                    @blk.gpsimd
                    def _(g):
                        C.sem("cG").inc(g.memset(ones_th[:], 1.0))
                        C.sem("cG").wait(g)
                        g.dma_start(out=gT9[8:9, :], in_=ones_th[:]).then_inc(sOD.h, 16)
                        sOD.n += 16
                        sOD.wait(g)
                        C.sem("cG").inc(g.engine_nop())
                gt_marks = [(sGB, sGB.n), (C.sem("cG"), C.sem("cG").n)]

                NH1max = max(nh1(br) for br in cfg["branches"])
                NHhmax = max(nhh(br) for br in cfg["branches"])
                w1sb = eh.enter_context(nc.sbuf_tensor(f"{cfg['tag']}_w1{th}", [128, ND, NH1max * 128], BF16))
                w2sb = eh.enter_context(nc.sbuf_tensor(f"{cfg['tag']}_w2{th}", [128, NHhmax, D], BF16))
                b1c = eh.enter_context(nc.sbuf_tensor(f"{cfg['tag']}_b1{th}", [128, NH1max], F32))
                b1row = eh.enter_context(nc.sbuf_tensor(f"{cfg['tag']}_b1r{th}", [1, NH1max * 128], BF16))
                hT = eh.enter_context(nc.sbuf_tensor(f"{cfg['tag']}_hT{th}", [128, NHhmax, TH], BF16))
                gb = eh.enter_context(nc.sbuf_tensor(f"{cfg['tag']}_gb{th}", [128, TH], BF16))
                gRow = [eh.enter_context(nc.sbuf_tensor(f"{cfg['tag']}_gr{th}{i}", [1, TH], BF16)) for i in range(2)]
                yacc = eh.enter_context(nc.sbuf_tensor(f"{cfg['tag']}_ya{th}", [128, E, TH], F32))
                zb = [eh.enter_context(nc.sbuf_tensor(f"{cfg['tag']}_zb{th}{i}", [128, TH], BF16)) for i in range(2)]
                z2b = [eh.enter_context(nc.sbuf_tensor(f"{cfg['tag']}_z2{th}{i}", [128, TH], BF16)) for i in range(2)]
                with contextlib.ExitStack() as es2:
                    ps_h = [es2.enter_context(nc.psum_tensor(f"{cfg['tag']}_ph{th}{i}", [128, TH], F32)) for i in range(2)]
                    ps_y = [es2.enter_context(nc.psum_tensor(f"{cfg['tag']}_py{th}{i}", [128, TH], F32)) for i in range(2)]
                    blk = es2.enter_context(nc.Block())
                    sGM, sGBe, sHM, sZ1, sZ2, sZ3, sHT, sYM, sYA = C.stage(9)
                    sWL = C.sem("dW")
                    sGR = [C.sem("dR0"), C.sem("dR1")]
                    (bWL, bGM, bGBe, bHM, bZ1, bZ2, bZ3, bHT, bYM, bYA) = (
                        s.n for s in (sWL, sGM, sGBe, sHM, sZ1, sZ2, sZ3, sHT, sYM, sYA))
                    bGR = [sGR[0].n, sGR[1].n]
                    BRS = cfg["branches"]
                    NBR = len(BRS)
                    hm_at, z1_at, ht_at, ym_at, ya_at, wl_at, gm_at = [], [], [], [], [], [], []
                    hm, z1t, htc, ym, ya, wl, gm = 0, 0, 0, 0, 0, 0, 0
                    for br in BRS:
                        hm_at.append(hm); z1_at.append(z1t); ht_at.append(htc)
                        ym_at.append(ym); ya_at.append(ya); wl_at.append(wl); gm_at.append(gm)
                        hm += nhh(br)
                        z1t += nhh(br)
                        htc += nhh(br)
                        ym += E
                        ya += E
                        wl += (ND * (2 if br.get("w1u") is not None else 1) + nhh(br) +
                               (1 if br.get("b1r") is not None else 0) +
                               (1 if br.get("b1row") is not None else 0)) * 16
                        gm += 1 if br.get("gidx") is not None else 0

                    @blk.gpsimd
                    def _(g):
                        for bi, br in enumerate(BRS):
                            if bi >= 1:
                                sHM.wait(g, bHM + hm_at[bi])
                                sYM.wait(g, bYM + ym_at[bi])
                                sZ1.wait(g, bZ1 + z1_at[bi])
                            n1 = nh1(br)
                            n1g = br["w1"].shape[1] // 128
                            for dc in range(ND):
                                g.dma_start(out=w1sb[:, dc, 0:n1g * 128],
                                            in_=br["w1"][dc * 128:(dc + 1) * 128, :]).then_inc(sWL.h, 16)
                                sWL.n += 16
                                if br.get("w1u") is not None:
                                    g.dma_start(out=w1sb[:, dc, n1g * 128:n1 * 128],
                                                in_=br["w1u"][dc * 128:(dc + 1) * 128, :]).then_inc(sWL.h, 16)
                                    sWL.n += 16
                            for hc in range(nhh(br)):
                                g.dma_start(out=w2sb[:, hc, :],
                                            in_=br["w2"][hc * 128:(hc + 1) * 128, :]).then_inc(sWL.h, 16)
                                sWL.n += 16
                            if br.get("b1r") is not None:
                                g.dma_start(out=b1c[:, 0:n1], in_=br["b1r"]).then_inc(sWL.h, 16)
                                sWL.n += 16
                            if br.get("b1row") is not None:
                                g.dma_start(out=b1row[:, 0:n1 * 128],
                                            in_=br["b1row"][None, :]).then_inc(sWL.h, 16)
                                sWL.n += 16
                            if br.get("gidx") is not None:
                                gi = gm_at[bi]
                                for s_, n_ in gt_marks:
                                    s_.wait(g, n_)
                                if gi >= 2:
                                    sGM.wait(g, bGM + gi - 1)
                                g.dma_start(out=gRow[gi % 2][:],
                                            in_=gT9[br["gidx"]:br["gidx"] + 1, :]).then_inc(sGR[gi % 2].h, 16)
                                sGR[gi % 2].n += 16

                    @blk.tensor
                    def _(t):
                        for s_, n_ in gt_marks:
                            s_.wait(t, n_)
                        for bi, br in enumerate(BRS):
                            n1, nh = nh1(br), nhh(br)
                            glu = br["act"] == "glu_silu"
                            sWL.wait(t, bWL + wl_at[bi] +
                                     (ND * (2 if br.get("w1u") is not None else 1) + nh +
                                      (1 if br.get("b1r") is not None else 0) +
                                      (1 if br.get("b1row") is not None else 0)) * 16)
                            if br.get("gidx") is not None:
                                gi = gm_at[bi]
                                if bi >= 1:
                                    sYA.wait(t, bYA + ya_at[bi])
                                sGR[gi % 2].wait(t, bGR[gi % 2] + (gi // 2 + 1) * 16)
                                ins = None
                                for j in range(NW):
                                    ins = t.matmul(ps_y[0][:, j * JW:(j + 1) * JW],
                                                   C.const["ones_row"][:, :],
                                                   gRow[gi % 2][:, j * JW:(j + 1) * JW],
                                                   start=True, stop=True)
                                sGM.inc(ins)
                            for hc in range(nh):
                                if glu:
                                    if hc >= 1:
                                        sZ2.wait(t, bZ2 + z1_at[bi] + hc)
                                    n1g = br["w1"].shape[1] // 128
                                    for half in range(2):
                                        col0 = half * n1g * 128 + hc * 128
                                        ins = None
                                        for dc in range(ND):
                                            for j in range(NW):
                                                ins = t.matmul(ps_h[half][:, j * JW:(j + 1) * JW],
                                                               w1sb[:, dc, col0:col0 + 128],
                                                               xt[:, dc, j * JW:(j + 1) * JW],
                                                               start=(dc == 0), stop=(dc == ND - 1))
                                    sHM.inc(ins)
                                else:
                                    if hc >= 2:
                                        sZ1.wait(t, bZ1 + z1_at[bi] + hc - 1)
                                    elif bi >= 1:
                                        sZ1.wait(t, bZ1 + z1_at[bi])
                                    pb = ps_h[hc % 2]
                                    has_b1r = br.get("b1row") is not None
                                    ins = None
                                    for dc in range(ND):
                                        for j in range(NW):
                                            ins = t.matmul(pb[:, j * JW:(j + 1) * JW],
                                                           w1sb[:, dc, hc * 128:(hc + 1) * 128],
                                                           xt[:, dc, j * JW:(j + 1) * JW],
                                                           start=(dc == 0),
                                                           stop=(dc == ND - 1 and not has_b1r))
                                    if has_b1r:
                                        for j in range(NW):
                                            ins = t.matmul(pb[:, j * JW:(j + 1) * JW],
                                                           b1row[:, hc * 128:(hc + 1) * 128],
                                                           ones_th[:, j * JW:(j + 1) * JW],
                                                           start=False, stop=(j == NW - 1))
                                    sHM.inc(ins)
                            sHT.wait(t, bHT + ht_at[bi] + nh)
                            if br.get("gidx") is not None:
                                sGBe.wait(t, bGBe + gm_at[bi] + 1)
                            for od in range(E):
                                pb = ps_y[od % 2]
                                if od >= 2:
                                    sYA.wait(t, bYA + ya_at[bi] + od - 1)
                                elif bi >= 1:
                                    sYA.wait(t, bYA + ya_at[bi])
                                ins = None
                                bias_here = (bi == 0 and cfg.get("b2all") is not None)
                                for hc in range(nh):
                                    last = (hc == nh - 1) and not bias_here
                                    for j in range(NW):
                                        ins = t.matmul(pb[:, j * JW:(j + 1) * JW],
                                                       w2sb[:, hc, od * 128:(od + 1) * 128],
                                                       hT[:, hc, j * JW:(j + 1) * JW],
                                                       start=(hc == 0), stop=last)
                                if bias_here:
                                    for j in range(NW):
                                        ins = t.matmul(pb[:, j * JW:(j + 1) * JW],
                                                       b2_sb[0:kb, od * 128:(od + 1) * 128],
                                                       gT9[0:kb, j * JW:(j + 1) * JW],
                                                       start=False, stop=True)
                                sYM.inc(ins)

                    @blk.scalar
                    def _(s):
                        for bi, br in enumerate(BRS):
                            nh = nhh(br)
                            tanh_mode = br["act"] == "gelu_tanh"
                            for hc in range(nh):
                                p = hc % 2
                                step = z1_at[bi] + hc
                                sHM.wait(s, bHM + hm_at[bi] + hc + 1)
                                lag = 1 if br["act"] == "glu_silu" else 2
                                if hc >= lag:
                                    sHT.wait(s, bHT + ht_at[bi] + hc - lag + 1)
                                elif bi >= 1:
                                    sHT.wait(s, bHT + ht_at[bi])
                                b1 = b1c[:, hc:hc + 1] if br.get("b1r") is not None else 0.0
                                if br["act"] == "relu":
                                    ins = s.activation(zb[p][:], ps_h[p][:], AF.Relu, bias=b1)
                                elif br["act"] == "gelu":
                                    ins = s.activation(zb[p][:], ps_h[p][:], AF.Gelu, bias=b1)
                                elif br["act"] == "glu_silu":
                                    ins = s.activation(zb[0][:], ps_h[0][:], AF.Silu)
                                else:
                                    cS.inc(s.activation(zb[p][:], ps_h[p][:], AF.Copy))
                                    cS.wait(s)
                                    ins = s.activation(z2b[p][:], ps_h[p][:], AF.Square)
                                ins.then_inc(sZ1.h, 1)
                                sZ1.n += 1
                                if tanh_mode and hc >= 1:
                                    sZ2.wait(s, bZ2 + z1_at[bi] + hc)
                                    s.activation(z2b[(hc - 1) % 2][:], z2b[(hc - 1) % 2][:],
                                                 AF.Sigmoid, scale=2.0).then_inc(sZ3.h, 1)
                                    sZ3.n += 1
                            if tanh_mode:
                                sZ2.wait(s, bZ2 + z1_at[bi] + nh)
                                s.activation(z2b[(nh - 1) % 2][:], z2b[(nh - 1) % 2][:],
                                             AF.Sigmoid, scale=2.0).then_inc(sZ3.h, 1)
                                sZ3.n += 1

                    @blk.vector
                    def _(v):
                        for bi, br in enumerate(BRS):
                            nh = nhh(br)
                            tanh_mode = br["act"] == "gelu_tanh"
                            has_g = br.get("gidx") is not None
                            if has_g:
                                sGM.wait(v, bGM + gm_at[bi] + 1)
                                if bi >= 1:
                                    sHT.wait(v, bHT + ht_at[bi])
                                cV.inc(v.tensor_copy(gb[:], ps_y[0][:]))
                                cV.wait(v)
                                v.engine_nop().then_inc(sGBe.h, 1)
                                sGBe.n += 1
                            if bi >= 1:
                                sYM.wait(v, bYM + ym_at[bi])
                            def hwrite(hc, src):
                                if has_g:
                                    return v.tensor_tensor(hT[:, hc, :], src[:], gb[:], ALU.mult)
                                return v.tensor_copy(hT[:, hc, :], src[:])
                            def gt_fin(hc):
                                p2 = hc % 2
                                sZ3.wait(v, bZ3 + z1_at[bi] + hc + 1)
                                cV.inc(v.tensor_tensor(zb[p2][:], zb[p2][:], z2b[p2][:], ALU.mult))
                                cV.wait(v)
                                sHT.inc(hwrite(hc, zb[p2]))
                            for hc in range(nh):
                                p = hc % 2
                                sZ1.wait(v, bZ1 + z1_at[bi] + hc + 1)
                                if br["act"] in ("relu", "gelu"):
                                    ins = hwrite(hc, zb[p])
                                    sHT.inc(ins)
                                elif br["act"] == "glu_silu":
                                    cV.inc(v.tensor_tensor(zb[0][:], zb[0][:], ps_h[1][:], ALU.mult))
                                    cV.wait(v)
                                    v.engine_nop().then_inc(sZ2.h, 1)
                                    sZ2.n += 1
                                    sHT.inc(hwrite(hc, zb[0]))
                                else:
                                    cV.inc(v.tensor_scalar(z2b[p][:], z2b[p][:], GC * GK, GC,
                                                           ALU.mult, ALU.add))
                                    cV.wait(v)
                                    cV.inc(v.tensor_tensor(z2b[p][:], z2b[p][:], zb[p][:], ALU.mult))
                                    cV.wait(v)
                                    v.engine_nop().then_inc(sZ2.h, 1)
                                    sZ2.n += 1
                                    if hc >= 1:
                                        gt_fin(hc - 1)
                            if tanh_mode:
                                gt_fin(nh - 1)
                            for od in range(E):
                                sYM.wait(v, bYM + ym_at[bi] + od + 1)
                                if bi == 0:
                                    ins = v.tensor_copy(yacc[:, od, :], ps_y[od % 2][:])
                                else:
                                    ins = v.tensor_tensor(yacc[:, od, :], yacc[:, od, :],
                                                          ps_y[od % 2][:], ALU.add)
                                sYA.inc(ins)

                _moe_final(C, cfg["tag"], th, x_in, x_out, yacc, row0, nt2,
                           (sYA, sYA.n), done_marks)
    return done_marks



def build_l0_attn_program(scale=1, qw_cap=None):
    C = Core(scale=scale)
    if qw_cap:
        C.qw_cap = qw_cap
    nc = C.nc
    tpc = C.tpc
    x_in = nc.dram_tensor("x", [tpc, D], F32, kind="ExternalInput").ap()
    x_out = nc.dram_tensor("xo", [tpc, D], F32, kind="ExternalOutput").ap()
    W = {
        "w_qkv": nc.dram_tensor("L0_attn_w_qkv", [D, 3 * D], F32, kind="ExternalInput").ap(),
        "w_out": nc.dram_tensor("L0_attn_w_out", [D, D], F32, kind="ExternalInput").ap(),
        "b_out": nc.dram_tensor("L0_attn_b_out", [D], F32, kind="ExternalInput").ap(),
    }
    kv_src = nc.dram_tensor("l0_kv_src", [2 * tpc * D], BF16).ap()
    kv_all = nc.dram_tensor("l0_kv_all", [2 * 2 * tpc * D], BF16).ap()
    with C.es:
        build_consts(C)
        done = l0_attn(C, x_in, x_out, W, kv_src, kv_all)
        with nc.Block() as blk:
            @blk.gpsimd
            def _(g):
                for s_, n_ in done:
                    s_.wait(g, n_)
    return C





def _moe_final(C, tag, th, x_in, x_out, yacc, row0, nt2, ya_mark, done_marks):
    nc = C.nc
    E = 8
    with contextlib.ExitStack() as es2:
        ps_tp = [es2.enter_context(nc.psum_tensor(f"{tag}_ptp{th}{i}", [128, 1024], F32)) for i in range(2)]
        xres = [es2.enter_context(nc.sbuf_tensor(f"{tag}_fx{th}{i}", [128, D], F32)) for i in range(2)]
        onew = [es2.enter_context(nc.sbuf_tensor(f"{tag}_fo{th}{i}", [128, D], F32)) for i in range(2)]
        blk = es2.enter_context(nc.Block())
        sXL0, sXL1, sTP, sOA, sOS0, sOS1 = C.stage(6)
        sXL = [sXL0, sXL1]
        sOS = [sOS0, sOS1]
        bTP, bOA = sTP.n, sOA.n
        bXL = [sXL0.n, sXL1.n]
        bOS = [sOS0.n, sOS1.n]

        def _store(sync, i):
            sOA.wait(sync, bOA + i + 1)
            sync.dma_start(out=x_out[row0 + i * 128:row0 + (i + 1) * 128, :],
                           in_=onew[i % 2][:]).then_inc(sOS[i % 2].h, 16)
            sOS[i % 2].n += 16

        @blk.sync
        def _(sync):
            for i in range(nt2):
                if i >= 2:
                    sOA.wait(sync, bOA + i - 1)
                sync.dma_start(out=xres[i % 2][:],
                               in_=x_in[row0 + i * 128:row0 + (i + 1) * 128, :]).then_inc(sXL[i % 2].h, 16)
                sXL[i % 2].n += 16
                if i >= 2:
                    _store(sync, i - 2)
            for i in range(max(0, nt2 - 2), nt2):
                _store(sync, i)
            sOS0.wait(sync)
            sOS1.wait(sync)

        @blk.tensor
        def _(t):
            ya_mark[0].wait(t, ya_mark[1])
            for i in range(nt2):
                if i >= 2:
                    sOA.wait(t, bOA + i - 1)
                ins = None
                for od in range(E):
                    ins = t.transpose(ps_tp[i % 2][:, od * 128:(od + 1) * 128],
                                      yacc[:, od, i * 128:(i + 1) * 128], C.const["identf"][:])
                sTP.inc(ins)

        @blk.vector
        def _(v):
            for i in range(nt2):
                sTP.wait(v, bTP + i + 1)
                sXL[i % 2].wait(v, bXL[i % 2] + (i // 2 + 1) * 16)
                if i >= 2:
                    sOS[i % 2].wait(v, bOS[i % 2] + (i // 2) * 16)
                sOA.inc(v.tensor_tensor(onew[i % 2][:], xres[i % 2][:], ps_tp[i % 2][:], ALU.add))
    done_marks.append((sOS0, sOS0.n))
    done_marks.append((sOS1, sOS1.n))


def moe_cfg(C, variant):
    nc = C.nc
    t = nc.dram_tensor
    E = 8
    if variant == "l0_ffn":
        Wfc_s = t("L0_ffn_Wfc_s", [D, 4 * D], F32, kind="ExternalInput").ap()
        Wproj_s = t("L0_ffn_Wproj_s", [4 * D, D], F32, kind="ExternalInput").ap()
        Wr = t("L0_ffn_Wr", [D, E], F32, kind="ExternalInput").ap()
        Wfc = t("L0_ffn_Wfc", [E, D, 4 * D], F32, kind="ExternalInput").ap()
        Wproj = t("L0_ffn_Wproj", [E, 4 * D, D], F32, kind="ExternalInput").ap()
        bfc_s = t("L0_ffn_bfc_s", [4 * D], F32, kind="ExternalInput").ap()
        bfc = t("L0_ffn_bfc", [E, 4 * D], F32, kind="ExternalInput").ap()
        b2all = t("L0f_b2all", [9, D], F32, kind="ExternalInput").ap()
        H2 = 2 * D
        branches = []
        for half in range(2):
            branches.append(dict(w1=Wfc_s[:, half * H2:(half + 1) * H2],
                                 w2=Wproj_s[half * H2:(half + 1) * H2, :],
                                 b1row=bfc_s[half * H2:(half + 1) * H2],
                                 gidx=None, act="gelu_tanh"))
        for e in range(E):
            for half in range(2):
                branches.append(dict(w1=Wfc[e, :, half * H2:(half + 1) * H2],
                                     w2=Wproj[e, half * H2:(half + 1) * H2, :],
                                     b1row=bfc[e, half * H2:(half + 1) * H2],
                                     gidx=e, act="gelu_tanh"))
        return dict(tag="l0f", router_w=Wr, router_b=None, branches=branches,
                    b2all=b2all, kb=9, l3_denom=False)
    if variant == "l1_ffn":
        Wg = t("L1_ffn_Wg", [D, E], F32, kind="ExternalInput").ap()
        W1 = t("L1_ffn_W1", [E, D, 2 * D], F32, kind="ExternalInput").ap()
        W2 = t("L1_ffn_W2", [E, 2 * D, D], F32, kind="ExternalInput").ap()
        b1r = t("L1f_b1r", [E, 128, 16], F32, kind="ExternalInput").ap()
        b2 = t("L1_ffn_b2", [E, D], F32, kind="ExternalInput").ap()
        branches = [dict(w1=W1[e], w2=W2[e], b1r=b1r[e], gidx=e, act="relu")
                    for e in range(E)]
        return dict(tag="l1f", router_w=Wg, router_b=None, branches=branches,
                    b2all=b2, kb=8, l3_denom=False)
    if variant == "l2_ffn":
        rw = t("L2_ffn_router_w", [D, E], F32, kind="ExternalInput").ap()
        rb = t("L2_ffn_router_b", [E], F32, kind="ExternalInput").ap()
        W1 = t("L2_ffn_W1", [E, D, 512], F32, kind="ExternalInput").ap()
        W2 = t("L2_ffn_W2", [E, 512, D], F32, kind="ExternalInput").ap()
        b1r = t("L2f_b1r", [E, 128, 4], F32, kind="ExternalInput").ap()
        b2 = t("L2_ffn_b2", [E, D], F32, kind="ExternalInput").ap()
        branches = [dict(w1=W1[e], w2=W2[e], b1r=b1r[e], gidx=e, act="gelu")
                    for e in range(E)]
        return dict(tag="l2f", router_w=rw, router_b=rb, branches=branches,
                    b2all=b2, kb=8, l3_denom=False)
    if variant == "l3_ffn":
        rwT = t("L3f_router_wT", [D, E], F32, kind="ExternalInput").ap()
        w13 = t("L3_ffn_w13", [E, D, 4 * D], F32, kind="ExternalInput").ap()
        w2 = t("L3_ffn_w2", [E, 2 * D, D], F32, kind="ExternalInput").ap()
        branches = []
        for e in range(E):
            for half in range(2):
                branches.append(dict(w1=w13[e, :, half * D:(half + 1) * D],
                                     w1u=w13[e, :, 2 * D + half * D:2 * D + (half + 1) * D],
                                     w2=w2[e, half * D:(half + 1) * D, :],
                                     b1r=None, gidx=e, act="glu_silu"))
        return dict(tag="l3f", router_w=rwT, router_b=None, branches=branches,
                    b2all=None, kb=0, l3_denom=True)
    raise ValueError(variant)


def prep_derived(inp):
    d = {}
    r = lambda b: np.ascontiguousarray(b.reshape(b.shape[0] // 128, 128).T)
    d["L0f_b2all"] = np.concatenate([inp["L0_ffn_bproj"],
                                     inp["L0_ffn_bproj_s"][None, :]], 0).astype(np.float32)
    d["L1f_b1r"] = np.stack([r(inp["L1_ffn_b1"][e]) for e in range(8)]).astype(np.float32)
    d["L2f_b1r"] = np.stack([r(inp["L2_ffn_b1"][e]) for e in range(8)]).astype(np.float32)
    d["L3f_router_wT"] = np.ascontiguousarray(inp["L3_ffn_router_w"].T).astype(np.float32)
    return d


def build_moe_program(variant, scale=1):
    C = Core(scale=scale)
    nc = C.nc
    tpc = C.tpc
    x_in = nc.dram_tensor("x", [tpc, D], F32, kind="ExternalInput").ap()
    x_out = nc.dram_tensor("xo", [tpc, D], F32, kind="ExternalOutput").ap()
    cfg = moe_cfg(C, variant)
    with C.es:
        build_consts(C)
        done = moe_ffn(C, x_in, x_out, cfg)
        with nc.Block() as blk:
            @blk.gpsimd
            def _(g):
                for s_, n_ in done:
                    s_.wait(g, n_)
    return C




def l1_attn(C, x_in, x_out, W, y1_dram, kv_src, kv_all, wait_first=None):
    nc = C.nc
    tpc, nt = C.tpc, C.nt
    H, HD = 16, 64
    sCC = C.sem("cc")
    cV, cS = C.sem("cV"), C.sem("cS")
    dW = C.sem("dW")

    with contextlib.ExitStack() as es:
        woutT = es.enter_context(nc.sbuf_tensor("l1_wo", [128, ND, D], BF16))
        lnw_row = es.enter_context(nc.sbuf_tensor("l1_lnw", [1, D], BF16))
        lnb_row = es.enter_context(nc.sbuf_tensor("l1_lnb", [1, D], BF16))
        w_bc = es.enter_context(nc.sbuf_tensor("l1_wbc", [128, D], BF16))
        b_bc = es.enter_context(nc.sbuf_tensor("l1_bbc", [128, D], BF16))
        sel_sb = es.enter_context(nc.sbuf_tensor("l1_sel", [128, 128], F32))
        q_sb = es.enter_context(nc.sbuf_tensor("l1_q", [128, nt, H, HD], BF16))
        es_qkv = contextlib.ExitStack()
        wqkvT = es_qkv.enter_context(nc.sbuf_tensor("l1_wq", [128, ND, 3 * D], BF16))
        bW = dW.n
        with nc.Block() as blk:
            @blk.gpsimd
            def _(g):
                for c in range(ND):
                    g.dma_start(out=wqkvT[:, c, :], in_=W["wqkvT"][c * 128:(c + 1) * 128, :]).then_inc(dW.h, 16)
                    g.dma_start(out=woutT[:, c, :], in_=W["woutT"][c * 128:(c + 1) * 128, :]).then_inc(dW.h, 16)
                g.dma_start(out=lnw_row[:], in_=W["ln_w"][None, :]).then_inc(dW.h, 16)
                g.dma_start(out=lnb_row[:], in_=W["ln_b"][None, :]).then_inc(dW.h, 16)
                g.dma_start(out=sel_sb[:], in_=W["sel"]).then_inc(dW.h, 16)
                dW.n += (2 * ND + 3) * 16
                dW.wait(g)
        wready = dW.n

        with contextlib.ExitStack() as es2:
            ps_b = es2.enter_context(nc.psum_tensor("l1_psb", [128, D], F32))
            blk = es2.enter_context(nc.Block())
            sBM, sBE = C.stage(2)
            bBM, bBE = sBM.n, sBE.n

            @blk.tensor
            def _(t):
                wait_consts(C, t)
                ins = None
                for j in range(D // 512):
                    ins = t.matmul(ps_b[:, j * 512:(j + 1) * 512], C.const["ones_row"][:, :],
                                   lnw_row[:, j * 512:(j + 1) * 512], start=True, stop=True)
                sBM.inc(ins)
                sBE.wait(t, bBE + 1)
                ins = None
                for j in range(D // 512):
                    ins = t.matmul(ps_b[:, j * 512:(j + 1) * 512], C.const["ones_row"][:, :],
                                   lnb_row[:, j * 512:(j + 1) * 512], start=True, stop=True)
                sBM.inc(ins)

            @blk.vector
            def _(v):
                sBM.wait(v, bBM + 1)
                sBE.inc(v.tensor_copy(w_bc[:], ps_b[:]))
                sBM.wait(v, bBM + 2)
                sBE.inc(v.tensor_copy(b_bc[:], ps_b[:]))
        bc_done = (C.stage(2)[1], C.stage(2)[1].n)

        y1_marks = ln_to_xt(C, x_in, None, "l1p", y_dram=y1_dram, wait_first=wait_first)
        xt2 = es_qkv.enter_context(nc.sbuf_tensor("l1_xt2", [128, ND, tpc], BF16))
        xt_sem, xt_n = ln_to_xt(C, y1_dram, xt2, "l1a", affine=(w_bc, b_bc),
                                wait_first=y1_marks + [bc_done])

        with contextlib.ExitStack() as es2:
            ps_qkv = [es2.enter_context(nc.psum_tensor(f"l1_pq{i}", [128, 1536], F32)) for i in range(2)]
            stg = [es2.enter_context(nc.sbuf_tensor(f"l1_st{i}", [128, 2048], BF16)) for i in range(2)]
            blk = es2.enter_context(nc.Block())
            sQM, sQE, sKS0, sKS1 = C.stage(4)
            sKS = [sKS0, sKS1]
            bQM, bQE = sQM.n, sQE.n
            bKS = [sKS0.n, sKS1.n]
            NS = 2 * nt

            @blk.tensor
            def _(t):
                xt_sem.wait(t, xt_n)
                for st in range(NS):
                    i, hf = st // 2, st % 2
                    if st >= 2:
                        sQE.wait(t, bQE + st - 1)
                    pb = ps_qkv[st % 2]
                    ins = None
                    for dc in range(ND):
                        for j in range(3):
                            ins = t.matmul(pb[:, j * 512:(j + 1) * 512],
                                           xt2[:, dc, i * 128:(i + 1) * 128],
                                           wqkvT[:, dc, hf * 1536 + j * 512:hf * 1536 + (j + 1) * 512],
                                           start=(dc == 0), stop=(dc == ND - 1))
                    sQM.inc(ins)

            @blk.scalar
            def _(s):
                for st in range(NS):
                    i, hf = st // 2, st % 2
                    sQM.wait(s, bQM + st + 1)
                    if i >= 2 and hf == 0:
                        sKS[i % 2].wait(s, bKS[i % 2] + (i // 2) * 16)
                    pv = ps_qkv[st % 2][:].rearrange("p (h c) -> p h c", h=8)
                    cS.inc(s.activation(q_sb[:, i, hf * 8:(hf + 1) * 8, :], pv[:, :, 0:HD], AF.Copy))
                    cS.inc(s.activation(stg[i % 2][:, hf * 512:(hf + 1) * 512].rearrange("p (h c) -> p h c", h=8),
                                        pv[:, :, HD:2 * HD], AF.Copy))
                    ins = s.activation(stg[i % 2][:, 1024 + hf * 512:1024 + (hf + 1) * 512].rearrange("p (h c) -> p h c", h=8),
                                       pv[:, :, 2 * HD:3 * HD], AF.Copy)
                    sQE.inc(ins)
                    cS.n += 0

            @blk.sync
            def _(sync):
                kv2d = kv_src.rearrange("(a b) -> a b", b=2 * D)
                for i in range(nt):
                    sQE.wait(sync, bQE + i * 2 + 2)
                    sync.dma_start(out=kv2d[i * 128:(i + 1) * 128, :],
                                   in_=stg[i % 2][:]).then_inc(sKS[i % 2].h, 16)
                    sKS[i % 2].n += 16
                sKS0.wait(sync)
                sKS1.wait(sync)
        kv_written = [(sKS0, sKS0.n), (sKS1, sKS1.n)]
        es_qkv.close()

        with nc.Block() as blk:
            @blk.gpsimd
            def _(g):
                for s_, n_ in kv_written:
                    s_.wait(g, n_)
                sCC.inc(g.collective_compute("AllGather", ALU.bypass, replica_groups=ALL8,
                                             ins=[kv_src.rearrange("(a b) -> a b", b=2 * D)],
                                             outs=[kv_all.rearrange("(a b) -> a b", b=2 * D)]))
        kv_ready = sCC.n

        with contextlib.ExitStack() as es2:
            kv8 = [es2.enter_context(nc.sbuf_tensor(f"l1_kv8{i}", [128, 8, 2 * D], BF16)) for i in range(2)]
            sc = es2.enter_context(nc.sbuf_tensor("l1_sc", [128, H, 8], F32))
            st2 = es2.enter_context(nc.sbuf_tensor("l1_st2", [128, H, 8], F32))
            mx = es2.enter_context(nc.sbuf_tensor("l1_mx", [128, H], F32))
            prod = es2.enter_context(nc.sbuf_tensor("l1_prod", [128, D], BF16))
            oacc = es2.enter_context(nc.sbuf_tensor("l1_oacc", [128, D], F32))
            obf = [es2.enter_context(nc.sbuf_tensor(f"l1_obf{i}", [128, D], BF16)) for i in range(2)]
            xres = [es2.enter_context(nc.sbuf_tensor(f"l1_xr{i}", [128, D], F32)) for i in range(2)]
            y1r = [es2.enter_context(nc.sbuf_tensor(f"l1_y1{i}", [128, D], F32)) for i in range(2)]
            onew = [es2.enter_context(nc.sbuf_tensor(f"l1_on{i}", [128, D], F32)) for i in range(2)]
            ps_tp = [es2.enter_context(nc.psum_tensor(f"l1_ptp{i}", [128, 1024], BF16)) for i in range(2)]
            aoT = [es2.enter_context(nc.sbuf_tensor(f"l1_aoT{i}", [128, 1024], BF16)) for i in range(2)]
            ps_o = [es2.enter_context(nc.psum_tensor(f"l1_po{i}", [128, 1024], F32)) for i in range(2)]
            blk = es2.enter_context(nc.Block())
            (sKL0, sKL1, sEXm, sOB, sTPm, sOM, sOA, sOS0, sOS1, sXL0, sXL1,
             sYL0, sYL1, sEXr, sTPe) = C.stage(15)
            sKL = [sKL0, sKL1]
            sOS = [sOS0, sOS1]
            sXL = [sXL0, sXL1]
            sYL = [sYL0, sYL1]
            bEXm, bOB, bTPm, bOM, bOA, bTPe = sEXm.n, sOB.n, sTPm.n, sOM.n, sOA.n, sTPe.n
            bEXr = sEXr.n
            bKL = [sKL0.n, sKL1.n]
            bOS = [sOS0.n, sOS1.n]
            bXL = [sXL0.n, sXL1.n]
            bYL = [sYL0.n, sYL1.n]

            def _store(sync, i):
                sOA.wait(sync, bOA + i + 1)
                sync.dma_start(out=x_out[i * 128:(i + 1) * 128, :],
                               in_=onew[i % 2][:]).then_inc(sOS[i % 2].h, 16)
                sOS[i % 2].n += 16

            @blk.sync
            def _(sync):
                sCC.wait(sync, kv_ready)
                kva = kv_all.rearrange("(a b) -> a b", b=2 * D)
                for i in range(nt):
                    if i >= 2:
                        sOB.wait(sync, bOB + i - 1)
                    for j in range(8):
                        sync.dma_start(out=kv8[i % 2][:, j, :],
                                       in_=kva[j * tpc + i * 128:j * tpc + (i + 1) * 128, :]).then_inc(sKL[i % 2].h, 16)
                        sKL[i % 2].n += 16
                    if i >= 2:
                        sOA.wait(sync, bOA + i - 1)
                    sync.dma_start(out=xres[i % 2][:],
                                   in_=x_in[i * 128:(i + 1) * 128, :]).then_inc(sXL[i % 2].h, 16)
                    sXL[i % 2].n += 16
                    sync.dma_start(out=y1r[i % 2][:],
                                   in_=y1_dram[i * 128:(i + 1) * 128, :]).then_inc(sYL[i % 2].h, 16)
                    sYL[i % 2].n += 16
                    if i >= 2:
                        _store(sync, i - 2)
                for i in range(max(0, nt - 2), nt):
                    _store(sync, i)
                sOS0.wait(sync)
                sOS1.wait(sync)

            @blk.vector
            def _(v):
                for i in range(nt):
                    sKL[i % 2].wait(v, bKL[i % 2] + (i // 2 + 1) * 128)
                    kvt = kv8[i % 2]
                    for j in range(8):
                        cV.inc(v.tensor_tensor(prod[:].rearrange("p (h e) -> p h e", h=H),
                                               q_sb[:, i, :, :],
                                               kvt[:, j, 0:D].rearrange("p (h e) -> p h e", h=H),
                                               ALU.mult))
                        cV.wait(v)
                        cV.inc(v.reduce_sum(sc[:, :, j:j + 1],
                                            prod[:].rearrange("p (h e) -> p h e", h=H), axis=AX.X))
                        cV.wait(v)
                    cV.inc(v.reduce_max(mx[:, :].rearrange("p (h o) -> p h o", o=1), sc[:, :, :],
                                        axis=AX.X, negate=True))
                    cV.wait(v)
                    cV.inc(v.tensor_tensor(st2[:], sc[:],
                                           mx[:].rearrange("p (h o) -> p h o", o=1).broadcast_to([128, H, 8]),
                                           ALU.add))
                    cV.wait(v)
                    v.engine_nop().then_inc(sEXr.h, 1)
                    sEXr.n += 1
                    sEXm.wait(v, bEXm + i + 1)
                    cV.inc(v.tensor_tensor(st2[:], st2[:], sel_sb[:].rearrange("p (h e) -> p h e", h=H), ALU.mult))
                    cV.wait(v)
                    cV.inc(v.reduce_sum(mx[:, :].rearrange("p (h o) -> p h o", o=1), st2[:], axis=AX.X))
                    cV.wait(v)
                    cV.inc(v.reciprocal(mx[:], mx[:]))
                    cV.wait(v)
                    cV.inc(v.tensor_tensor(st2[:], st2[:],
                                           mx[:].rearrange("p (h o) -> p h o", o=1).broadcast_to([128, H, 8]),
                                           ALU.mult))
                    cV.wait(v)
                    for j in range(8):
                        wj = st2[:, :, j:j + 1].broadcast_to([128, H, HD])
                        if j == 0:
                            cV.inc(v.tensor_tensor(oacc[:].rearrange("p (h e) -> p h e", h=H),
                                                   kvt[:, j, D:2 * D].rearrange("p (h e) -> p h e", h=H),
                                                   wj, ALU.mult))
                            cV.wait(v)
                        else:
                            cV.inc(v.tensor_tensor(prod[:].rearrange("p (h e) -> p h e", h=H),
                                                   kvt[:, j, D:2 * D].rearrange("p (h e) -> p h e", h=H),
                                                   wj, ALU.mult))
                            cV.wait(v)
                            cV.inc(v.tensor_tensor(oacc[:], oacc[:], prod[:], ALU.add))
                            cV.wait(v)
                    if i >= 2:
                        sTPm.wait(v, bTPm + i - 1)
                    cV.inc(v.tensor_copy(obf[i % 2][:], oacc[:]))
                    cV.wait(v)
                    v.engine_nop().then_inc(sOB.h, 1)
                    sOB.n += 1
                    sOM.wait(v, bOM + i + 1)
                    sXL[i % 2].wait(v, bXL[i % 2] + (i // 2 + 1) * 16)
                    sYL[i % 2].wait(v, bYL[i % 2] + (i // 2 + 1) * 16)
                    if i >= 2:
                        sOS[i % 2].wait(v, bOS[i % 2] + (i // 2) * 16)
                    cV.inc(v.tensor_tensor(onew[i % 2][:], xres[i % 2][:], y1r[i % 2][:], ALU.add))
                    cV.wait(v)
                    sOA.inc(v.tensor_tensor(onew[i % 2][:], onew[i % 2][:], ps_o[i % 2][:], ALU.add))

            @blk.scalar
            def _(s):
                for i in range(nt):
                    sEXr.wait(s, bEXr + i + 1)
                    ins = s.activation(st2[:], st2[:], AF.Exp, scale=1.0 / 32.0)
                    ins.then_inc(sEXm.h, 1)
                    sEXm.n += 1
                    sTPm.wait(s, bTPm + i + 1)
                    if i >= 2:
                        sOM.wait(s, bOM + i - 1)
                    ins = s.activation(aoT[i % 2][:], ps_tp[i % 2][:], AF.Copy)
                    ins.then_inc(sTPe.h, 1)
                    sTPe.n += 1

            @blk.tensor
            def _(t):
                for i in range(nt):
                    sOB.wait(t, bOB + i + 1)
                    if i >= 2:
                        sTPe.wait(t, bTPe + i - 1)
                    ins = None
                    for c in range(ND):
                        ins = t.transpose(ps_tp[i % 2][:, c * 128:(c + 1) * 128],
                                          obf[i % 2][:, c * 128:(c + 1) * 128], C.const["ident"][:])
                    sTPm.inc(ins)
                    sTPe.wait(t, bTPe + i + 1)
                    if i >= 2:
                        sOA.wait(t, bOA + i - 1)
                    ins = None
                    for dc in range(ND):
                        for j in range(D // 512):
                            ins = t.matmul(ps_o[i % 2][:, j * 512:(j + 1) * 512],
                                           aoT[i % 2][:, dc * 128:(dc + 1) * 128],
                                           woutT[:, dc, j * 512:(j + 1) * 512],
                                           start=(dc == 0), stop=(dc == ND - 1))
                    sOM.inc(ins)
        return [(sOS0, sOS0.n), (sOS1, sOS1.n)]


def build_l1_attn_program(scale=1):
    C = Core(scale=scale)
    nc = C.nc
    tpc = C.tpc
    x_in = nc.dram_tensor("x", [tpc, D], F32, kind="ExternalInput").ap()
    x_out = nc.dram_tensor("xo", [tpc, D], F32, kind="ExternalOutput").ap()
    W = {
        "wqkvT": nc.dram_tensor("L1a_wqkvT", [D, 3 * D], F32, kind="ExternalInput").ap(),
        "woutT": nc.dram_tensor("L1a_woutT", [D, D], F32, kind="ExternalInput").ap(),
        "ln_w": nc.dram_tensor("L1_attn_ln_w", [D], F32, kind="ExternalInput").ap(),
        "ln_b": nc.dram_tensor("L1_attn_ln_b", [D], F32, kind="ExternalInput").ap(),
        "sel": nc.dram_tensor("L1a_sel", [128, 128], F32, kind="ExternalInput").ap(),
    }
    y1 = nc.dram_tensor("l1_y1d", [tpc, D], F32).ap()
    kv_src = nc.dram_tensor("l1_kv_src", [tpc * 2 * D], BF16).ap()
    kv_all = nc.dram_tensor("l1_kv_all", [8 * tpc * 2 * D], BF16).ap()
    with C.es:
        build_consts(C)
        done = l1_attn(C, x_in, x_out, W, y1, kv_src, kv_all)
        with nc.Block() as blk:
            @blk.gpsimd
            def _(g):
                for s_, n_ in done:
                    s_.wait(g, n_)
    return C




def l2_attn(C, x_in, x_out, W, y1_dram, xz_src, xz_all, xconv_d, y3_d, op_d, rs_d, wait_first=None):
    nc = C.nc
    tpc, tf, nt = C.tpc, C.tf, C.nt
    DI = 2 * D
    NCC = DI // 128
    TH2 = tf // 2
    TQ4 = tf // 4
    NW4 = max(1, TQ4 // 512)
    JW4 = min(512, TQ4)
    NW2 = max(1, TH2 // 512)
    JW2 = min(512, TH2)
    sCC = C.sem("cc")
    cV, cS, cG = C.sem("cV"), C.sem("cS"), C.sem("cG")
    dW = C.sem("dW")
    dG2 = C.sem("dG2")

    with contextlib.ExitStack() as es:
        convw = es.enter_context(nc.sbuf_tensor("l2_cw", [128, 16, 4], F32))
        convb = es.enter_context(nc.sbuf_tensor("l2_cb", [128, 16], F32))
        alog = es.enter_context(nc.sbuf_tensor("l2_al", [128, 16, 8], F32))
        A_sb = es.enter_context(nc.sbuf_tensor("l2_A", [128, 16, 8], F32))
        wxb = es.enter_context(nc.sbuf_tensor("l2_wxb", [128, 16, 8], BF16))
        wxc = es.enter_context(nc.sbuf_tensor("l2_wxc", [128, 16, 8], BF16))
        wxd = es.enter_context(nc.sbuf_tensor("l2_wxd", [128, 16, 1], BF16))
        dr = es.enter_context(nc.sbuf_tensor("l2_dr", [128, 16], F32))
        selc = es.enter_context(nc.sbuf_tensor("l2_sel", [128, 1], F32))
        dsel = es.enter_context(nc.sbuf_tensor("l2_dsel", [128, 16], F32))
        g_row = es.enter_context(nc.sbuf_tensor("l2_gr", [1, D], BF16))
        b_row = es.enter_context(nc.sbuf_tensor("l2_br2", [1, D], BF16))
        w_bc = es.enter_context(nc.sbuf_tensor("l2_wbc", [128, D], BF16))
        b_bc = es.enter_context(nc.sbuf_tensor("l2_bbc", [128, D], BF16))
        with nc.Block() as blk:
            @blk.gpsimd
            def _(g):
                for name, dst in [("convw_r", convw[:, :, :]), ("convb_r", convb[:, :]),
                                  ("Alog_my", alog[:, :, :]), ("wxb", wxb[:, :, :]),
                                  ("wxc", wxc[:, :, :]), ("wxd", wxd[:, :, :]),
                                  ("Dr", dr[:, :]), ("sel", selc[:, :])]:
                    g.dma_start(out=dst, in_=W[name]).then_inc(dW.h, 16)
                    dW.n += 16
                g.dma_start(out=g_row[:], in_=W["gamma"][None, :]).then_inc(dW.h, 16)
                g.dma_start(out=b_row[:], in_=W["beta"][None, :]).then_inc(dW.h, 16)
                dW.n += 32
                dW.wait(g)
                cG.inc(g.tensor_scalar(dsel[:, :], dr[:, :], selc[:, 0:1], None, ALU.mult))
        cready = (dW.n, cG.n)

        with contextlib.ExitStack() as es2:
            ps_b = es2.enter_context(nc.psum_tensor("l2_psb", [128, D], F32))
            blk = es2.enter_context(nc.Block())
            sBM, sBE = C.stage(2)
            bBM, bBE = sBM.n, sBE.n

            @blk.scalar
            def _(s):
                cG.wait(s, cready[1])
                cS.inc(s.activation(A_sb[:, :, :], alog[:, :, :], AF.Exp))

            @blk.vector
            def _(v):
                cS.wait(v)
                cV.inc(v.tensor_scalar(A_sb[:, :, :], A_sb[:, :, :], -1.0, None, ALU.mult))
                sBM.wait(v, bBM + 1)
                sBE.inc(v.tensor_copy(w_bc[:], ps_b[:]))
                sBM.wait(v, bBM + 2)
                sBE.inc(v.tensor_copy(b_bc[:], ps_b[:]))

            @blk.tensor
            def _(t):
                wait_consts(C, t)
                dW.wait(t, cready[0])
                ins = None
                for j in range(D // 512):
                    ins = t.matmul(ps_b[:, j * 512:(j + 1) * 512], C.const["ones_row"][:, :],
                                   g_row[:, j * 512:(j + 1) * 512], start=True, stop=True)
                sBM.inc(ins)
                sBE.wait(t, bBE + 1)
                ins = None
                for j in range(D // 512):
                    ins = t.matmul(ps_b[:, j * 512:(j + 1) * 512], C.const["ones_row"][:, :],
                                   b_row[:, j * 512:(j + 1) * 512], start=True, stop=True)
                sBM.inc(ins)
            aff_marks = [(sBE, bBE + 2), (cV, cV.n)]

        y1_marks = ln_to_xt(C, x_in, None, "l2p", y_dram=y1_dram, wait_first=wait_first)
        es_w = contextlib.ExitStack()
        win = es_w.enter_context(nc.sbuf_tensor("l2_win", [128, ND, 4 * D], BF16))
        xt2 = es_w.enter_context(nc.sbuf_tensor("l2_xt2", [128, ND, tpc], BF16))
        xt_sem, xt_n = ln_to_xt(C, y1_dram, xt2, "l2a", affine=(w_bc, b_bc),
                                wait_first=y1_marks + aff_marks)

        bWi = dW.n
        with nc.Block() as blk:
            @blk.gpsimd
            def _(g):
                for dc in range(ND):
                    g.dma_start(out=win[:, dc, :], in_=W["W_in"][dc * 128:(dc + 1) * 128, :]).then_inc(dW.h, 16)
                    dW.n += 16
        win_ready = dW.n

        NWt = max(1, tpc // 512)
        JWt = min(512, tpc)
        with contextlib.ExitStack() as es2:
            ps_xz = [es2.enter_context(nc.psum_tensor(f"l2_pxz{i}", [128, tpc], F32)) for i in range(2)]
            stg = [es2.enter_context(nc.sbuf_tensor(f"l2_stg{i}", [128, tpc], BF16)) for i in range(2)]
            blk = es2.enter_context(nc.Block())
            sMM, sEV, sST0, sST1 = C.stage(4)
            sST = [sST0, sST1]
            bMM, bEV = sMM.n, sEV.n
            bST = [sST0.n, sST1.n]
            xz2d = xz_src.rearrange("(a b) -> a b", b=tpc)

            @blk.tensor
            def _(t):
                dW.wait(t, win_ready)
                xt_sem.wait(t, xt_n)
                for cc in range(4 * ND):
                    if cc >= 2:
                        sEV.wait(t, bEV + cc - 1)
                    pb = ps_xz[cc % 2]
                    ins = None
                    for dc in range(ND):
                        for j in range(NWt):
                            ins = t.matmul(pb[:, j * JWt:(j + 1) * JWt],
                                           win[:, dc, cc * 128:(cc + 1) * 128],
                                           xt2[:, dc, j * JWt:(j + 1) * JWt],
                                           start=(dc == 0), stop=(dc == ND - 1))
                    sMM.inc(ins)

            @blk.scalar
            def _(s):
                for cc in range(4 * ND):
                    sMM.wait(s, bMM + cc + 1)
                    if cc >= 2:
                        sST[cc % 2].wait(s, bST[cc % 2] + (cc // 2) * 16)
                    sEV.inc(s.activation(stg[cc % 2][:], ps_xz[cc % 2][:], AF.Copy))

            @blk.sync
            def _(sync):
                for cc in range(4 * ND):
                    sEV.wait(sync, bEV + cc + 1)
                    sync.dma_start(out=xz2d[cc * 128:(cc + 1) * 128, :],
                                   in_=stg[cc % 2][:]).then_inc(sST[cc % 2].h, 16)
                    sST[cc % 2].n += 16
                sST0.wait(sync)
                sST1.wait(sync)
        es_w.close()

        with nc.Block() as blk:
            @blk.gpsimd
            def _(g):
                sCC.inc(g.collective_compute("AllGather", ALU.bypass, replica_groups=PAIRS,
                                             ins=[xz_src.rearrange("(a b) -> a b", b=tpc)],
                                             outs=[xz_all.rearrange("(a b) -> a b", b=tpc)]))
        xz_ready = sCC.n

        es_scan = contextlib.ExitStack()
        bcB = es_scan.enter_context(nc.sbuf_tensor("l2_bcB", [128, 8, tf], BF16))
        bcC = es_scan.enter_context(nc.sbuf_tensor("l2_bcC", [128, 8, tf], BF16))
        dt_bc = es_scan.enter_context(nc.sbuf_tensor("l2_dtbc", [128, tf], F32))
        rows_d = nc.dram_tensor(f"l2_rows", [18 * tf], F32).ap().rearrange("(a b) -> a b", b=tf)
        xza = xz_all.rearrange("(a b) -> a b", b=tpc)
        xcd = xconv_d.rearrange("(a b) -> a b", b=tf)
        with contextlib.ExitStack() as es2:
            xcload = [es2.enter_context(nc.sbuf_tensor(f"l2_xcl{i}", [128, TQ4 + 3], BF16)) for i in range(2)]
            rstgB = es2.enter_context(nc.sbuf_tensor("l2_rsb", [8, TQ4], F32))
            rstgC = es2.enter_context(nc.sbuf_tensor("l2_rsc", [8, TQ4], F32))
            rstgD = es2.enter_context(nc.sbuf_tensor("l2_rsd2", [1, TQ4], F32))
            cacc = [es2.enter_context(nc.sbuf_tensor(f"l2_ca{i}", [128, TQ4], F32)) for i in range(2)]
            xcv = [es2.enter_context(nc.sbuf_tensor(f"l2_xcv{i}", [128, TQ4], BF16)) for i in range(2)]
            ps_B = es2.enter_context(nc.psum_tensor("l2_pB", [8, TQ4], F32))
            ps_Cp = es2.enter_context(nc.psum_tensor("l2_pC", [8, TQ4], F32))
            ps_d = es2.enter_context(nc.psum_tensor("l2_pd", [1, TQ4], F32))
            blk = es2.enter_context(nc.Block())
            sXL0, sXL1, sCVb, sSI, sXS0, sXS1, sXPM, sXPE, sXPS = C.stage(9)
            sXL = [sXL0, sXL1]
            sXS = [sXS0, sXS1]
            bCVb, bSI, bXPM, bXPE, bXPS = sCVb.n, sSI.n, sXPM.n, sXPE.n, sXPS.n
            bXL = [sXL0.n, sXL1.n]
            bXS = [sXS0.n, sXS1.n]
            nloads = []
            for tq in range(4):
                for cc in range(NCC):
                    g0 = tq * TQ4 - 3
                    crosses = (g0 < tpc) and ((tq + 1) * TQ4 > tpc)
                    nloads.append(2 if crosses else 1)
            cum = [[0], [0]]
            for it, nl in enumerate(nloads):
                p = it % 2
                cum[p].append(cum[p][-1] + nl * 16)
                cum[1 - p].append(cum[1 - p][-1])

            @blk.gpsimd
            def _(g):
                cG.inc(g.memset(xcload[0][:, 0:3], 0.0))
                cG.inc(g.memset(xcload[1][:, 0:3], 0.0))
            pad_mark = cG.n

            @blk.sync
            def _(sync):
                sCC.wait(sync, xz_ready)
                cG.wait(sync, pad_mark)
                def _xp_store(sync, tq):
                    sXPE.wait(sync, bXPE + tq + 1)
                    sync.dma_start(out=rows_d[0:8, tq * TQ4:(tq + 1) * TQ4],
                                   in_=rstgB[:, :]).then_inc(sXPS.h, 16)
                    sync.dma_start(out=rows_d[8:16, tq * TQ4:(tq + 1) * TQ4],
                                   in_=rstgC[:, :]).then_inc(sXPS.h, 16)
                    sync.dma_start(out=rows_d[16:17, tq * TQ4:(tq + 1) * TQ4],
                                   in_=rstgD[:, :]).then_inc(sXPS.h, 16)
                    sXPS.n += 48

                for tq in range(4):
                    if tq >= 1:
                        _xp_store(sync, tq - 1)
                    for cc in range(NCC):
                        it = tq * NCC + cc
                        if it >= 2:
                            sCVb.wait(sync, bCVb + it - 1)
                        g0 = tq * TQ4 - 3
                        gs = max(g0, 0)
                        off = gs - g0
                        for r in range(2):
                            lo = max(gs, r * tpc)
                            hi = min((tq + 1) * TQ4, (r + 1) * tpc)
                            if lo < hi:
                                src = xza[r * 4 * D + cc * 128:r * 4 * D + (cc + 1) * 128,
                                          lo - r * tpc:hi - r * tpc]
                                sync.dma_start(out=xcload[it % 2][:, off + (lo - gs):off + (hi - gs)],
                                               in_=src).then_inc(sXL[it % 2].h, 16)
                                sXL[it % 2].n += 16
                        if it >= 2:
                            it2 = it - 2
                            tq2, cc2 = it2 // NCC, it2 % NCC
                            sSI.wait(sync, bSI + it2 + 1)
                            sync.dma_start(out=xcd[cc2 * 128:(cc2 + 1) * 128, tq2 * TQ4:(tq2 + 1) * TQ4],
                                           in_=xcv[it2 % 2][:]).then_inc(sXS[it2 % 2].h, 16)
                            sXS[it2 % 2].n += 16
                for it2 in range(4 * NCC - 2, 4 * NCC):
                    tq2, cc2 = it2 // NCC, it2 % NCC
                    sSI.wait(sync, bSI + it2 + 1)
                    sync.dma_start(out=xcd[cc2 * 128:(cc2 + 1) * 128, tq2 * TQ4:(tq2 + 1) * TQ4],
                                   in_=xcv[it2 % 2][:]).then_inc(sXS[it2 % 2].h, 16)
                    sXS[it2 % 2].n += 16
                _xp_store(sync, 3)
                sXPS.wait(sync)
                sXS0.wait(sync)
                sXS1.wait(sync)

            @blk.vector
            def _(v):
                for tq in range(4):
                    for cc in range(NCC):
                        it = tq * NCC + cc
                        p = it % 2
                        sXL[p].wait(v, bXL[p] + cum[p][it + 1])
                        xl = xcload[p]
                        ca = cacc[p]
                        if it >= 2:
                            sSI.wait(v, bSI + it - 1)
                        cV.inc(v.tensor_scalar(ca[:], xl[:, 3:3 + TQ4], convw[:, cc, 3:4], None, ALU.mult))
                        cV.wait(v)
                        for tap in range(3):
                            cV.inc(v.scalar_tensor_tensor(ca[:], xl[:, tap:tap + TQ4],
                                                          convw[:, cc, tap:tap + 1], ca[:],
                                                          ALU.mult, ALU.add))
                            cV.wait(v)
                        v.engine_nop().then_inc(sCVb.h, 1)
                        sCVb.n += 1
                    sXPM.wait(v, bXPM + (tq + 1) * NCC)
                    if tq >= 1:
                        sXPS.wait(v, bXPS + tq * 3 * 16)
                    cV.inc(v.tensor_copy(rstgB[:, :], ps_B[:]))
                    cV.wait(v)
                    cV.inc(v.tensor_copy(rstgC[:, :], ps_Cp[:]))
                    cV.wait(v)
                    cV.inc(v.tensor_copy(rstgD[:, :], ps_d[:]))
                    cV.wait(v)
                    v.engine_nop().then_inc(sXPE.h, 1)
                    sXPE.n += 1

            @blk.scalar
            def _(s):
                for tq in range(4):
                    for cc in range(NCC):
                        it = tq * NCC + cc
                        sCVb.wait(s, bCVb + it + 1)
                        if it >= 2:
                            sXS[it % 2].wait(s, bXS[it % 2] + (it // 2) * 16)
                            sXPM.wait(s, bXPM + it - 1)
                        ins = s.activation(xcv[it % 2][:], cacc[it % 2][:], AF.Silu,
                                           bias=convb[:, cc:cc + 1])
                        ins.then_inc(sSI.h, 1)
                        sSI.n += 1

            @blk.tensor
            def _(t):
                for tq in range(4):
                    if tq >= 1:
                        sXPE.wait(t, bXPE + tq)
                    for cc in range(NCC):
                        it = tq * NCC + cc
                        sSI.wait(t, bSI + it + 1)
                        last = (cc == NCC - 1)
                        ins = None
                        for j in range(NW4):
                            t.matmul(ps_B[:, j * JW4:(j + 1) * JW4], wxb[:, cc, :],
                                     xcv[it % 2][:, j * JW4:(j + 1) * JW4],
                                     start=(cc == 0), stop=last)
                            t.matmul(ps_Cp[:, j * JW4:(j + 1) * JW4], wxc[:, cc, :],
                                     xcv[it % 2][:, j * JW4:(j + 1) * JW4],
                                     start=(cc == 0), stop=last)
                            ins = t.matmul(ps_d[:, j * JW4:(j + 1) * JW4], wxd[:, cc, :],
                                           xcv[it % 2][:, j * JW4:(j + 1) * JW4],
                                           start=(cc == 0), stop=last)
                        sXPM.inc(ins)
        conv_done = (C.stage(9)[8], C.stage(9)[8].n)

        with contextlib.ExitStack() as es2:
            dtr = es2.enter_context(nc.sbuf_tensor("l2_dtr", [1, tf], F32))
            dtr2 = es2.enter_context(nc.sbuf_tensor("l2_dtr2", [1, tf], F32))
            ps_bc = [es2.enter_context(nc.psum_tensor(f"l2_pbc{i}", [128, TH2], F32)) for i in range(2)]
            blk = es2.enter_context(nc.Block())
            sDL, sSP, sBM2, sBE2, sDS = C.stage(5)
            bDL, bSP, bBM2, bBE2, bDS = sDL.n, sSP.n, sBM2.n, sBE2.n, sDS.n

            @blk.sync
            def _(sync):
                conv_done[0].wait(sync, conv_done[1])
                sync.dma_start(out=dtr[:], in_=rows_d[16:17, :]).then_inc(sDL.h, 16)
                sDL.n += 16
                sSP.wait(sync, bSP + 3)
                sync.dma_start(out=rows_d[17:18, :], in_=dtr[:]).then_inc(sDS.h, 16)
                sDS.n += 16
                sDS.wait(sync)

            @blk.scalar
            def _(s):
                sSP.wait(s, bSP + 1)
                cS.inc(s.activation(dtr[:], dtr[:], AF.Abs))
                cS.wait(s)
                cS.inc(s.activation(dtr[:], dtr[:], AF.Exp, scale=-1.0))
                cS.wait(s)
                ins = s.activation(dtr[:], dtr[:], AF.Ln, bias=1.0)
                ins.then_inc(sSP.h, 1)
                sSP.n += 1

            @blk.vector
            def _(v):
                sDL.wait(v, bDL + 16)
                cV.inc(v.tensor_scalar(dtr2[:], dtr[:], 0.0, None, ALU.max))
                cV.wait(v)
                v.engine_nop().then_inc(sSP.h, 1)
                sSP.n += 1
                sSP.wait(v, bSP + 2)
                cV.inc(v.tensor_tensor(dtr[:], dtr[:], dtr2[:], ALU.add))
                cV.wait(v)
                v.engine_nop().then_inc(sSP.h, 1)
                sSP.n += 1
                for k in range(2):
                    sBM2.wait(v, bBM2 + k + 1)
                    cV.inc(v.tensor_copy(dt_bc[:, k * TH2:(k + 1) * TH2], ps_bc[k % 2][:]))
                    cV.wait(v)
                    v.engine_nop().then_inc(sBE2.h, 1)
                    sBE2.n += 1

            @blk.tensor
            def _(t):
                sSP.wait(t, bSP + 3)
                for k in range(2):
                    ins = None
                    for j in range(NW2):
                        ins = t.matmul(ps_bc[k % 2][:, j * JW2:(j + 1) * JW2],
                                       C.const["ones_row_f"][:, :],
                                       dtr[:, k * TH2 + j * JW2:k * TH2 + (j + 1) * JW2],
                                       start=True, stop=True)
                    sBM2.inc(ins)
        dt_done = (C.stage(5)[3], C.stage(5)[3].n)

        with contextlib.ExitStack() as es2:
            ps_bc = [es2.enter_context(nc.psum_tensor(f"l2_pbc2{i}", [128, TH2], F32)) for i in range(2)]
            rrow2 = [es2.enter_context(nc.sbuf_tensor(f"l2_rr2{i}", [1, tf], F32)) for i in range(2)]
            dtr_sb = es2.enter_context(nc.sbuf_tensor("l2_dts", [1, tf], F32))
            blk = es2.enter_context(nc.Block())
            sBM3, sBE3, sRM = C.stage(3)
            sRP = [C.sem("dR0"), C.sem("dR1")]
            bBM3, bBE3, bRM = sBM3.n, sBE3.n, sRM.n
            bRP = [sRP[0].n, sRP[1].n]
            sDTL = C.sem("dG2")

            @blk.gpsimd
            def _(g):
                dt_done[0].wait(g, dt_done[1])
                g.dma_start(out=dtr_sb[:], in_=rows_d[17:18, :]).then_inc(sDTL.h, 16)
                sDTL.n += 16
                sDTL.wait(g)
                cG.inc(g.engine_nop())
                for k in range(16):
                    p = k % 2
                    if k >= 2:
                        sBM3.wait(g, bBM3 + (k - 2) * 2 + 2)
                    src_row = rows_d[k:k + 1, :] if k < 8 else rows_d[k:k + 1, :]
                    g.dma_start(out=rrow2[p][:], in_=src_row).then_inc(sRP[p].h, 16)
                    sRP[p].n += 16

            @blk.vector
            def _(v):
                def _evac(idx):
                    sBM3.wait(v, bBM3 + idx + 1)
                    k2, half = idx // 2, idx % 2
                    dstt = bcB if k2 < 8 else bcC
                    nn = k2 if k2 < 8 else k2 - 8
                    cV.inc(v.tensor_copy(dstt[:, nn, half * TH2:(half + 1) * TH2],
                                         ps_bc[idx % 2][:]))
                    cV.wait(v)
                    v.engine_nop().then_inc(sBE3.h, 1)
                    sBE3.n += 1

                for k in range(16):
                    if k < 8:
                        p = k % 2
                        sRP[p].wait(v, bRP[p] + (k // 2 + 1) * 16)
                        cG.wait(v, cG.n)
                        cV.inc(v.tensor_tensor(rrow2[p][:], rrow2[p][:], dtr_sb[:], ALU.mult))
                        cV.wait(v)
                        v.engine_nop().then_inc(sRM.h, 1)
                        sRM.n += 1
                    if k >= 1:
                        _evac(2 * (k - 1))
                        _evac(2 * (k - 1) + 1)
                _evac(30)
                _evac(31)

            @blk.tensor
            def _(t):
                for k in range(16):
                    p = k % 2
                    if k < 8:
                        sRM.wait(t, bRM + k + 1)
                    else:
                        sRP[p].wait(t, bRP[p] + (k // 2 + 1) * 16)
                    for half in range(2):
                        idx = k * 2 + half
                        if idx >= 2:
                            sBE3.wait(t, bBE3 + idx - 1)
                        ins = None
                        for j in range(NW2):
                            ins = t.matmul(ps_bc[idx % 2][:, j * JW2:(j + 1) * JW2],
                                           C.const["ones_row_f"][:, :],
                                           rrow2[p][:, half * TH2 + j * JW2:half * TH2 + (j + 1) * JW2],
                                           start=True, stop=True)
                        sBM3.inc(ins)
        bc_done = (C.stage(3)[1], C.stage(3)[1].n)

        yacc = es_scan.enter_context(nc.sbuf_tensor("l2_ya", [128, TH2], F32))
        with contextlib.ExitStack() as es2:
            xcvc = [es2.enter_context(nc.sbuf_tensor(f"l2_sxc{i}", [128, tf], BF16)) for i in range(1)]
            dA = es2.enter_context(nc.sbuf_tensor("l2_dA", [128, TH2], F32))
            u_b = es2.enter_context(nc.sbuf_tensor("l2_u", [128, TH2], BF16))
            h_b = es2.enter_context(nc.sbuf_tensor("l2_h", [128, TH2], BF16))
            hC = es2.enter_context(nc.sbuf_tensor("l2_hC", [128, TH2], BF16))
            hlast = es2.enter_context(nc.sbuf_tensor("l2_hl", [128, 8], F32))
            zt = es2.enter_context(nc.sbuf_tensor("l2_zt", [128, TH2], BF16))
            szt = es2.enter_context(nc.sbuf_tensor("l2_szt", [128, TH2], BF16))
            y3b = [es2.enter_context(nc.sbuf_tensor(f"l2_y3{i}", [128, TH2], BF16)) for i in range(1)]
            blk = es2.enter_context(nc.Block())
            (sXC0, sXC1, sDA, sU, sH, sHC, sYA2, sZL0, sZL1, sSZ, sY3, sY3S0, sY3S1) = C.stage(13)
            sXC = [sXC0, sXC1]
            sZL = [sZL0, sZL1]
            sY3S = [sY3S0, sY3S1]
            (bDA, bU, bH, bHC, bYA2, bSZ, bY3) = (s.n for s in (sDA, sU, sH, sHC, sYA2, sSZ, sY3))
            bXC = [sXC0.n, sXC1.n]
            bZL = [sZL0.n, sZL1.n]
            bY3S = [sY3S0.n, sY3S1.n]
            y3d2 = y3_d.rearrange("(a b) -> a b", b=tf)

            @blk.sync
            def _(sync):
                bc_done[0].wait(sync, bc_done[1])
                for dc in range(NCC):
                    if dc >= 1:
                        sY3.wait(sync, bY3 + (dc - 1) * 2 + 2)
                    sync.dma_start(out=xcvc[0][:, :],
                                   in_=xcd[dc * 128:(dc + 1) * 128, :]).then_inc(sXC[dc % 2].h, 16)
                    sXC[dc % 2].n += 16
                    for th in range(2):
                        idx = dc * 2 + th
                        if idx >= 2:
                            sSZ.wait(sync, bSZ + idx - 1)
                        src = xza[th * 4 * D + 2 * D + dc * 128:th * 4 * D + 2 * D + (dc + 1) * 128, :]
                        sync.dma_start(out=zt[:, :] if False else zt[:],
                                       in_=src).then_inc(sZL[idx % 2].h, 16)
                        sZL[idx % 2].n += 16
                        sY3.wait(sync, bY3 + idx + 1)
                        sync.dma_start(out=y3d2[dc * 128:(dc + 1) * 128, th * TH2:(th + 1) * TH2],
                                       in_=y3b[0][:]).then_inc(sY3S[idx % 2].h, 16)
                        sY3S[idx % 2].n += 16
                sY3S0.wait(sync)
                sY3S1.wait(sync)

            @blk.scalar
            def _(s):
                for dc in range(NCC):
                    for th in range(2):
                        idx = dc * 2 + th
                        for n in range(8):
                            k = idx * 8 + n
                            if k >= 1:
                                sU.wait(s, bU + k)
                                sH.wait(s, bH + k)
                            ins = s.activation(dA[:], dt_bc[:, th * TH2:(th + 1) * TH2], AF.Exp,
                                               scale=A_sb[:, dc, n:n + 1])
                            ins.then_inc(sDA.h, 1)
                            sDA.n += 1
                        sZL[idx % 2].wait(s, bZL[idx % 2] + (idx // 2 + 1) * 16)
                        sY3.wait(s, bY3 + idx) if idx >= 1 else None
                        ins = s.activation(szt[:], zt[:], AF.Silu)
                        ins.then_inc(sSZ.h, 1)
                        sSZ.n += 1

            @blk.vector
            def _(v):
                for dc in range(NCC):
                    p = dc % 2
                    sXC[p].wait(v, bXC[p] + (dc // 2 + 1) * 16)
                    for th in range(2):
                        idx = dc * 2 + th
                        for n in range(8):
                            k = idx * 8 + n
                            sDA.wait(v, bDA + k + 1)
                            cV.inc(v.tensor_tensor(u_b[:], xcvc[0][:, th * TH2:(th + 1) * TH2],
                                                   bcB[:, n, th * TH2:(th + 1) * TH2], ALU.mult))
                            cV.wait(v)
                            v.engine_nop().then_inc(sU.h, 1)
                            sU.n += 1
                            init = 0.0 if th == 0 else hlast[:, n:n + 1]
                            cV.inc(v.tensor_tensor_scan(h_b[:], dA[:], u_b[:], init,
                                                        ALU.mult, ALU.add))
                            cV.wait(v)
                            if th == 0:
                                cV.inc(v.tensor_copy(hlast[:, n:n + 1], h_b[:, TH2 - 1:TH2]))
                                cV.wait(v)
                            v.engine_nop().then_inc(sH.h, 1)
                            sH.n += 1
                            cV.inc(v.tensor_tensor(hC[:], h_b[:],
                                                   bcC[:, n, th * TH2:(th + 1) * TH2], ALU.mult))
                            cV.wait(v)
                            if n == 0:
                                cV.inc(v.tensor_copy(yacc[:], hC[:]))
                            else:
                                cV.inc(v.tensor_tensor(yacc[:], yacc[:], hC[:], ALU.add))
                            cV.wait(v)
                        cV.inc(v.scalar_tensor_tensor(yacc[:], xcvc[0][:, th * TH2:(th + 1) * TH2],
                                                      dsel[:, dc:dc + 1], yacc[:],
                                                      ALU.mult, ALU.add))
                        cV.wait(v)
                        sSZ.wait(v, bSZ + idx + 1)
                        if idx >= 1:
                            p1 = (idx - 1) % 2
                            sY3S[p1].wait(v, bY3S[p1] + ((idx - 1) // 2 + 1) * 16)
                        cV.inc(v.tensor_tensor(y3b[0][:], yacc[:], szt[:], ALU.mult))
                        cV.wait(v)
                        v.engine_nop().then_inc(sY3.h, 1)
                        sY3.n += 1
        scan_done = (C.stage(13)[12], C.stage(13)[12].n)
        es_scan.close()

        wout = es.enter_context(nc.sbuf_tensor("l2_wo", [128, NCC, D], BF16))
        bWo = dW.n
        with nc.Block() as blk:
            @blk.gpsimd
            def _(g):
                for hc in range(NCC):
                    g.dma_start(out=wout[:, hc, :], in_=W["W_out"][hc * 128:(hc + 1) * 128, :]).then_inc(dW.h, 16)
                    dW.n += 16
        wout_ready = dW.n

        NT8 = tf // 512
        with contextlib.ExitStack() as es2:
            y3c = es2.enter_context(nc.sbuf_tensor("l2_y3c", [128, NCC, 512], BF16))
            outp = es2.enter_context(nc.sbuf_tensor("l2_outp", [128, 8, 512], BF16))
            ostg = [es2.enter_context(nc.sbuf_tensor(f"l2_os{i}", [128, 1024], BF16)) for i in range(2)]
            ps_o = [es2.enter_context(nc.psum_tensor(f"l2_po{i}", [128, 512], F32)) for i in range(4)]
            ps_tp = [es2.enter_context(nc.psum_tensor(f"l2_ptp{i}", [128, 1024], BF16)) for i in range(2)]
            blk = es2.enter_context(nc.Block())
            (sYC0, sYC1, sOM, sOE, sTPm, sTPe, sOS0, sOS1) = C.stage(8)
            sYC = [sYC0, sYC1]
            sOS = [sOS0, sOS1]
            bOM, bOE, bTPm, bTPe = sOM.n, sOE.n, sTPm.n, sTPe.n
            bYC = [sYC0.n, sYC1.n]
            bOS = [sOS0.n, sOS1.n]
            opd2 = op_d.rearrange("(a b) -> a b", b=D)
            y3d2g = y3_d.rearrange("(a b) -> a b", b=tf)
            NTT = 512 // 128

            @blk.sync
            def _(sync):
                scan_done[0].wait(sync, scan_done[1])
                for t8 in range(NT8):
                    if t8 >= 2:
                        sOM.wait(sync, bOM + (t8 - 2) * 8 + 8)
                    for hc in range(NCC):
                        sync.dma_start(out=y3c[:, hc, :],
                                       in_=y3d2g[hc * 128:(hc + 1) * 128, t8 * 512:(t8 + 1) * 512]).then_inc(sYC[t8 % 2].h, 16)
                        sYC[t8 % 2].n += 16
                    for tt in range(NTT):
                        idx = t8 * NTT + tt
                        sTPe.wait(sync, bTPe + idx + 1)
                        sync.dma_start(out=opd2[t8 * 512 + tt * 128:t8 * 512 + (tt + 1) * 128, :],
                                       in_=ostg[idx % 2][:]).then_inc(sOS[idx % 2].h, 16)
                        sOS[idx % 2].n += 16
                sOS0.wait(sync)
                sOS1.wait(sync)

            @blk.tensor
            def _(t):
                dW.wait(t, wout_ready)
                for t8 in range(NT8):
                    sYC[t8 % 2].wait(t, bYC[t8 % 2] + (t8 // 2 + 1) * NCC * 16)
                    for g2 in range(2):
                        for dmc in range(4):
                            mg = g2 * 4 + dmc
                            if t8 >= 1 or g2 >= 1:
                                sOE.wait(t, bOE + (t8 * 2 + g2 - 1) * 4 + dmc + 1)
                            ins = None
                            for hc in range(NCC):
                                ins = t.matmul(ps_o[dmc][:],
                                               wout[:, hc, mg * 128:(mg + 1) * 128],
                                               y3c[:, hc, :],
                                               start=(hc == 0), stop=(hc == NCC - 1))
                            sOM.inc(ins)
                    for tt in range(NTT):
                        idx = t8 * NTT + tt
                        sOE.wait(t, bOE + (t8 + 1) * 8)
                        if idx >= 2:
                            sTPe.wait(t, bTPe + idx - 1)
                        ins = None
                        for mg in range(8):
                            ins = t.transpose(ps_tp[idx % 2][:, mg * 128:(mg + 1) * 128],
                                              outp[:, mg, tt * 128:(tt + 1) * 128],
                                              C.const["ident"][:])
                        sTPm.inc(ins)

            @blk.scalar
            def _(s):
                for t8 in range(NT8):
                    for g2 in range(2):
                        for dmc in range(4):
                            mg = g2 * 4 + dmc
                            sOM.wait(s, bOM + t8 * 8 + g2 * 4 + dmc + 1)
                            if t8 >= 1 and g2 == 0 and dmc == 0:
                                sTPm.wait(s, bTPm + t8 * NTT)
                            ins = s.activation(outp[:, mg, :], ps_o[dmc][:], AF.Copy)
                            ins.then_inc(sOE.h, 1)
                            sOE.n += 1
                    for tt in range(NTT):
                        idx = t8 * NTT + tt
                        sTPm.wait(s, bTPm + idx + 1)
                        if idx >= 2:
                            sOS[idx % 2].wait(s, bOS[idx % 2] + (idx // 2) * 16)
                        ins = s.activation(ostg[idx % 2][:], ps_tp[idx % 2][:], AF.Copy)
                        ins.then_inc(sTPe.h, 1)
                        sTPe.n += 1

        with nc.Block() as blk:
            @blk.gpsimd
            def _(g):
                sCC.inc(g.collective_compute("ReduceScatter", ALU.add, replica_groups=PAIRS,
                                             ins=[op_d.rearrange("(a b) -> a b", b=D)],
                                             outs=[rs_d.rearrange("(a b) -> a b", b=D)]))
        rs_ready = sCC.n

        with contextlib.ExitStack() as es2:
            rsb = [es2.enter_context(nc.sbuf_tensor(f"l2_rs{i}", [128, D], BF16)) for i in range(2)]
            xres = [es2.enter_context(nc.sbuf_tensor(f"l2_fx{i}", [128, D], F32)) for i in range(2)]
            y1r = [es2.enter_context(nc.sbuf_tensor(f"l2_fy{i}", [128, D], F32)) for i in range(2)]
            onew = [es2.enter_context(nc.sbuf_tensor(f"l2_fo{i}", [128, D], F32)) for i in range(2)]
            blk = es2.enter_context(nc.Block())
            (sRL0, sRL1, sXL0b, sXL1b, sYL0, sYL1, sOA, sOSf0, sOSf1) = C.stage(9)
            sRL = [sRL0, sRL1]
            sXLb = [sXL0b, sXL1b]
            sYL = [sYL0, sYL1]
            sOSf = [sOSf0, sOSf1]
            bOA = sOA.n
            bRL = [sRL0.n, sRL1.n]
            bXLb = [sXL0b.n, sXL1b.n]
            bYL = [sYL0.n, sYL1.n]
            bOSf = [sOSf0.n, sOSf1.n]
            rsd2 = rs_d.rearrange("(a b) -> a b", b=D)

            def _store(sync, i):
                sOA.wait(sync, bOA + i + 1)
                sync.dma_start(out=x_out[i * 128:(i + 1) * 128, :],
                               in_=onew[i % 2][:]).then_inc(sOSf[i % 2].h, 16)
                sOSf[i % 2].n += 16

            @blk.sync
            def _(sync):
                sCC.wait(sync, rs_ready)
                for i in range(nt):
                    if i >= 2:
                        sOA.wait(sync, bOA + i - 1)
                    sync.dma_start(out=rsb[i % 2][:], in_=rsd2[i * 128:(i + 1) * 128, :]).then_inc(sRL[i % 2].h, 16)
                    sRL[i % 2].n += 16
                    sync.dma_start(out=xres[i % 2][:], in_=x_in[i * 128:(i + 1) * 128, :]).then_inc(sXLb[i % 2].h, 16)
                    sXLb[i % 2].n += 16
                    sync.dma_start(out=y1r[i % 2][:], in_=y1_dram[i * 128:(i + 1) * 128, :]).then_inc(sYL[i % 2].h, 16)
                    sYL[i % 2].n += 16
                    if i >= 2:
                        _store(sync, i - 2)
                for i in range(max(0, nt - 2), nt):
                    _store(sync, i)
                sOSf0.wait(sync)
                sOSf1.wait(sync)

            @blk.vector
            def _(v):
                for i in range(nt):
                    p = i % 2
                    sRL[p].wait(v, bRL[p] + (i // 2 + 1) * 16)
                    sXLb[p].wait(v, bXLb[p] + (i // 2 + 1) * 16)
                    sYL[p].wait(v, bYL[p] + (i // 2 + 1) * 16)
                    if i >= 2:
                        sOSf[p].wait(v, bOSf[p] + (i // 2) * 16)
                    cV.inc(v.tensor_tensor(onew[p][:], xres[p][:], y1r[p][:], ALU.add))
                    cV.wait(v)
                    sOA.inc(v.tensor_tensor(onew[p][:], onew[p][:], rsb[p][:], ALU.add))
        return [(sOSf0, sOSf0.n), (sOSf1, sOSf1.n)]



def build_l2_attn_program(scale=1):
    C = Core(scale=scale)
    nc = C.nc
    tpc, tf = C.tpc, C.tf
    DI = 2 * D
    x_in = nc.dram_tensor("x", [tpc, D], F32, kind="ExternalInput").ap()
    x_out = nc.dram_tensor("xo", [tpc, D], F32, kind="ExternalOutput").ap()
    W = {
        "W_in": nc.dram_tensor("L2_attn_W_in", [D, 2 * DI], F32, kind="ExternalInput").ap(),
        "W_out": nc.dram_tensor("L2_attn_W_out", [DI, D], F32, kind="ExternalInput").ap(),
        "gamma": nc.dram_tensor("L2_attn_gamma", [D], F32, kind="ExternalInput").ap(),
        "beta": nc.dram_tensor("L2_attn_beta", [D], F32, kind="ExternalInput").ap(),
        "convw_r": nc.dram_tensor("L2a_convw_r", [128, 16, 4], F32, kind="ExternalInput").ap(),
        "convb_r": nc.dram_tensor("L2a_convb_r", [128, 16], F32, kind="ExternalInput").ap(),
        "Alog_my": nc.dram_tensor("L2a_Alog_my", [128, 16, 8], F32, kind="ExternalInput").ap(),
        "wxb": nc.dram_tensor("L2a_wxb", [128, 16, 8], F32, kind="ExternalInput").ap(),
        "wxc": nc.dram_tensor("L2a_wxc", [128, 16, 8], F32, kind="ExternalInput").ap(),
        "wxd": nc.dram_tensor("L2a_wxd", [128, 16, 1], F32, kind="ExternalInput").ap(),
        "Dr": nc.dram_tensor("L2a_Dr", [128, 16], F32, kind="ExternalInput").ap(),
        "sel": nc.dram_tensor("L2a_sel", [128, 1], F32, kind="ExternalInput").ap(),
    }
    y1 = nc.dram_tensor("l2_y1d", [tpc, D], F32).ap()
    xz_src = nc.dram_tensor("l2_xz_src", [2 * DI * tpc], BF16).ap()
    xz_all = nc.dram_tensor("l2_xz_all", [2 * 2 * DI * tpc], BF16).ap()
    xconv_d = nc.dram_tensor("l2_xconv", [DI * tf], BF16).ap()
    y3_d = nc.dram_tensor("l2_y3d", [DI * tf], BF16).ap()
    op_d = nc.dram_tensor("l2_opd", [tf * D], BF16).ap()
    rs_d = nc.dram_tensor("l2_rsd", [tpc * D], BF16).ap()
    with C.es:
        build_consts(C)
        done = l2_attn(C, x_in, x_out, W, y1, xz_src, xz_all, xconv_d, y3_d, op_d, rs_d)
        with nc.Block() as blk:
            @blk.gpsimd
            def _(g):
                for s_, n_ in done:
                    s_.wait(g, n_)
    return C


def l2_derived(inp, core_id):
    DI = 2 * D
    r3 = lambda a: np.ascontiguousarray(a.reshape(16, 128, -1).transpose(1, 0, 2)).astype(np.float32)
    noff = (core_id % 2) * 8
    d = {}
    d["L2a_convw_r"] = r3(inp["L2_attn_conv_w"][:, 0, :])
    d["L2a_convb_r"] = r3(inp["L2_attn_conv_b"][:, None])[:, :, 0]
    d["L2a_Alog_my"] = r3(inp["L2_attn_A_log"])[:, :, noff:noff + 8]
    wx = inp["L2_attn_W_x"]
    d["L2a_wxb"] = r3(wx[:, noff:noff + 8])
    d["L2a_wxc"] = r3(wx[:, 16 + noff:16 + noff + 8])
    d["L2a_wxd"] = r3(wx[:, 32:33])
    d["L2a_Dr"] = r3(inp["L2_attn_D_skip"][:, None])[:, :, 0]
    d["L2a_sel"] = np.full((128, 1), 1.0 if core_id % 2 == 0 else 0.0, np.float32)
    return d




def l3_attn(C, x_in, x_out, W, kv_src, kv_all, wait_first=None):
    nc = C.nc
    tpc, tf, nt = C.tpc, C.tf, C.nt
    H, HD = 16, 64
    NKC = tf // 128
    QW = min(512, tpc)
    NQH = tpc // QW
    sCC = C.sem("cc")
    cV, cS, cG = C.sem("cV"), C.sem("cS"), C.sem("cG")
    dW = C.sem("dW")

    with contextlib.ExitStack() as es:
        gl = es.enter_context(nc.sbuf_tensor("l3_gl", [128, 8], F32))
        bl = es.enter_context(nc.sbuf_tensor("l3_bl", [128, 8], F32))
        lamc = es.enter_context(nc.sbuf_tensor("l3_lam", [128, 1], F32))
        nlam = es.enter_context(nc.sbuf_tensor("l3_nlam", [128, 1], F32))
        oml = es.enter_context(nc.sbuf_tensor("l3_oml", [128, 1], F32))
        ones_cb = es.enter_context(nc.sbuf_tensor("l3_onec", [128, 1], BF16))
        q1T = es.enter_context(nc.sbuf_tensor("l3_q1", [128, ND, tpc], BF16))
        q2T = es.enter_context(nc.sbuf_tensor("l3_q2", [128, ND, tpc], BF16))
        with nc.Block() as blk:
            @blk.gpsimd
            def _(g):
                g.dma_start(out=gl[:], in_=W["gam_r"]).then_inc(dW.h, 16)
                g.dma_start(out=bl[:], in_=W["bet_r"]).then_inc(dW.h, 16)
                g.dma_start(out=lamc[:], in_=W["lam_col"]).then_inc(dW.h, 16)
                dW.n += 48
                dW.wait(g)
                g.memset(ones_cb[:], 1.0)
                cG.inc(g.engine_nop())

            @blk.vector
            def _(v):
                cG.wait(v)
                cV.inc(v.tensor_scalar(nlam[:], lamc[:], -1.0, None, ALU.mult))
                cV.wait(v)
                cV.inc(v.tensor_scalar(oml[:], lamc[:], -1.0, 1.0, ALU.mult, ALU.add))
                cV.wait(v)
                cV.inc(v.tensor_scalar(gl[:], gl[:], oml[:, 0:1], None, ALU.mult))
                cV.wait(v)
                cV.inc(v.tensor_scalar(bl[:], bl[:], oml[:, 0:1], None, ALU.mult))
                cV.wait(v)
        pre_done = (cV, cV.n, cG, cG.n)

        es_qkv = contextlib.ExitStack()
        w5 = es_qkv.enter_context(nc.sbuf_tensor("l3_w5", [128, ND, 5 * D], BF16))
        xt = es_qkv.enter_context(nc.sbuf_tensor("l3_xt", [128, ND, tpc], BF16))
        bW5 = dW.n
        with nc.Block() as blk:
            @blk.gpsimd
            def _(g):
                for wi, wn in enumerate(["wq1c", "wk1c", "wq2c", "wk2c", "wvc"]):
                    for dc in range(ND):
                        g.dma_start(out=w5[:, dc, wi * D:(wi + 1) * D],
                                    in_=W[wn][dc * 128:(dc + 1) * 128, :]).then_inc(dW.h, 16)
                        dW.n += 16
        w5_ready = dW.n

        xt_sem, xt_n = ln_to_xt(C, x_in, xt, "l3a", wait_first=wait_first)

        NH2 = tpc // QW
        NQK = 4 * ND * NH2
        with contextlib.ExitStack() as es2:
            ps_qk = [es2.enter_context(nc.psum_tensor(f"l3pqk{i}", [128, QW], F32)) for i in range(2)]
            ps_v = [es2.enter_context(nc.psum_tensor(f"l3pv{i}", [128, min(D, 1024)], F32)) for i in range(2)]
            stage = [es2.enter_context(nc.sbuf_tensor(f"l3st{i}", [128, max(QW, D)], BF16)) for i in range(2)]
            blk = es2.enter_context(nc.Block())
            sMM, sEVq, sSTa, sSTb = C.stage(4)
            sST = [sSTa, sSTb]
            bMM, bEVq = sMM.n, sEVq.n
            bST = [sSTa.n, sSTb.n]

            @blk.tensor
            def _(t):
                dW.wait(t, w5_ready)
                xt_sem.wait(t, xt_n)
                for st in range(NQK):
                    cc, hf = st // NH2, st % NH2
                    if st >= 2:
                        sEVq.wait(t, bEVq + st - 1)
                    pb = ps_qk[st % 2]
                    ins = None
                    for dc in range(ND):
                        ins = t.matmul(pb[:, 0:QW],
                                       w5[:, dc, cc * 128:(cc + 1) * 128],
                                       xt[:, dc, hf * QW:(hf + 1) * QW],
                                       start=(dc == 0), stop=(dc == ND - 1))
                    sMM.inc(ins)
                for i in range(nt):
                    if NQK + i >= 2:
                        sEVq.wait(t, bEVq + NQK + i - 1)
                    pb = ps_v[i % 2]
                    ins = None
                    for dc in range(ND):
                        for j in range(D // 512):
                            ins = t.matmul(pb[:, j * 512:(j + 1) * 512],
                                           xt[:, dc, i * 128:(i + 1) * 128],
                                           w5[:, dc, 4 * D + j * 512:4 * D + (j + 1) * 512],
                                           start=(dc == 0), stop=(dc == ND - 1))
                    sMM.inc(ins)

            @blk.scalar
            def _(s):
                for st in range(NQK):
                    cc, hf = st // NH2, st % NH2
                    wsel = cc // ND
                    ccl = cc % ND
                    sMM.wait(s, bMM + st + 1)
                    if wsel == 0:
                        ins = s.activation(q1T[:, ccl, hf * QW:(hf + 1) * QW],
                                           ps_qk[st % 2][:], AF.Copy)
                    elif wsel == 2:
                        ins = s.activation(q2T[:, ccl, hf * QW:(hf + 1) * QW],
                                           ps_qk[st % 2][:], AF.Copy)
                    else:
                        sidx = _l3_sidx(st, NH2)
                        if sidx >= 2:
                            q = sidx % 2
                            sST[q].wait(s, bST[q] + (sidx // 2) * 16)
                        ins = s.activation(stage[st % 2][:, :QW], ps_qk[st % 2][:], AF.Copy)
                    sEVq.inc(ins)
                for i in range(nt):
                    sMM.wait(s, bMM + NQK + i + 1)
                    sidx = 2 * ND * NH2 + i
                    if sidx >= 2:
                        q = sidx % 2
                        sST[q].wait(s, bST[q] + (sidx // 2) * 16)
                    ins = s.activation(stage[i % 2][:, :D], ps_v[i % 2][:], AF.Copy)
                    sEVq.inc(ins)

            @blk.sync
            def _(sync):
                k1v = kv_src[0:D * tpc].rearrange("(p t) -> p t", t=tpc)
                k2v = kv_src[D * tpc:2 * D * tpc].rearrange("(p t) -> p t", t=tpc)
                vv = kv_src[2 * D * tpc:3 * D * tpc].rearrange("(p t) -> p t", t=D)
                for st in range(NQK):
                    cc, hf = st // NH2, st % NH2
                    wsel = cc // ND
                    ccl = cc % ND
                    if wsel not in (1, 3):
                        continue
                    sidx = _l3_sidx(st, NH2)
                    tgt = k1v if wsel == 1 else k2v
                    sEVq.wait(sync, bEVq + st + 1)
                    sync.dma_start(out=tgt[ccl * 128:(ccl + 1) * 128, hf * QW:(hf + 1) * QW],
                                   in_=stage[st % 2][:, :QW]).then_inc(sST[sidx % 2].h, 16)
                    sST[sidx % 2].n += 16
                for i in range(nt):
                    sidx = 2 * ND * NH2 + i
                    sEVq.wait(sync, bEVq + NQK + i + 1)
                    sync.dma_start(out=vv[i * 128:(i + 1) * 128, :],
                                   in_=stage[i % 2][:, :D]).then_inc(sST[sidx % 2].h, 16)
                    sST[sidx % 2].n += 16
                sSTa.wait(sync)
                sSTb.wait(sync)
        kv_written = [(sSTa, sSTa.n), (sSTb, sSTb.n)]
        es_qkv.close()
        attnT = es.enter_context(nc.sbuf_tensor("l3_at", [128, ND, tpc], BF16))

        with nc.Block() as blk:
            @blk.gpsimd
            def _(g):
                for s_, n_ in kv_written:
                    s_.wait(g, n_)
                sCC.inc(g.collective_compute("AllGather", ALU.bypass, replica_groups=PAIRS,
                                             ins=[kv_src.rearrange("(a b) -> a b", b=D)],
                                             outs=[kv_all.rearrange("(a b) -> a b", b=D)]))
        kv_ready = sCC.n

        v65 = es.enter_context(nc.sbuf_tensor("l3_v65", [128, tf // 128, H, HD + 1], BF16))
        NVT = tf // 128
        half_t = NVT // 2
        BL3 = 3 * tpc * D
        with contextlib.ExitStack() as es2:
            vstage = [es2.enter_context(nc.sbuf_tensor(f"l3vst{i}", [128, D], BF16)) for i in range(2)]
            kth1 = es2.enter_context(nc.sbuf_tensor("l3kt1", [128, tf], BF16))
            kth2 = es2.enter_context(nc.sbuf_tensor("l3kt2", [128, tf], BF16))
            probs = [es2.enter_context(nc.sbuf_tensor(f"l3pr{i}", [128, QW], BF16)) for i in range(2)]
            usb = es2.enter_context(nc.sbuf_tensor("l3usb", [64, QW], F32))
            u2sb = es2.enter_context(nc.sbuf_tensor("l3u2sb", [64, QW], F32))
            rrow = es2.enter_context(nc.sbuf_tensor("l3rr", [1, QW], F32))
            attnT64 = es2.enter_context(nc.sbuf_tensor("l3at64", [64, 2, tpc], BF16))
            ps_sc = [es2.enter_context(nc.psum_tensor(f"l3psc{i}", [128, QW], F32)) for i in range(2)]
            ps_u1 = es2.enter_context(nc.psum_tensor("l3psu1", [65, QW], F32))
            ps_u2 = es2.enter_context(nc.psum_tensor("l3psu2", [65, QW], F32))
            ps_rb = es2.enter_context(nc.psum_tensor("l3psrb", [64, QW], F32))
            blk = es2.enter_context(nc.Block())
            (sVL0, sVL1, sVC, sVO, sKL0, sKL1, sSC, sEX, sAV, sRC, sRB, sUE, sAT) = C.stage(13)
            sVL = [sVL0, sVL1]
            sKL = [sKL0, sKL1]
            sMG = [C.sem("dM0"), C.sem("dM1")]
            (bVC, bVO, bSC, bEX, bAV, bRC, bRB, bUE, bAT) = (
                s.n for s in (sVC, sVO, sSC, sEX, sAV, sRC, sRB, sUE, sAT))
            bVL = [sVL0.n, sVL1.n]
            bKL = [sKL0.n, sKL1.n]
            bMG = [sMG[0].n, sMG[1].n]

            @blk.sync
            def _(sync):
                sCC.wait(sync, kv_ready)
                for i in range(NVT):
                    if i >= 2:
                        sVC.wait(sync, bVC + i - 1)
                    r, ii = (0, i) if i < half_t else (1, i - half_t)
                    src = kv_all[r * BL3 + 2 * D * tpc + ii * 128 * D:
                                 r * BL3 + 2 * D * tpc + (ii + 1) * 128 * D]
                    sync.dma_start(out=vstage[i % 2][:],
                                   in_=src.rearrange("(p t) -> p t", t=D)).then_inc(sVL[i % 2].h, 16)
                    sVL[i % 2].n += 16
                for h in range(H):
                    if h >= 2:
                        sSC.wait(sync, bSC + (h - 1) * NQH * NKC * 2)
                    for ki, kth in ((0, kth1), (1, kth2)):
                        for r in range(2):
                            src = kv_all[r * BL3 + ki * D * tpc + h * HD * tpc:
                                         r * BL3 + ki * D * tpc + (h + 1) * HD * tpc]
                            sync.dma_start(out=kth[(h % 2) * 64:(h % 2) * 64 + 64, r * tpc:(r + 1) * tpc],
                                           in_=src.rearrange("(p t) -> p t", t=tpc)).then_inc(sKL[h % 2].h, 16)
                            sKL[h % 2].n += 16

            @blk.gpsimd
            def _(g):
                sVO.inc(g.memset(v65[:, :, :, HD:HD + 1], 1.0))
                for h in range(H):
                    sAT.wait(g, bAT + (h + 1) * NQH)
                    g.dma_start(out=attnT[(h % 2) * 64:(h % 2) * 64 + 64, h // 2, :],
                                in_=attnT64[:, h % 2, :]).then_inc(sMG[h % 2].h, 16)
                    sMG[h % 2].n += 16
                sMG[0].wait(g)
                sMG[1].wait(g)

            @blk.vector
            def _(v):
                for i in range(NVT):
                    sVL[i % 2].wait(v, bVL[i % 2] + (i // 2 + 1) * 16)
                    sVC.inc(v.tensor_copy(v65[:, i, :, 0:HD],
                                          vstage[i % 2][:].rearrange("p (h e) -> p h e", h=H)))
                for h in range(H):
                    for qh in range(NQH):
                        hq = h * NQH + qh
                        sAV.wait(v, bAV + (hq + 1) * 2 * NKC)
                        if hq >= 1:
                            sRB.wait(v, bRB + 2 * hq)
                        sRC.inc(v.reciprocal(rrow[:], ps_u1[64:65, :]))
                        sUE.wait(v, bUE + 2 * hq + 1)
                        sRB.wait(v, bRB + 2 * hq + 1)
                        cV.inc(v.tensor_tensor(usb[:], usb[:], ps_rb[:], ALU.mult))
                        cV.wait(v)
                        sRC.inc(v.reciprocal(rrow[:], ps_u2[64:65, :]))
                        sUE.wait(v, bUE + 2 * hq + 2)
                        sRB.wait(v, bRB + 2 * hq + 2)
                        cV.inc(v.tensor_tensor(u2sb[:], u2sb[:], ps_rb[:], ALU.mult))
                        cV.wait(v)
                        if qh == 0 and h >= 2:
                            sMG[h % 2].wait(v, bMG[h % 2] + (h // 2) * 16)
                        dst = attnT64[:, h % 2, qh * QW:(qh + 1) * QW]
                        sAT.inc(v.scalar_tensor_tensor(dst, u2sb[:], nlam[0:64, 0:1], usb[:],
                                                       ALU.mult, ALU.add))

            @blk.tensor
            def _(t):
                for s_, n_ in [pre_done[:2], pre_done[2:]]:
                    s_.wait(t, n_)
                sVC.wait(t, bVC + NVT)
                sVO.wait(t, bVO + 1)
                step = 0
                for h in range(H):
                    sKL[h % 2].wait(t, bKL[h % 2] + (h // 2 + 1) * 64)
                    for qh in range(NQH):
                        hq = h * NQH + qh
                        for pass2, (kth, qT, psu) in enumerate([(kth1, q1T, ps_u1), (kth2, q2T, ps_u2)]):
                            qrows = qT[(h % 2) * 64:(h % 2) * 64 + 64, h // 2, :]
                            for kc in range(NKC):
                                if step >= 2:
                                    sEX.wait(t, bEX + step - 1)
                                pb = ps_sc[step % 2]
                                ins = t.matmul(pb[:, 0:QW],
                                               kth[(h % 2) * 64:(h % 2) * 64 + 64, kc * 128:(kc + 1) * 128],
                                               qrows[:, qh * QW:(qh + 1) * QW],
                                               start=True, stop=True)
                                sSC.inc(ins)
                                sEX.wait(t, bEX + step + 1)
                                if kc == 0 and hq >= 1:
                                    sRC.wait(t, bRC + (hq - 1) * 2 + 1 + pass2)
                                    sUE.wait(t, bUE + hq * 2 + pass2)
                                ins = t.matmul(psu[:, 0:QW],
                                               v65[:, kc, h, :],
                                               probs[step % 2][:, 0:QW],
                                               start=(kc == 0), stop=(kc == NKC - 1))
                                sAV.inc(ins)
                                step += 1
                        for pass2 in range(2):
                            sRC.wait(t, bRC + hq * 2 + pass2 + 1)
                            if pass2 == 0 and hq >= 1:
                                sAT.wait(t, bAT + hq)
                            ins = t.matmul(ps_rb[:, 0:QW],
                                           C.const["ones_row_f"][:, 0:64],
                                           rrow[:, 0:QW], start=True, stop=True)
                            sRB.inc(ins)

            @blk.scalar
            def _(s):
                step = 0
                for h in range(H):
                    for qh in range(NQH):
                        hq = h * NQH + qh
                        for pass2 in range(2):
                            for kc in range(NKC):
                                sSC.wait(s, bSC + step + 1)
                                if step >= 2:
                                    sAV.wait(s, bAV + step - 1)
                                sEX.inc(s.activation(probs[step % 2][:], ps_sc[step % 2][:],
                                                     AF.Exp, bias=-SHIFT, scale=1.0 / 8.0))
                                step += 1
                            sAV.wait(s, bAV + step)
                            if hq * 2 + pass2 >= 1:
                                sAT.wait(s, bAT + hq)
                            tgt = usb if pass2 == 0 else u2sb
                            psu = ps_u1 if pass2 == 0 else ps_u2
                            sUE.inc(s.activation(tgt[:], psu[0:64, :], AF.Copy))
        attn_done = [(sMG[0], sMG[0].n), (sMG[1], sMG[1].n)]

        JW3 = min(512, tpc)
        NJ = tpc // JW3
        NTT3 = JW3 // 128
        with contextlib.ExitStack() as es2:
            sq_sb = es2.enter_context(nc.sbuf_tensor("l3_sq", [128, JW3], BF16))
            rowst = es2.enter_context(nc.sbuf_tensor("l3_rst", [1, 2 * JW3], F32))
            rbr = es2.enter_context(nc.sbuf_tensor("l3_rbr", [128, JW3], F32))
            rbmu = es2.enter_context(nc.sbuf_tensor("l3_rbmu", [128, JW3], F32))
            tbuf = es2.enter_context(nc.sbuf_tensor("l3_tb", [128, JW3], F32))
            obf = es2.enter_context(nc.sbuf_tensor("l3_obf", [128, 8, JW3], BF16))
            xres = [es2.enter_context(nc.sbuf_tensor(f"l3_fx{i}", [128, D], F32)) for i in range(2)]
            onew = [es2.enter_context(nc.sbuf_tensor(f"l3_fo{i}", [128, D], F32)) for i in range(2)]
            ps_s1 = es2.enter_context(nc.psum_tensor("l3_ps1", [1, JW3], F32))
            ps_s2 = es2.enter_context(nc.psum_tensor("l3_ps2", [1, JW3], F32))
            ps_rb2 = [es2.enter_context(nc.psum_tensor(f"l3_prb{i}", [128, JW3], F32)) for i in range(2)]
            ps_tp = [es2.enter_context(nc.psum_tensor(f"l3_ptp{i}", [128, 1024], BF16)) for i in range(2)]
            blk = es2.enter_context(nc.Block())
            (sSQ, sSM, sRW, sRBM, sRBE, sAP, sTPm, sOA, sXLa, sXLb, sOSa, sOSb) = C.stage(12)
            sXL = [sXLa, sXLb]
            sOS = [sOSa, sOSb]
            (bSQ, bSM, bRW, bRBM, bRBE, bAP, bTPm, bOA) = (
                s.n for s in (sSQ, sSM, sRW, sRBM, sRBE, sAP, sTPm, sOA))
            bXL = [sXLa.n, sXLb.n]
            bOS = [sOSa.n, sOSb.n]

            @blk.scalar
            def _(s):
                for j in range(NJ):
                    for dc in range(ND):
                        k = j * ND + dc
                        for s_, n_ in attn_done:
                            s_.wait(s, n_)
                        if k >= 1:
                            sSM.wait(s, bSM + 2 * k)
                        ins = s.activation(sq_sb[:], attnT[:, dc, j * JW3:(j + 1) * JW3], AF.Square)
                        ins.then_inc(sSQ.h, 1)
                        sSQ.n += 1
                    sSM.wait(s, bSM + 2 * (j * ND + ND))
                    sRW.wait(s, bRW + j * 3 + 1)
                    cS.inc(s.activation(rowst[:, 0:JW3], rowst[:, 0:JW3], AF.Ln, bias=LN_EPS))
                    cS.wait(s)
                    ins = s.activation(rowst[:, 0:JW3], rowst[:, 0:JW3], AF.Exp, scale=-0.5)
                    ins.then_inc(sRW.h, 1)
                    sRW.n += 1

            @blk.tensor
            def _(t):
                for j in range(NJ):
                    if j >= 1:
                        sRW.wait(t, bRW + (j - 1) * 3 + 1)
                    for dc in range(ND):
                        k = j * ND + dc
                        sSQ.wait(t, bSQ + k + 1)
                        ins = t.matmul(ps_s1[0:1, :],
                                       ones_cb[:, 0:1],
                                       attnT[:, dc, j * JW3:(j + 1) * JW3],
                                       start=(dc == 0), stop=(dc == ND - 1))
                        sSM.inc(ins)
                        ins = t.matmul(ps_s2[0:1, :],
                                       ones_cb[:, 0:1],
                                       sq_sb[:],
                                       start=(dc == 0), stop=(dc == ND - 1))
                        sSM.inc(ins)
                    sRW.wait(t, bRW + j * 3 + 3)
                    if j >= 1:
                        sRBE.wait(t, bRBE + j * 2)
                    ins = t.matmul(ps_rb2[0][:, 0:JW3], C.const["ones_row_f"][:, :],
                                   rowst[:, 0:JW3], start=True, stop=True)
                    sRBM.inc(ins)
                    ins = t.matmul(ps_rb2[1][:, 0:JW3], C.const["ones_row_f"][:, :],
                                   rowst[:, JW3:2 * JW3], start=True, stop=True)
                    sRBM.inc(ins)
                    for tt in range(NTT3):
                        i = j * NTT3 + tt
                        sAP.wait(t, bAP + j * ND + ND) if tt == 0 else None
                        if i >= 2:
                            sOA.wait(t, bOA + i - 1)
                        ins = None
                        for dc in range(ND):
                            ins = t.transpose(ps_tp[i % 2][:, dc * 128:(dc + 1) * 128],
                                              obf[:, dc, tt * 128:(tt + 1) * 128],
                                              C.const["ident"][:])
                        sTPm.inc(ins)

            @blk.vector
            def _(v):
                for j in range(NJ):
                    sSM.wait(v, bSM + 2 * (j * ND + ND))
                    cV.inc(v.tensor_scalar(rowst[:, JW3:2 * JW3], ps_s1[0:1, :], 1.0 / D, None, ALU.mult))
                    cV.wait(v)
                    cV.inc(v.tensor_scalar(tbuf[0:1, 0:JW3], rowst[:, JW3:2 * JW3], None, None, ALU.bypass)) if False else None
                    cV.inc(v.tensor_scalar(rowst[:, 0:JW3], ps_s2[0:1, :], 1.0 / D, None, ALU.mult))
                    cV.wait(v)
                    cV.inc(v.tensor_tensor(tbuf[0:1, 0:JW3], rowst[:, JW3:2 * JW3], rowst[:, JW3:2 * JW3], ALU.mult))
                    cV.wait(v)
                    cV.inc(v.tensor_tensor(rowst[:, 0:JW3], rowst[:, 0:JW3], tbuf[0:1, 0:JW3], ALU.subtract))
                    cV.wait(v)
                    v.engine_nop().then_inc(sRW.h, 1)
                    sRW.n += 1
                    sRW.wait(v, bRW + j * 3 + 2)
                    cV.inc(v.tensor_tensor(rowst[:, JW3:2 * JW3], rowst[:, JW3:2 * JW3], rowst[:, 0:JW3], ALU.mult))
                    cV.wait(v)
                    v.engine_nop().then_inc(sRW.h, 1)
                    sRW.n += 1
                    sRBM.wait(v, bRBM + j * 2 + 2)
                    cV.inc(v.tensor_copy(rbr[:], ps_rb2[0][:]))
                    cV.wait(v)
                    cV.inc(v.tensor_copy(rbmu[:], ps_rb2[1][:]))
                    cV.wait(v)
                    v.engine_nop().then_inc(sRBE.h, 2)
                    sRBE.n += 2
                    for dc in range(ND):
                        k = j * ND + dc
                        cV.inc(v.tensor_tensor(tbuf[:], attnT[:, dc, j * JW3:(j + 1) * JW3], rbr[:], ALU.mult))
                        cV.wait(v)
                        cV.inc(v.tensor_tensor(tbuf[:], tbuf[:], rbmu[:], ALU.subtract))
                        cV.wait(v)
                        cV.inc(v.tensor_scalar(obf[:, dc, :], tbuf[:], gl[:, dc:dc + 1], bl[:, dc:dc + 1],
                                               ALU.mult, ALU.add))
                        cV.wait(v)
                        v.engine_nop().then_inc(sAP.h, 1)
                        sAP.n += 1
                    for tt in range(NTT3):
                        i = j * NTT3 + tt
                        sTPm.wait(v, bTPm + i + 1)
                        sXL[i % 2].wait(v, bXL[i % 2] + (i // 2 + 1) * 16)
                        if i >= 2:
                            sOS[i % 2].wait(v, bOS[i % 2] + (i // 2) * 16)
                        sOA.inc(v.tensor_tensor(onew[i % 2][:], xres[i % 2][:], ps_tp[i % 2][:], ALU.add))

            @blk.sync
            def _(sync):
                def _store(sync, i):
                    sOA.wait(sync, bOA + i + 1)
                    sync.dma_start(out=x_out[i * 128:(i + 1) * 128, :],
                                   in_=onew[i % 2][:]).then_inc(sOS[i % 2].h, 16)
                    sOS[i % 2].n += 16
                for i in range(nt):
                    if i >= 2:
                        sOA.wait(sync, bOA + i - 1)
                    sync.dma_start(out=xres[i % 2][:],
                                   in_=x_in[i * 128:(i + 1) * 128, :]).then_inc(sXL[i % 2].h, 16)
                    sXL[i % 2].n += 16
                    if i >= 2:
                        _store(sync, i - 2)
                for i in range(max(0, nt - 2), nt):
                    _store(sync, i)
                sOSa.wait(sync)
                sOSb.wait(sync)
        return [(sOSa, sOSa.n), (sOSb, sOSb.n)]


def _l3_sidx(st, NH2):
    cc, hf = st // NH2, st % NH2
    wsel = cc // ND
    ccl = cc % ND
    if wsel == 1:
        return (ccl * NH2 + hf)
    if wsel == 3:
        return ND * NH2 + (ccl * NH2 + hf)
    return -1


def build_l3_attn_program(scale=1):
    C = Core(scale=scale)
    nc = C.nc
    tpc = C.tpc
    x_in = nc.dram_tensor("x", [tpc, D], F32, kind="ExternalInput").ap()
    x_out = nc.dram_tensor("xo", [tpc, D], F32, kind="ExternalOutput").ap()
    W = {k: nc.dram_tensor(f"L3a_{k}", [D, D], F32, kind="ExternalInput").ap()
         for k in ["wq1c", "wk1c", "wq2c", "wk2c", "wvc"]}
    W["gam_r"] = nc.dram_tensor("L3a_gam_r", [128, 8], F32, kind="ExternalInput").ap()
    W["bet_r"] = nc.dram_tensor("L3a_bet_r", [128, 8], F32, kind="ExternalInput").ap()
    W["lam_col"] = nc.dram_tensor("L3a_lam_col", [128, 1], F32, kind="ExternalInput").ap()
    kv_src = nc.dram_tensor("l3_kv_src", [3 * tpc * D], BF16).ap()
    kv_all = nc.dram_tensor("l3_kv_all", [2 * 3 * tpc * D], BF16).ap()
    with C.es:
        build_consts(C)
        done = l3_attn(C, x_in, x_out, W, kv_src, kv_all)
        with nc.Block() as blk:
            @blk.gpsimd
            def _(g):
                for s_, n_ in done:
                    s_.wait(g, n_)
    return C


def l3_derived(inp):
    d = {}
    for k, nm in [("wq1c", "L3_attn_wq1"), ("wk1c", "L3_attn_wk1"),
                  ("wq2c", "L3_attn_wq2"), ("wk2c", "L3_attn_wk2"),
                  ("wvc", "L3_attn_wv")]:
        d["L3a_" + k] = np.ascontiguousarray(
            inp[nm].transpose(1, 0, 2).reshape(D, D)).astype(np.float32)
    r = lambda b: np.ascontiguousarray(b.reshape(8, 128).T)
    d["L3a_gam_r"] = r(inp["L3_attn_ln_gamma"]).astype(np.float32)
    d["L3a_bet_r"] = r(inp["L3_attn_ln_beta"]).astype(np.float32)
    d["L3a_lam_col"] = np.full((128, 1), float(np.asarray(inp["L3_attn_lamb"])), np.float32)
    return d


def build_full_program(scale=1):
    C = Core(scale=scale)
    nc = C.nc
    tpc, tf = C.tpc, C.tf
    DI = 2 * D
    t = nc.dram_tensor
    x_in = t("x", [tpc, D], F32, kind="ExternalInput").ap()
    x_out = t("xo", [tpc, D], F32, kind="ExternalOutput").ap()
    xh = [t(f"xh{i}", [tpc, D], F32).ap() for i in range(7)]

    W0 = {
        "w_qkv": t("L0_attn_w_qkv", [D, 3 * D], F32, kind="ExternalInput").ap(),
        "w_out": t("L0_attn_w_out", [D, D], F32, kind="ExternalInput").ap(),
        "b_out": t("L0_attn_b_out", [D], F32, kind="ExternalInput").ap(),
    }
    W1 = {
        "wqkvT": t("L1a_wqkvT", [D, 3 * D], F32, kind="ExternalInput").ap(),
        "woutT": t("L1a_woutT", [D, D], F32, kind="ExternalInput").ap(),
        "ln_w": t("L1_attn_ln_w", [D], F32, kind="ExternalInput").ap(),
        "ln_b": t("L1_attn_ln_b", [D], F32, kind="ExternalInput").ap(),
        "sel": t("L1a_sel", [128, 128], F32, kind="ExternalInput").ap(),
    }
    W2 = {
        "W_in": t("L2_attn_W_in", [D, 2 * DI], F32, kind="ExternalInput").ap(),
        "W_out": t("L2_attn_W_out", [DI, D], F32, kind="ExternalInput").ap(),
        "gamma": t("L2_attn_gamma", [D], F32, kind="ExternalInput").ap(),
        "beta": t("L2_attn_beta", [D], F32, kind="ExternalInput").ap(),
        "convw_r": t("L2a_convw_r", [128, 16, 4], F32, kind="ExternalInput").ap(),
        "convb_r": t("L2a_convb_r", [128, 16], F32, kind="ExternalInput").ap(),
        "Alog_my": t("L2a_Alog_my", [128, 16, 8], F32, kind="ExternalInput").ap(),
        "wxb": t("L2a_wxb", [128, 16, 8], F32, kind="ExternalInput").ap(),
        "wxc": t("L2a_wxc", [128, 16, 8], F32, kind="ExternalInput").ap(),
        "wxd": t("L2a_wxd", [128, 16, 1], F32, kind="ExternalInput").ap(),
        "Dr": t("L2a_Dr", [128, 16], F32, kind="ExternalInput").ap(),
        "sel": t("L2a_sel", [128, 1], F32, kind="ExternalInput").ap(),
    }
    W3 = {k: t(f"L3a_{k}", [D, D], F32, kind="ExternalInput").ap()
          for k in ["wq1c", "wk1c", "wq2c", "wk2c", "wvc"]}
    W3["gam_r"] = t("L3a_gam_r", [128, 8], F32, kind="ExternalInput").ap()
    W3["bet_r"] = t("L3a_bet_r", [128, 8], F32, kind="ExternalInput").ap()
    W3["lam_col"] = t("L3a_lam_col", [128, 1], F32, kind="ExternalInput").ap()

    kv0_src = t("l0_kv_src", [2 * tpc * D], BF16).ap()
    kv0_all = t("l0_kv_all", [2 * 2 * tpc * D], BF16).ap()
    y1_1 = t("l1_y1d", [tpc, D], F32).ap()
    kv1_src = t("l1_kv_src", [tpc * 2 * D], BF16).ap()
    kv1_all = t("l1_kv_all", [8 * tpc * 2 * D], BF16).ap()
    y1_2 = t("l2_y1d", [tpc, D], F32).ap()
    xz_src = t("l2_xz_src", [2 * DI * tpc], BF16).ap()
    xz_all = t("l2_xz_all", [2 * 2 * DI * tpc], BF16).ap()
    xconv_d = t("l2_xconv", [DI * tf], BF16).ap()
    y3_d = t("l2_y3d", [DI * tf], BF16).ap()
    op_d = t("l2_opd", [tf * D], BF16).ap()
    rs_d = t("l2_rsd", [tpc * D], BF16).ap()
    kv3_src = t("l3_kv_src", [3 * tpc * D], BF16).ap()
    kv3_all = t("l3_kv_all", [2 * 3 * tpc * D], BF16).ap()

    with C.es:
        build_consts(C)
        cfg0 = moe_cfg(C, "l0_ffn")
        cfg1 = moe_cfg(C, "l1_ffn")
        cfg2 = moe_cfg(C, "l2_ffn")
        cfg3 = moe_cfg(C, "l3_ffn")
        d = l0_attn(C, x_in, xh[0], W0, kv0_src, kv0_all)
        d = moe_ffn(C, xh[0], xh[1], cfg0, wait_first=list(d))
        d = l1_attn(C, xh[1], xh[2], W1, y1_1, kv1_src, kv1_all, wait_first=list(d))
        d = moe_ffn(C, xh[2], xh[3], cfg1, wait_first=list(d))
        d = l2_attn(C, xh[3], xh[4], W2, y1_2, xz_src, xz_all, xconv_d, y3_d,
                    op_d, rs_d, wait_first=list(d))
        d = moe_ffn(C, xh[4], xh[5], cfg2, wait_first=list(d))
        d = l3_attn(C, xh[5], xh[6], W3, kv3_src, kv3_all, wait_first=list(d))
        d = moe_ffn(C, xh[6], x_out, cfg3, wait_first=list(d))
        with nc.Block() as blk:
            @blk.gpsimd
            def _(g):
                for s_, n_ in d:
                    s_.wait(g, n_)
    return C


def _ext_names(C):
    import concourse.mybir as mb
    names = set()
    for alloc in C.nc.m.functions[0].allocations:
        if isinstance(alloc, mb.MemoryLocationSet) and alloc.kind == "ExternalInput":
            for mem in alloc.memorylocations:
                names.add(mem.name)
    return names


def _l1a_sel(hf):
    sel = np.zeros((128, 128), np.float32)
    for h in range(16):
        for j in range(8):
            sel[:, h * 8 + j] = 1.0 if (j % 2) == hf else 0.0
    return sel


def kernel(**inputs):
    from concourse.bass_utils import run_bass_kernel_spmd
    import ml_dtypes
    x = np.asarray(inputs["L0_attn_x"], np.float32)
    pool = {k: np.asarray(v) for k, v in inputs.items()}
    pool.update(prep_derived(inputs))
    pool.update(l3_derived(inputs))
    pool["c_ident"] = np.eye(128, dtype=ml_dtypes.bfloat16)
    pool["c_identf"] = np.eye(128, dtype=np.float32)
    pool["L1a_wqkvT"] = np.ascontiguousarray(np.asarray(inputs["L1_attn_w_qkv"]).T).astype(np.float32)
    pool["L1a_woutT"] = np.ascontiguousarray(np.asarray(inputs["L1_attn_w_out"]).T).astype(np.float32)
    percore = []
    for c in range(8):
        d = l2_derived(inputs, c)
        d["L1a_sel"] = _l1a_sel(c % 2)
        percore.append(d)
    C = build_full_program()
    names = _ext_names(C)
    in_maps = []
    for c in range(8):
        m = {"x": np.ascontiguousarray(x[c // 2, (c % 2) * TPC:(c % 2 + 1) * TPC])}
        for k in names:
            if k == "x" or k == "partition_id":
                continue
            m[k] = percore[c][k] if k in percore[c] else pool[k]
        in_maps.append(m)
    res = run_bass_kernel_spmd(C.nc, in_maps, core_ids=list(range(8)))
    out = np.zeros((4, 4096, 1024), np.float32)
    for c in range(8):
        out[c // 2, (c % 2) * TPC:(c % 2 + 1) * TPC] = np.asarray(res.results[c]["xo"])
    return out


if __name__ == "__main__":
    C = build_l0_attn_program(scale=16)
    print("built ok")


# version 04
# speedup vs baseline: 1.3234x; 1.3234x over previous
import contextlib
import numpy as np

import concourse.bass as bass
import concourse.mybir as mybir
from concourse.bass_utils import run_bass_kernel_spmd

F32 = mybir.dt.float32
BF16 = mybir.dt.bfloat16
AF = mybir.ActivationFunctionType
ALU = mybir.AluOpType
AX = mybir.AxisListType

B, T, D = 4, 4096, 1024
NCORE = 8
TPC = T * B // NCORE
ND = D // 128
LN_EPS = 1e-5
SHIFT = 16.0
PAIRS = [[0, 1], [2, 3], [4, 5], [6, 7]]
ALL8 = [[0, 1, 2, 3, 4, 5, 6, 7]]

_PALETTE = ["p%02d" % i for i in range(40)]


class CSem:
    def __init__(self, handle):
        self.h = handle
        self.n = 0

    def inc(self, ins, k=1):
        ins.then_inc(self.h, k)
        self.n += k
        return self.n

    def wait(self, eng, n=None):
        eng.wait_ge(self.h, self.n if n is None else n)


class Core:
    def __init__(self, scale=1):
        self.nc = bass.Bass(target_bir_lowering=False)
        self.es = contextlib.ExitStack()
        self.scale = scale
        self.tpc = TPC // scale
        self.tf = T // scale
        self.nt = self.tpc // 128
        names = ["dG", "dG2", "dW", "dW2", "dM0", "dM1", "dR0", "dR1", "cc", "cV", "cS", "cT", "cG"] + _PALETTE
        self.sems = {n: CSem(self.es.enter_context(self.nc.semaphore(n)))
                     for n in names}
        self.const = {}

    def sem(self, name):
        return self.sems[name]

    def stage(self, k):
        return [self.sems[_PALETTE[i]] for i in range(k)]

    def sbuf(self, name, shape, dtype):
        return self.es.enter_context(self.nc.sbuf_tensor(name, shape, dtype))



FCONSTS = [LN_EPS, -SHIFT, 1.0 / D, -0.5, 1.0 / 32.0, -1.0 / D]


def build_consts(C, fvals=FCONSTS):
    nc = C.nc
    ident_in = nc.dram_tensor("c_ident", [128, 128], BF16, kind="ExternalInput").ap()
    identf_in = nc.dram_tensor("c_identf", [128, 128], F32, kind="ExternalInput").ap()
    C.const["ident"] = C.sbuf("ident", [128, 128], BF16)
    C.const["identf"] = C.sbuf("identf", [128, 128], F32)
    C.const["ones_row"] = C.sbuf("ones_row", [1, 128], BF16)
    C.const["ones_row_f"] = C.sbuf("ones_row_f", [1, 128], F32)
    fpool = C.sbuf("fpool", [128, max(1, len(fvals))], F32)
    dG = C.sem("dG")
    sG = C.sems[_PALETTE[-1]]
    with nc.Block() as blk:
        @blk.gpsimd
        def _(g):
            g.dma_start(out=C.const["ident"][:], in_=ident_in[:]).then_inc(dG.h, 16)
            g.dma_start(out=C.const["identf"][:], in_=identf_in[:]).then_inc(dG.h, 16)
            dG.n += 32
            dG.wait(g)
            g.memset(C.const["ones_row"][:], 1.0)
            ins = g.memset(C.const["ones_row_f"][:], 1.0)
            for i, v in enumerate(fvals):
                ins = g.memset(fpool[:, i:i + 1], float(v))
            sG.inc(ins)
    for i, v in enumerate(fvals):
        nc.const_aps.register(F32, float(v), fpool[:, i:i + 1])
    C.const_ready = (sG, sG.n)


def wait_consts(C, eng):
    s, n = C.const_ready
    s.wait(eng, n)



def ln_to_xt(C, x_src, xt, tag, affine=None, y_dram=None, wait_first=None, row0=0, nrows=None):
    assert not (affine is not None and y_dram is not None)
    nc = C.nc
    nt = (nrows // 128) if nrows is not None else C.nt
    ident = C.const["ident"]
    sLD0, sLD1, sNM, sRS, sY, sTP, sEV, sYD0, sYD1 = C.stage(9)
    sLD = [sLD0, sLD1]
    sYD = [sYD0, sYD1]
    bNM, bRS, bY, bTP, bEV = (s.n for s in (sNM, sRS, sY, sTP, sEV))
    bLD = [sLD0.n, sLD1.n]
    bYD = [sYD0.n, sYD1.n]

    with contextlib.ExitStack() as bes:
        xtile = [bes.enter_context(nc.sbuf_tensor(f"lnx{tag}{i}", [128, D], F32)) for i in range(2)]
        ytile = [bes.enter_context(nc.sbuf_tensor(f"lny{tag}{i}", [128, D], BF16)) for i in range(2)] if xt is not None else None
        yf32 = [bes.enter_context(nc.sbuf_tensor(f"lnyf{tag}{i}", [128, D], F32)) for i in range(2)] if y_dram is not None else None
        stats = bes.enter_context(nc.sbuf_tensor(f"lnst{tag}", [128, 4], F32))
        scrap = bes.enter_context(nc.sbuf_tensor(f"lnsc{tag}", [128, D], BF16))
        tp_ps = [bes.enter_context(nc.psum_tensor(f"lntp{tag}{i}", [128, 512], BF16)) for i in range(2)] if xt is not None else None
        blk = bes.enter_context(nc.Block())

        def _ystore(sync, i):
            sY.wait(sync, bY + i + 1)
            sync.dma_start(out=y_dram[row0 + i * 128:row0 + (i + 1) * 128, :],
                           in_=yf32[i % 2][:]).then_inc(sYD[i % 2].h, 16)
            sYD[i % 2].n += 16

        @blk.sync
        def _(sync):
            if wait_first is not None:
                for wf in (wait_first if isinstance(wait_first, list) else [wait_first]):
                    wf[0].wait(sync, wf[1])
            for i in range(nt):
                if i >= 2:
                    sY.wait(sync, bY + i - 1)
                sync.dma_start(out=xtile[i % 2][:],
                               in_=x_src[row0 + i * 128:row0 + (i + 1) * 128, :]).then_inc(sLD[i % 2].h, 16)
                sLD[i % 2].n += 16
                if y_dram is not None and i >= 2:
                    _ystore(sync, i - 2)
            if y_dram is not None:
                for i in range(max(0, nt - 2), nt):
                    _ystore(sync, i)
                sYD0.wait(sync)
                sYD1.wait(sync)

        cV = C.sem("cV")

        @blk.vector
        def _(v):
            for i in range(nt):
                sLD[i % 2].wait(v, bLD[i % 2] + (i // 2 + 1) * 16)
                xti = xtile[i % 2]
                cV.inc(v.reduce_sum(stats[:, 0:1], xti[:], axis=AX.X))
                cV.wait(v)
                sNM.inc(v.tensor_scalar(stats[:, 1:2], stats[:, 0:1], -1.0 / D, None, ALU.mult))
                sNM.wait(v)
                sRS.wait(v, bRS + i + 1)
                if xt is not None and i >= 2:
                    sTP.wait(v, bTP + (i - 2) * 2 + 2)
                if y_dram is not None and i >= 2:
                    sYD[i % 2].wait(v, bYD[i % 2] + (i // 2) * 16)
                if y_dram is not None:
                    ins = v.tensor_scalar(yf32[i % 2][:], xti[:], stats[:, 1:2],
                                          stats[:, 3:4], ALU.add, ALU.mult)
                    if xt is not None:
                        cV.inc(ins)
                        cV.wait(v)
                        ins = v.tensor_copy(ytile[i % 2][:], yf32[i % 2][:])
                else:
                    yi = ytile[i % 2]
                    ins = v.tensor_scalar(yi[:], xti[:], stats[:, 1:2], stats[:, 3:4],
                                          ALU.add, ALU.mult)
                    if affine is not None:
                        wb, bb = affine
                        cV.inc(ins)
                        cV.wait(v)
                        cV.inc(v.tensor_tensor(yi[:], yi[:], wb[:], ALU.mult))
                        cV.wait(v)
                        ins = v.tensor_tensor(yi[:], yi[:], bb[:], ALU.add)
                sY.inc(ins)
                if xt is not None and i >= 1:
                    _ln_evac(v, xt, tp_ps, i - 1, bTP, sTP, sEV)
            if xt is not None:
                _ln_evac(v, xt, tp_ps, nt - 1, bTP, sTP, sEV)

        cS = C.sem("cS")

        @blk.scalar
        def _(s):
            for i in range(nt):
                sNM.wait(s, bNM + i + 1)
                if i >= 1:
                    sY.wait(s, bY + i)
                cS.inc(s.activation(scrap[:], xtile[i % 2][:], AF.Square, bias=stats[:, 1:2],
                                    accum_out=stats[:, 2:3]))
                cS.wait(s)
                cS.inc(s.activation(stats[:, 3:4], stats[:, 2:3], AF.Ln, bias=LN_EPS, scale=1.0 / D))
                cS.wait(s)
                sRS.inc(s.activation(stats[:, 3:4], stats[:, 3:4], AF.Exp, scale=-0.5))

        if xt is not None:
            @blk.tensor
            def _(t):
                wait_consts(C, t)
                for i in range(nt):
                    sY.wait(t, bY + i + 1)
                    for g in range(2):
                        if i >= 1:
                            sEV.wait(t, bEV + (i - 1) * 2 + g + 1)
                        ins = None
                        for c in range(4 * g, 4 * g + 4):
                            ins = t.transpose(tp_ps[g][:, (c % 4) * 128:(c % 4) * 128 + 128],
                                              ytile[i % 2][:, c * 128:(c + 1) * 128], ident[:])
                        sTP.inc(ins)
    if xt is not None:
        return (sEV, sEV.n)
    return [(sYD0, sYD0.n), (sYD1, sYD1.n)]


def _ln_evac(v, xt, tp_ps, i, bTP, sTP, sEV):
    for g in range(2):
        sTP.wait(v, bTP + i * 2 + g + 1)
        dst = xt[:, 4 * g:4 * g + 4, i * 128:(i + 1) * 128]
        sEV.inc(v.tensor_copy(dst, tp_ps[g][:].rearrange("p (c n) -> p c n", c=4)))



def l0_attn(C, x_in, x_out, W, kv_src, kv_all, wait_first=None):
    nc = C.nc
    tpc, tf, nt = C.tpc, C.tf, C.nt
    H, HD = 16, 64
    NKC = tf // 128
    QW = min(getattr(C, "qw_cap", 1024), tpc)
    NQH = tpc // QW
    sCC = C.sem("cc")
    kv_src2d = kv_src.rearrange("(a b) -> a b", b=D)
    kv_all2d = kv_all.rearrange("(a b) -> a b", b=D)

    with contextlib.ExitStack() as es:
        wout = es.enter_context(nc.sbuf_tensor("l0_wout", [128, ND, D], BF16))
        bout = es.enter_context(nc.sbuf_tensor("l0_bout", [1, D], BF16))
        qT = es.enter_context(nc.sbuf_tensor("l0_qT", [128, ND, tpc], BF16))
        es_qkv = contextlib.ExitStack()
        wqkv = es_qkv.enter_context(nc.sbuf_tensor("l0_wqkv", [128, ND, 3 * D], BF16))
        sW = C.sem("dG")
        with nc.Block() as blk:
            @blk.gpsimd
            def _(g):
                for c in range(ND):
                    g.dma_start(out=wqkv[:, c, :], in_=W["w_qkv"][c * 128:(c + 1) * 128, :]).then_inc(sW.h, 16)
                    g.dma_start(out=wout[:, c, :], in_=W["w_out"][c * 128:(c + 1) * 128, :]).then_inc(sW.h, 16)
                g.dma_start(out=bout[:], in_=W["b_out"][None, :]).then_inc(sW.h, 16)
        sW.n += (2 * ND + 1) * 16
        wready = sW.n

        xt = es_qkv.enter_context(nc.sbuf_tensor("l0_xt", [128, ND, tpc], BF16))
        xt_sem, xt_n = ln_to_xt(C, x_in, xt, "l0a", wait_first=wait_first)

        NH2 = tpc // QW
        NQK = 2 * ND * NH2
        NSTG = ND * NH2 + nt
        with contextlib.ExitStack() as es2:
            ps_qk = [es2.enter_context(nc.psum_tensor(f"l0pqk{i}", [128, QW], F32)) for i in range(2)]
            ps_v = [es2.enter_context(nc.psum_tensor(f"l0pv{i}", [128, min(D, 1024)], F32)) for i in range(2)]
            stage = [es2.enter_context(nc.sbuf_tensor(f"l0st{i}", [128, max(QW, D)], BF16)) for i in range(2)]
            blk = es2.enter_context(nc.Block())
            sMM, sEVq, sSTa, sSTb = C.stage(4)
            sST = [sSTa, sSTb]
            bMM, bEVq = sMM.n, sEVq.n
            bST = [sSTa.n, sSTb.n]

            @blk.tensor
            def _(t):
                sW.wait(t, wready)
                xt_sem.wait(t, xt_n)
                for st in range(NQK):
                    cc, hf = st // NH2, st % NH2
                    if st >= 2:
                        sEVq.wait(t, bEVq + st - 1)
                    pb = ps_qk[st % 2]
                    ins = None
                    for dc in range(ND):
                        for j in range(QW // 512) or [0]:
                            jw = min(512, QW)
                            ins = t.matmul(pb[:, j * 512:j * 512 + jw],
                                           wqkv[:, dc, cc * 128:(cc + 1) * 128],
                                           xt[:, dc, hf * QW + j * 512:hf * QW + j * 512 + jw],
                                           start=(dc == 0), stop=(dc == ND - 1))
                    sMM.inc(ins)
                for i in range(nt):
                    if NQK + i >= 2:
                        sEVq.wait(t, bEVq + NQK + i - 1)
                    pb = ps_v[i % 2]
                    ins = None
                    for dc in range(ND):
                        for j in range(D // 512):
                            ins = t.matmul(pb[:, j * 512:(j + 1) * 512],
                                           xt[:, dc, i * 128:(i + 1) * 128],
                                           wqkv[:, dc, 2 * D + j * 512:2 * D + (j + 1) * 512],
                                           start=(dc == 0), stop=(dc == ND - 1))
                    sMM.inc(ins)

            @blk.scalar
            def _(s):
                for st in range(NQK):
                    cc, hf = st // NH2, st % NH2
                    sMM.wait(s, bMM + st + 1)
                    if cc < ND:
                        ins = s.activation(qT[:, cc, hf * QW:(hf + 1) * QW],
                                           ps_qk[st % 2][:], AF.Copy)
                    else:
                        sidx = st - ND * NH2
                        if sidx >= 2:
                            q = sidx % 2
                            sST[q].wait(s, bST[q] + (sidx // 2) * 16)
                        ins = s.activation(stage[st % 2][:, :QW], ps_qk[st % 2][:], AF.Copy)
                    sEVq.inc(ins)
                for i in range(nt):
                    sMM.wait(s, bMM + NQK + i + 1)
                    sidx = ND * NH2 + i
                    if sidx >= 2:
                        q = sidx % 2
                        sST[q].wait(s, bST[q] + (sidx // 2) * 16)
                    ins = s.activation(stage[i % 2][:, :D], ps_v[i % 2][:], AF.Copy)
                    sEVq.inc(ins)

            @blk.sync
            def _(sync):
                kT_view = kv_src[0:D * tpc].rearrange("(p t) -> p t", t=tpc)
                v_view = kv_src[D * tpc:2 * D * tpc].rearrange("(p t) -> p t", t=D)
                for st in range(ND * NH2, NQK):
                    cc, hf = st // NH2, st % NH2
                    kc = cc - ND
                    sidx = st - ND * NH2
                    sEVq.wait(sync, bEVq + st + 1)
                    sync.dma_start(out=kT_view[kc * 128:(kc + 1) * 128, hf * QW:(hf + 1) * QW],
                                   in_=stage[st % 2][:, :QW]).then_inc(sST[sidx % 2].h, 16)
                    sST[sidx % 2].n += 16
                for i in range(nt):
                    sEVq.wait(sync, bEVq + NQK + i + 1)
                    sidx = ND * NH2 + i
                    sync.dma_start(out=v_view[i * 128:(i + 1) * 128, :],
                                   in_=stage[i % 2][:, :D]).then_inc(sST[sidx % 2].h, 16)
                    sST[sidx % 2].n += 16
        kv_written = [(sSTa, sSTa.n), (sSTb, sSTb.n)]

        es_qkv.close()

        with nc.Block() as blk:
            @blk.gpsimd
            def _(g):
                for s_, n_ in kv_written:
                    s_.wait(g, n_)
                sCC.inc(g.collective_compute("AllGather", ALU.bypass,
                                             replica_groups=PAIRS,
                                             ins=[kv_src2d], outs=[kv_all2d]))
        kv_ready = sCC.n

        v65 = es.enter_context(nc.sbuf_tensor("l0_v65", [128, tf // 128, H, HD + 1], BF16))
        attnT = es.enter_context(nc.sbuf_tensor("l0_attnT", [128, ND, tpc], BF16))
        NVT = tf // 128
        half_t = NVT // 2
        with contextlib.ExitStack() as es2:
            vstage = [es2.enter_context(nc.sbuf_tensor(f"l0vst{i}", [128, D], BF16)) for i in range(2)]
            kth = es2.enter_context(nc.sbuf_tensor("l0kth", [128, tf], BF16))
            attnT64 = es2.enter_context(nc.sbuf_tensor("l0at64", [64, 2, tpc], BF16))
            probs = [es2.enter_context(nc.sbuf_tensor(f"l0pr{i}", [128, QW], BF16)) for i in range(2)]
            usb = es2.enter_context(nc.sbuf_tensor("l0usb", [64, QW], F32))
            rrow = es2.enter_context(nc.sbuf_tensor("l0rr", [1, QW], F32))
            ps_sc = [es2.enter_context(nc.psum_tensor(f"l0psc{i}", [128, QW], F32)) for i in range(2)]
            ps_u = es2.enter_context(nc.psum_tensor("l0psu", [65, QW], F32))
            ps_rb = es2.enter_context(nc.psum_tensor("l0psrb", [64, QW], F32))
            blk = es2.enter_context(nc.Block())
            (sVL0, sVL1, sVC, sVO, sKL0, sKL1, sSC, sEX, sAV, sRC, sRB, sUE, sAT) = C.stage(13)
            sMG0, sMG1 = C.sem("dM0"), C.sem("dM1")
            sMG = [sMG0, sMG1]
            sVL = [sVL0, sVL1]
            sKL = [sKL0, sKL1]
            (bVC, bVO, bSC, bEX, bAV, bRC, bRB, bUE, bAT) = (
                s.n for s in (sVC, sVO, sSC, sEX, sAV, sRC, sRB, sUE, sAT))
            bMG = [sMG0.n, sMG1.n]
            bVL = [sVL0.n, sVL1.n]
            bKL = [sKL0.n, sKL1.n]

            @blk.sync
            def _(sync):
                sCC.wait(sync, kv_ready)
                for i in range(NVT):
                    if i >= 2:
                        sVC.wait(sync, bVC + i - 1)
                    r, ii = (0, i) if i < half_t else (1, i - half_t)
                    src = kv_all[r * 2 * tpc * D + D * tpc + ii * 128 * D:
                                 r * 2 * tpc * D + D * tpc + (ii + 1) * 128 * D]
                    sync.dma_start(out=vstage[i % 2][:],
                                   in_=src.rearrange("(p t) -> p t", t=D)).then_inc(sVL[i % 2].h, 16)
                    sVL[i % 2].n += 16
                for h in range(H):
                    if h >= 2:
                        sSC.wait(sync, bSC + (h - 1) * NQH * NKC)
                    for r in range(2):
                        src = kv_all[r * 2 * tpc * D + h * HD * tpc:
                                     r * 2 * tpc * D + (h + 1) * HD * tpc]
                        sync.dma_start(out=kth[(h % 2) * 64:(h % 2) * 64 + 64, r * tpc:(r + 1) * tpc],
                                       in_=src.rearrange("(p t) -> p t", t=tpc)).then_inc(sKL[h % 2].h, 16)
                        sKL[h % 2].n += 16

            @blk.gpsimd
            def _(g):
                sVO.inc(g.memset(v65[:, :, :, HD:HD + 1], 1.0))
                for h in range(16):
                    sAT.wait(g, bAT + (h + 1) * NQH)
                    g.dma_start(out=attnT[(h % 2) * 64:(h % 2) * 64 + 64, h // 2, :],
                                in_=attnT64[:, h % 2, :]).then_inc(sMG[h % 2].h, 16)
                    sMG[h % 2].n += 16
                sMG[0].wait(g)
                sMG[1].wait(g)

            @blk.vector
            def _(v):
                for i in range(NVT):
                    sVL[i % 2].wait(v, bVL[i % 2] + (i // 2 + 1) * 16)
                    sVC.inc(v.tensor_copy(v65[:, i, :, 0:HD],
                                          vstage[i % 2][:].rearrange("p (h e) -> p h e", h=H)))
                for h in range(H):
                    for qh in range(NQH):
                        hq = h * NQH + qh
                        sAV.wait(v, bAV + (hq + 1) * NKC)
                        if hq >= 1:
                            sRB.wait(v, bRB + hq)
                        sRC.inc(v.reciprocal(rrow[:], ps_u[64:65, :]))
                        sUE.wait(v, bUE + hq + 1)
                        sRB.wait(v, bRB + hq + 1)
                        if qh == 0 and h >= 2:
                            sMG[h % 2].wait(v, bMG[h % 2] + (h // 2) * 16)
                        dst = attnT64[:, h % 2, qh * QW:(qh + 1) * QW]
                        sAT.inc(v.tensor_tensor(dst, usb[:], ps_rb[:], ALU.mult))

            @blk.tensor
            def _(t):
                sVC.wait(t, bVC + NVT)
                sVO.wait(t, bVO + 1)

                def _av(t, pstep, pkc, h, hq):
                    sEX.wait(t, bEX + pstep + 1)
                    if pkc == 0 and hq >= 1:
                        sRC.wait(t, bRC + hq)
                        sUE.wait(t, bUE + hq)
                    ins = None
                    for j in range(max(1, QW // 512)):
                        jw = min(512, QW)
                        ins = t.matmul(ps_u[:, j * 512:j * 512 + jw],
                                       v65[:, pkc, h, :],
                                       probs[pstep % 2][:, j * 512:j * 512 + jw],
                                       start=(pkc == 0), stop=(pkc == NKC - 1))
                    sAV.inc(ins)

                step = 0
                for h in range(H):
                    sKL[h % 2].wait(t, bKL[h % 2] + (h // 2 + 1) * 32)
                    qrows = qT[(h % 2) * 64:(h % 2) * 64 + 64, h // 2, :]
                    for qh in range(NQH):
                        hq = h * NQH + qh
                        for kc in range(NKC):
                            if step >= 2:
                                sEX.wait(t, bEX + step - 1)
                            pb = ps_sc[step % 2]
                            ins = None
                            for j in range(max(1, QW // 512)):
                                jw = min(512, QW)
                                ins = t.matmul(pb[:, j * 512:j * 512 + jw],
                                               kth[(h % 2) * 64:(h % 2) * 64 + 64, kc * 128:(kc + 1) * 128],
                                               qrows[:, qh * QW + j * 512:qh * QW + j * 512 + jw],
                                               start=True, stop=True)
                            sSC.inc(ins)
                            if kc >= 1:
                                _av(t, step - 1, kc - 1, h, hq)
                            step += 1
                        _av(t, step - 1, NKC - 1, h, hq)
                        sRC.wait(t, bRC + hq + 1)
                        if hq >= 1:
                            sAT.wait(t, bAT + hq)
                        ins = None
                        for j in range(max(1, QW // 512)):
                            jw = min(512, QW)
                            ins = t.matmul(ps_rb[:, j * 512:j * 512 + jw],
                                           C.const["ones_row_f"][:, 0:64],
                                           rrow[:, j * 512:j * 512 + jw],
                                           start=True, stop=True)
                        sRB.inc(ins)

            @blk.scalar
            def _(s):
                step = 0
                for h in range(H):
                    for qh in range(NQH):
                        hq = h * NQH + qh
                        for kc in range(NKC):
                            sSC.wait(s, bSC + step + 1)
                            if step >= 2:
                                sAV.wait(s, bAV + step - 1)
                            sEX.inc(s.activation(probs[step % 2][:], ps_sc[step % 2][:],
                                                 AF.Exp, bias=-SHIFT, scale=1.0 / 32.0))
                            step += 1
                        sAV.wait(s, bAV + step)
                        if hq >= 1:
                            sAT.wait(s, bAT + hq)
                        sUE.inc(s.activation(usb[:], ps_u[0:64, :], AF.Copy))
        attn_done = [(sMG0, sMG0.n), (sMG1, sMG1.n)]

        with contextlib.ExitStack() as es2:
            ps_o = [es2.enter_context(nc.psum_tensor(f"l0po{i}", [128, D], F32)) for i in range(2)]
            xres = [es2.enter_context(nc.sbuf_tensor(f"l0xr{i}", [128, D], F32)) for i in range(2)]
            onew = [es2.enter_context(nc.sbuf_tensor(f"l0on{i}", [128, D], F32)) for i in range(2)]
            blk = es2.enter_context(nc.Block())
            sXL0, sXL1, sOM, sOA, sOS0, sOS1 = C.stage(6)
            sXL = [sXL0, sXL1]
            sOS = [sOS0, sOS1]
            bOM, bOA = sOM.n, sOA.n
            bXL = [sXL0.n, sXL1.n]
            bOS = [sOS0.n, sOS1.n]

            def _store(sync, i):
                sOA.wait(sync, bOA + i + 1)
                sync.dma_start(out=x_out[i * 128:(i + 1) * 128, :],
                               in_=onew[i % 2][:]).then_inc(sOS[i % 2].h, 16)
                sOS[i % 2].n += 16

            @blk.sync
            def _(sync):
                for i in range(nt):
                    if i >= 2:
                        sOA.wait(sync, bOA + i - 1)
                    sync.dma_start(out=xres[i % 2][:],
                                   in_=x_in[i * 128:(i + 1) * 128, :]).then_inc(sXL[i % 2].h, 16)
                    sXL[i % 2].n += 16
                    if i >= 2:
                        _store(sync, i - 2)
                for i in range(max(0, nt - 2), nt):
                    _store(sync, i)
                sOS[0].wait(sync)
                sOS[1].wait(sync)

            @blk.tensor
            def _(t):
                for s_, n_ in attn_done:
                    s_.wait(t, n_)
                for i in range(nt):
                    if i >= 2:
                        sOA.wait(t, bOA + i - 1)
                    pb = ps_o[i % 2]
                    for dc in range(ND):
                        for j in range(D // 512):
                            t.matmul(pb[:, j * 512:(j + 1) * 512],
                                     attnT[:, dc, i * 128:(i + 1) * 128],
                                     wout[:, dc, j * 512:(j + 1) * 512],
                                     start=(dc == 0), stop=False)
                    ins = None
                    for j in range(D // 512):
                        ins = t.matmul(pb[:, j * 512:(j + 1) * 512],
                                       C.const["ones_row"][:, :],
                                       bout[:, j * 512:(j + 1) * 512],
                                       start=False, stop=True)
                    sOM.inc(ins)

            @blk.vector
            def _(v):
                for i in range(nt):
                    sOM.wait(v, bOM + i + 1)
                    sXL[i % 2].wait(v, bXL[i % 2] + (i // 2 + 1) * 16)
                    if i >= 2:
                        sOS[i % 2].wait(v, bOS[i % 2] + (i // 2) * 16)
                    sOA.inc(v.tensor_tensor(onew[i % 2][:], xres[i % 2][:],
                                            ps_o[i % 2][:], ALU.add))
        return [(sOS0, sOS0.n), (sOS1, sOS1.n)]





def moe_ffn(C, x_in, x_out, cfg, wait_first=None):
    nc = C.nc
    tpc = C.tpc
    E = 8
    TH = tpc // 2
    nt2 = TH // 128
    NW = max(1, TH // 512)
    JW = min(512, TH)
    dG = C.sem("dG")
    cV, cS = C.sem("cV"), C.sem("cS")
    done_marks = []
    GC = 0.7978845608028654
    GK = 0.044715

    def nh1(br):
        n = br["w1"].shape[1] // 128
        if br.get("w1u") is not None:
            n += br["w1u"].shape[1] // 128
        return n

    def nhh(br):
        return br["w1"].shape[1] // 128 if br["act"] == "glu_silu" else nh1(br)

    with contextlib.ExitStack() as es:
        wr_sb = es.enter_context(nc.sbuf_tensor(f"{cfg['tag']}_wr", [128, ND, E], BF16))
        rb_sb = es.enter_context(nc.sbuf_tensor(f"{cfg['tag']}_rb", [1, E], BF16))
        kb = cfg.get("kb") or 1
        b2_sb = es.enter_context(nc.sbuf_tensor(f"{cfg['tag']}_b2", [max(kb, 1), D], BF16))
        with nc.Block() as blk:
            @blk.gpsimd
            def _(g):
                for dc in range(ND):
                    g.dma_start(out=wr_sb[:, dc, :],
                                in_=cfg["router_w"][dc * 128:(dc + 1) * 128, :]).then_inc(dG.h, 16)
                    dG.n += 16
                if cfg.get("router_b") is not None:
                    g.dma_start(out=rb_sb[:], in_=cfg["router_b"][None, :]).then_inc(dG.h, 16)
                    dG.n += 16
                if cfg.get("b2all") is not None:
                    g.dma_start(out=b2_sb[0:kb, :], in_=cfg["b2all"]).then_inc(dG.h, 16)
                    dG.n += 16
        consts_ready = dG.n

        for th in range(2):
            row0 = th * TH
            with contextlib.ExitStack() as eh:
                xt = eh.enter_context(nc.sbuf_tensor(f"{cfg['tag']}_xt{th}", [128, ND, TH], BF16))
                xt_sem, xt_n = ln_to_xt(C, x_in, xt, f"{cfg['tag']}{th}", row0=row0, nrows=TH,
                                        wait_first=wait_first if th == 0 else None)

                gT9 = eh.enter_context(nc.sbuf_tensor(f"{cfg['tag']}_gt{th}", [9, TH], BF16))
                ones_th = eh.enter_context(nc.sbuf_tensor(f"{cfg['tag']}_o1{th}", [1, TH], BF16))
                with contextlib.ExitStack() as es2:
                    stx = es2.enter_context(nc.sbuf_tensor(f"{cfg['tag']}_stx{th}", [128, 32], F32))
                    lg_sb = [es2.enter_context(nc.sbuf_tensor(f"{cfg['tag']}_lg{th}{i}", [128, E], F32)) for i in range(2)]
                    e_sb = [es2.enter_context(nc.sbuf_tensor(f"{cfg['tag']}_es{th}{i}", [128, E], F32)) for i in range(2)]
                    gbf = [es2.enter_context(nc.sbuf_tensor(f"{cfg['tag']}_gbf{th}{i}", [128, E], BF16)) for i in range(2)]
                    ps_lg = [es2.enter_context(nc.psum_tensor(f"{cfg['tag']}_plg{th}{i}", [128, E], F32)) for i in range(2)]
                    ps_gt = es2.enter_context(nc.psum_tensor(f"{cfg['tag']}_pgt{th}", [8, TH], BF16))
                    blk = es2.enter_context(nc.Block())
                    sLG, sLC, sNM, sEX, sGB, sGT = C.stage(6)
                    sOD = C.sem("dG2")
                    bLG, bLC, bNM, bEX, bGB, bGT = (s.n for s in (sLG, sLC, sNM, sEX, sGB, sGT))

                    @blk.tensor
                    def _(t):
                        dG.wait(t, consts_ready)
                        xt_sem.wait(t, xt_n)
                        for i in range(nt2):
                            if i >= 2:
                                sLC.wait(t, bLC + i - 1)
                            pb = ps_lg[i % 2]
                            ins = None
                            for dc in range(ND):
                                ins = t.matmul(pb[:], xt[:, dc, i * 128:(i + 1) * 128],
                                               wr_sb[:, dc, :], start=(dc == 0),
                                               stop=(dc == ND - 1 and cfg.get("router_b") is None))
                            if cfg.get("router_b") is not None:
                                ins = t.matmul(pb[:], C.const["ones_row"][:, :],
                                               rb_sb[:], start=False, stop=True)
                            sLG.inc(ins)
                            sGB.wait(t, bGB + i + 1)
                            sGT.inc(t.transpose(ps_gt[:, i * 128:(i + 1) * 128], gbf[i % 2][:],
                                                C.const["ident"][:]))

                    @blk.scalar
                    def _(s):
                        for i in range(nt2):
                            p = i % 2
                            o = p * 16
                            sLG.wait(s, bLG + i + 1)
                            if i >= 2:
                                sEX.wait(s, bEX + i - 1)
                                sGB.wait(s, bGB + i - 1)
                            s.activation(lg_sb[p][:], ps_lg[p][:], AF.Copy).then_inc(sLC.h, 1)
                            sLC.n += 1
                            sNM.wait(s, bNM + i + 1)
                            sLC.wait(s)
                            s.activation(e_sb[p][:], lg_sb[p][:], AF.Exp,
                                         bias=stx[:, o + 0:o + 1],
                                         accum_out=stx[:, o + 1:o + 2]).then_inc(sEX.h, 1)
                            sEX.n += 1

                    @blk.vector
                    def _(v):
                        for i in range(nt2):
                            p = i % 2
                            o = p * 16
                            sLC.wait(v, bLC + i + 1)
                            if i >= 2:
                                sGB.wait(v, bGB + i - 1)
                            cV.inc(v.reduce_max(stx[:, o + 0:o + 1], lg_sb[p][:], axis=AX.X, negate=True))
                            cV.wait(v)
                            v.engine_nop().then_inc(sNM.h, 1)
                            sNM.n += 1
                            sEX.wait(v, bEX + i + 1)
                            cV.inc(v.max(stx[:, o + 2:o + 10], e_sb[p][:]))
                            cV.wait(v)
                            if cfg.get("l3_denom"):
                                cV.inc(v.tensor_scalar(stx[:, o + 10:o + 11], stx[:, o + 1:o + 2],
                                                       1e-6, None, ALU.mult))
                                cV.wait(v)
                                cV.inc(v.tensor_tensor(stx[:, o + 10:o + 11], stx[:, o + 10:o + 11],
                                                       stx[:, o + 2:o + 3], ALU.add))
                                cV.wait(v)
                                cV.inc(v.tensor_tensor(stx[:, o + 10:o + 11], stx[:, o + 10:o + 11],
                                                       stx[:, o + 3:o + 4], ALU.add))
                            else:
                                cV.inc(v.tensor_tensor(stx[:, o + 10:o + 11], stx[:, o + 2:o + 3],
                                                       stx[:, o + 3:o + 4], ALU.add))
                            cV.wait(v)
                            cV.inc(v.reciprocal(stx[:, o + 11:o + 12], stx[:, o + 10:o + 11]))
                            cV.wait(v)
                            cV.inc(v.tensor_tensor(stx[:, o + 12:o + 13], stx[:, o + 2:o + 3],
                                                   stx[:, o + 11:o + 12], ALU.mult))
                            cV.wait(v)
                            cV.inc(v.tensor_tensor(stx[:, o + 13:o + 14], stx[:, o + 3:o + 4],
                                                   stx[:, o + 11:o + 12], ALU.mult))
                            cV.wait(v)
                            cV.inc(v.tensor_scalar(lg_sb[p][:], e_sb[p][:], stx[:, o + 2:o + 3],
                                                   stx[:, o + 12:o + 13], ALU.is_equal, ALU.mult))
                            cV.wait(v)
                            cV.inc(v.tensor_scalar(e_sb[p][:], e_sb[p][:], stx[:, o + 3:o + 4],
                                                   stx[:, o + 13:o + 14], ALU.is_equal, ALU.mult))
                            cV.wait(v)
                            if i >= 2:
                                sGT.wait(v, bGT + i - 1)
                            cV.inc(v.tensor_tensor(gbf[p][:], lg_sb[p][:], e_sb[p][:], ALU.add))
                            cV.wait(v)
                            v.engine_nop().then_inc(sGB.h, 1)
                            sGB.n += 1
                        sGT.wait(v, bGT + nt2)
                        cV.inc(v.tensor_copy(gT9[0:8, :], ps_gt[:]))
                        cV.wait(v)
                        v.engine_nop().then_inc(sGB.h, 1)
                        sGB.n += 1

                    @blk.gpsimd
                    def _(g):
                        C.sem("cG").inc(g.memset(ones_th[:], 1.0))
                        C.sem("cG").wait(g)
                        g.dma_start(out=gT9[8:9, :], in_=ones_th[:]).then_inc(sOD.h, 16)
                        sOD.n += 16
                        sOD.wait(g)
                        C.sem("cG").inc(g.engine_nop())
                gt_marks = [(sGB, sGB.n), (C.sem("cG"), C.sem("cG").n)]

                NH1max = max(nh1(br) for br in cfg["branches"])
                NHhmax = max(nhh(br) for br in cfg["branches"])
                w1sb = eh.enter_context(nc.sbuf_tensor(f"{cfg['tag']}_w1{th}", [128, ND, NH1max * 128], BF16))
                w2sb = eh.enter_context(nc.sbuf_tensor(f"{cfg['tag']}_w2{th}", [128, NHhmax, D], BF16))
                b1c = eh.enter_context(nc.sbuf_tensor(f"{cfg['tag']}_b1{th}", [128, NH1max], F32))
                b1row = eh.enter_context(nc.sbuf_tensor(f"{cfg['tag']}_b1r{th}", [1, NH1max * 128], BF16))
                hT = eh.enter_context(nc.sbuf_tensor(f"{cfg['tag']}_hT{th}", [128, NHhmax, TH], BF16))
                gb = eh.enter_context(nc.sbuf_tensor(f"{cfg['tag']}_gb{th}", [128, TH], BF16))
                gRow = [eh.enter_context(nc.sbuf_tensor(f"{cfg['tag']}_gr{th}{i}", [1, TH], BF16)) for i in range(2)]
                yacc = eh.enter_context(nc.sbuf_tensor(f"{cfg['tag']}_ya{th}", [128, E, TH], F32))
                zb = [eh.enter_context(nc.sbuf_tensor(f"{cfg['tag']}_zb{th}{i}", [128, TH], BF16)) for i in range(2)]
                z2b = [eh.enter_context(nc.sbuf_tensor(f"{cfg['tag']}_z2{th}{i}", [128, TH], BF16)) for i in range(2)]
                with contextlib.ExitStack() as es2:
                    ps_h = [es2.enter_context(nc.psum_tensor(f"{cfg['tag']}_ph{th}{i}", [128, TH], F32)) for i in range(2)]
                    ps_y = [es2.enter_context(nc.psum_tensor(f"{cfg['tag']}_py{th}{i}", [128, TH], F32)) for i in range(2)]
                    blk = es2.enter_context(nc.Block())
                    sGM, sGBe, sHM, sZ1, sZ2, sZ3, sHT, sYM, sYA = C.stage(9)
                    sWL = C.sem("dW")
                    sW2 = C.sem("dW2")
                    sGR = [C.sem("dR0"), C.sem("dR1")]
                    (bWL, bW2, bGM, bGBe, bHM, bZ1, bZ2, bZ3, bHT, bYM, bYA) = (
                        s.n for s in (sWL, sW2, sGM, sGBe, sHM, sZ1, sZ2, sZ3, sHT, sYM, sYA))
                    bGR = [sGR[0].n, sGR[1].n]
                    BRS = cfg["branches"]
                    NBR = len(BRS)
                    hm_at, z1_at, ht_at, ym_at, ya_at, wl_at, wlA_at, gm_at = [], [], [], [], [], [], [], []
                    hm, z1t, htc, ym, ya, wl, gm = 0, 0, 0, 0, 0, 0, 0
                    for br in BRS:
                        hm_at.append(hm); z1_at.append(z1t); ht_at.append(htc)
                        ym_at.append(ym); ya_at.append(ya); wl_at.append(wl); gm_at.append(gm)
                        hm += nhh(br)
                        z1t += nhh(br)
                        htc += nhh(br)
                        ym += E
                        ya += E
                        w1cnt = (ND * (2 if br.get("w1u") is not None else 1) +
                                 (1 if br.get("b1r") is not None else 0) +
                                 (1 if br.get("b1row") is not None else 0)) * 16
                        wlA_at.append(wl + w1cnt)
                        wl += w1cnt
                        gm += 1 if br.get("gidx") is not None else 0

                    @blk.gpsimd
                    def _(g):
                        for bi, br in enumerate(BRS):
                            if bi >= 1:
                                sHM.wait(g, bHM + hm_at[bi])
                                sZ1.wait(g, bZ1 + z1_at[bi])
                            n1 = nh1(br)
                            n1g = br["w1"].shape[1] // 128
                            for dc in range(ND):
                                g.dma_start(out=w1sb[:, dc, 0:n1g * 128],
                                            in_=br["w1"][dc * 128:(dc + 1) * 128, :]).then_inc(sWL.h, 16)
                                sWL.n += 16
                                if br.get("w1u") is not None:
                                    g.dma_start(out=w1sb[:, dc, n1g * 128:n1 * 128],
                                                in_=br["w1u"][dc * 128:(dc + 1) * 128, :]).then_inc(sWL.h, 16)
                                    sWL.n += 16
                            if br.get("b1r") is not None:
                                g.dma_start(out=b1c[:, 0:n1], in_=br["b1r"]).then_inc(sWL.h, 16)
                                sWL.n += 16
                            if br.get("b1row") is not None:
                                g.dma_start(out=b1row[:, 0:n1 * 128],
                                            in_=br["b1row"][None, :]).then_inc(sWL.h, 16)
                                sWL.n += 16
                            if br.get("gidx") is not None:
                                gi = gm_at[bi]
                                for s_, n_ in gt_marks:
                                    s_.wait(g, n_)
                                if gi >= 2:
                                    sGM.wait(g, bGM + gi - 1)
                                g.dma_start(out=gRow[gi % 2][:],
                                            in_=gT9[br["gidx"]:br["gidx"] + 1, :]).then_inc(sGR[gi % 2].h, 16)
                                sGR[gi % 2].n += 16
                            if bi >= 1:
                                sYM.wait(g, bYM + ym_at[bi])
                            for hc in range(nhh(br)):
                                g.dma_start(out=w2sb[:, hc, :],
                                            in_=br["w2"][hc * 128:(hc + 1) * 128, :]).then_inc(sW2.h, 16)
                                sW2.n += 16

                    @blk.tensor
                    def _(t):
                        for s_, n_ in gt_marks:
                            s_.wait(t, n_)
                        for bi, br in enumerate(BRS):
                            n1, nh = nh1(br), nhh(br)
                            glu = br["act"] == "glu_silu"
                            sWL.wait(t, bWL + wlA_at[bi])
                            if br.get("gidx") is not None:
                                gi = gm_at[bi]
                                if bi >= 1:
                                    sYA.wait(t, bYA + ya_at[bi])
                                sGR[gi % 2].wait(t, bGR[gi % 2] + (gi // 2 + 1) * 16)
                                ins = None
                                for j in range(NW):
                                    ins = t.matmul(ps_y[0][:, j * JW:(j + 1) * JW],
                                                   C.const["ones_row"][:, :],
                                                   gRow[gi % 2][:, j * JW:(j + 1) * JW],
                                                   start=True, stop=True)
                                sGM.inc(ins)
                            for hc in range(nh):
                                if glu:
                                    if hc >= 1:
                                        sZ2.wait(t, bZ2 + z1_at[bi] + hc)
                                    n1g = br["w1"].shape[1] // 128
                                    for half in range(2):
                                        col0 = half * n1g * 128 + hc * 128
                                        ins = None
                                        for dc in range(ND):
                                            for j in range(NW):
                                                ins = t.matmul(ps_h[half][:, j * JW:(j + 1) * JW],
                                                               w1sb[:, dc, col0:col0 + 128],
                                                               xt[:, dc, j * JW:(j + 1) * JW],
                                                               start=(dc == 0), stop=(dc == ND - 1))
                                    sHM.inc(ins)
                                else:
                                    if hc >= 2:
                                        sZ1.wait(t, bZ1 + z1_at[bi] + hc - 1)
                                    elif bi >= 1:
                                        sZ1.wait(t, bZ1 + z1_at[bi])
                                    pb = ps_h[hc % 2]
                                    has_b1r = br.get("b1row") is not None
                                    ins = None
                                    for dc in range(ND):
                                        for j in range(NW):
                                            ins = t.matmul(pb[:, j * JW:(j + 1) * JW],
                                                           w1sb[:, dc, hc * 128:(hc + 1) * 128],
                                                           xt[:, dc, j * JW:(j + 1) * JW],
                                                           start=(dc == 0),
                                                           stop=(dc == ND - 1 and not has_b1r))
                                    if has_b1r:
                                        for j in range(NW):
                                            ins = t.matmul(pb[:, j * JW:(j + 1) * JW],
                                                           b1row[:, hc * 128:(hc + 1) * 128],
                                                           ones_th[:, j * JW:(j + 1) * JW],
                                                           start=False, stop=(j == NW - 1))
                                    sHM.inc(ins)
                            sW2.wait(t, bW2 + (ht_at[bi] + nh) * 16)
                            sHT.wait(t, bHT + ht_at[bi] + nh)
                            if br.get("gidx") is not None:
                                sGBe.wait(t, bGBe + gm_at[bi] + 1)
                            for od in range(E):
                                pb = ps_y[od % 2]
                                if od >= 2:
                                    sYA.wait(t, bYA + ya_at[bi] + od - 1)
                                elif bi >= 1:
                                    sYA.wait(t, bYA + ya_at[bi])
                                ins = None
                                bias_here = (bi == 0 and cfg.get("b2all") is not None)
                                for hc in range(nh):
                                    last = (hc == nh - 1) and not bias_here
                                    for j in range(NW):
                                        ins = t.matmul(pb[:, j * JW:(j + 1) * JW],
                                                       w2sb[:, hc, od * 128:(od + 1) * 128],
                                                       hT[:, hc, j * JW:(j + 1) * JW],
                                                       start=(hc == 0), stop=last)
                                if bias_here:
                                    for j in range(NW):
                                        ins = t.matmul(pb[:, j * JW:(j + 1) * JW],
                                                       b2_sb[0:kb, od * 128:(od + 1) * 128],
                                                       gT9[0:kb, j * JW:(j + 1) * JW],
                                                       start=False, stop=True)
                                sYM.inc(ins)

                    @blk.scalar
                    def _(s):
                        for bi, br in enumerate(BRS):
                            nh = nhh(br)
                            tanh_mode = br["act"] == "gelu_tanh"
                            for hc in range(nh):
                                p = hc % 2
                                step = z1_at[bi] + hc
                                sHM.wait(s, bHM + hm_at[bi] + hc + 1)
                                lag = 1 if br["act"] == "glu_silu" else 2
                                if hc >= lag:
                                    sHT.wait(s, bHT + ht_at[bi] + hc - lag + 1)
                                elif bi >= 1:
                                    sHT.wait(s, bHT + ht_at[bi])
                                b1 = b1c[:, hc:hc + 1] if br.get("b1r") is not None else 0.0
                                if br["act"] == "relu":
                                    ins = s.activation(zb[p][:], ps_h[p][:], AF.Relu, bias=b1)
                                elif br["act"] == "gelu":
                                    ins = s.activation(zb[p][:], ps_h[p][:], AF.Gelu, bias=b1)
                                elif br["act"] == "glu_silu":
                                    ins = s.activation(zb[0][:], ps_h[0][:], AF.Silu)
                                else:
                                    cS.inc(s.activation(zb[p][:], ps_h[p][:], AF.Copy))
                                    cS.wait(s)
                                    ins = s.activation(z2b[p][:], ps_h[p][:], AF.Square)
                                ins.then_inc(sZ1.h, 1)
                                sZ1.n += 1
                                if tanh_mode and hc >= 1:
                                    sZ2.wait(s, bZ2 + z1_at[bi] + hc)
                                    s.activation(z2b[(hc - 1) % 2][:], z2b[(hc - 1) % 2][:],
                                                 AF.Sigmoid, scale=2.0).then_inc(sZ3.h, 1)
                                    sZ3.n += 1
                            if tanh_mode:
                                sZ2.wait(s, bZ2 + z1_at[bi] + nh)
                                s.activation(z2b[(nh - 1) % 2][:], z2b[(nh - 1) % 2][:],
                                             AF.Sigmoid, scale=2.0).then_inc(sZ3.h, 1)
                                sZ3.n += 1

                    @blk.vector
                    def _(v):
                        for bi, br in enumerate(BRS):
                            nh = nhh(br)
                            tanh_mode = br["act"] == "gelu_tanh"
                            has_g = br.get("gidx") is not None
                            if has_g:
                                sGM.wait(v, bGM + gm_at[bi] + 1)
                                if bi >= 1:
                                    sHT.wait(v, bHT + ht_at[bi])
                                cV.inc(v.tensor_copy(gb[:], ps_y[0][:]))
                                cV.wait(v)
                                v.engine_nop().then_inc(sGBe.h, 1)
                                sGBe.n += 1
                            if bi >= 1:
                                sYM.wait(v, bYM + ym_at[bi])
                            def hwrite(hc, src):
                                if has_g:
                                    return v.tensor_tensor(hT[:, hc, :], src[:], gb[:], ALU.mult)
                                return v.tensor_copy(hT[:, hc, :], src[:])
                            def gt_fin(hc):
                                p2 = hc % 2
                                sZ3.wait(v, bZ3 + z1_at[bi] + hc + 1)
                                cV.inc(v.tensor_tensor(zb[p2][:], zb[p2][:], z2b[p2][:], ALU.mult))
                                cV.wait(v)
                                sHT.inc(hwrite(hc, zb[p2]))
                            for hc in range(nh):
                                p = hc % 2
                                sZ1.wait(v, bZ1 + z1_at[bi] + hc + 1)
                                if br["act"] in ("relu", "gelu"):
                                    ins = hwrite(hc, zb[p])
                                    sHT.inc(ins)
                                elif br["act"] == "glu_silu":
                                    cV.inc(v.tensor_tensor(zb[0][:], zb[0][:], ps_h[1][:], ALU.mult))
                                    cV.wait(v)
                                    v.engine_nop().then_inc(sZ2.h, 1)
                                    sZ2.n += 1
                                    sHT.inc(hwrite(hc, zb[0]))
                                else:
                                    cV.inc(v.tensor_scalar(z2b[p][:], z2b[p][:], GC * GK, GC,
                                                           ALU.mult, ALU.add))
                                    cV.wait(v)
                                    cV.inc(v.tensor_tensor(z2b[p][:], z2b[p][:], zb[p][:], ALU.mult))
                                    cV.wait(v)
                                    v.engine_nop().then_inc(sZ2.h, 1)
                                    sZ2.n += 1
                                    if hc >= 1:
                                        gt_fin(hc - 1)
                            if tanh_mode:
                                gt_fin(nh - 1)
                            for od in range(E):
                                sYM.wait(v, bYM + ym_at[bi] + od + 1)
                                if bi == 0:
                                    ins = v.tensor_copy(yacc[:, od, :], ps_y[od % 2][:])
                                else:
                                    ins = v.tensor_tensor(yacc[:, od, :], yacc[:, od, :],
                                                          ps_y[od % 2][:], ALU.add)
                                sYA.inc(ins)

                _moe_final(C, cfg["tag"], th, x_in, x_out, yacc, row0, nt2,
                           (sYA, sYA.n), done_marks)
    return done_marks



def build_l0_attn_program(scale=1, qw_cap=None):
    C = Core(scale=scale)
    if qw_cap:
        C.qw_cap = qw_cap
    nc = C.nc
    tpc = C.tpc
    x_in = nc.dram_tensor("x", [tpc, D], F32, kind="ExternalInput").ap()
    x_out = nc.dram_tensor("xo", [tpc, D], F32, kind="ExternalOutput").ap()
    W = {
        "w_qkv": nc.dram_tensor("L0_attn_w_qkv", [D, 3 * D], F32, kind="ExternalInput").ap(),
        "w_out": nc.dram_tensor("L0_attn_w_out", [D, D], F32, kind="ExternalInput").ap(),
        "b_out": nc.dram_tensor("L0_attn_b_out", [D], F32, kind="ExternalInput").ap(),
    }
    kv_src = nc.dram_tensor("l0_kv_src", [2 * tpc * D], BF16).ap()
    kv_all = nc.dram_tensor("l0_kv_all", [2 * 2 * tpc * D], BF16).ap()
    with C.es:
        build_consts(C)
        done = l0_attn(C, x_in, x_out, W, kv_src, kv_all)
        with nc.Block() as blk:
            @blk.gpsimd
            def _(g):
                for s_, n_ in done:
                    s_.wait(g, n_)
    return C





def _moe_final(C, tag, th, x_in, x_out, yacc, row0, nt2, ya_mark, done_marks):
    nc = C.nc
    E = 8
    with contextlib.ExitStack() as es2:
        ps_tp = [es2.enter_context(nc.psum_tensor(f"{tag}_ptp{th}{i}", [128, 1024], F32)) for i in range(2)]
        xres = [es2.enter_context(nc.sbuf_tensor(f"{tag}_fx{th}{i}", [128, D], F32)) for i in range(2)]
        onew = [es2.enter_context(nc.sbuf_tensor(f"{tag}_fo{th}{i}", [128, D], F32)) for i in range(2)]
        blk = es2.enter_context(nc.Block())
        sXL0, sXL1, sTP, sOA, sOS0, sOS1 = C.stage(6)
        sXL = [sXL0, sXL1]
        sOS = [sOS0, sOS1]
        bTP, bOA = sTP.n, sOA.n
        bXL = [sXL0.n, sXL1.n]
        bOS = [sOS0.n, sOS1.n]

        def _store(sync, i):
            sOA.wait(sync, bOA + i + 1)
            sync.dma_start(out=x_out[row0 + i * 128:row0 + (i + 1) * 128, :],
                           in_=onew[i % 2][:]).then_inc(sOS[i % 2].h, 16)
            sOS[i % 2].n += 16

        @blk.sync
        def _(sync):
            for i in range(nt2):
                if i >= 2:
                    sOA.wait(sync, bOA + i - 1)
                sync.dma_start(out=xres[i % 2][:],
                               in_=x_in[row0 + i * 128:row0 + (i + 1) * 128, :]).then_inc(sXL[i % 2].h, 16)
                sXL[i % 2].n += 16
                if i >= 2:
                    _store(sync, i - 2)
            for i in range(max(0, nt2 - 2), nt2):
                _store(sync, i)
            sOS0.wait(sync)
            sOS1.wait(sync)

        @blk.tensor
        def _(t):
            ya_mark[0].wait(t, ya_mark[1])
            for i in range(nt2):
                if i >= 2:
                    sOA.wait(t, bOA + i - 1)
                ins = None
                for od in range(E):
                    ins = t.transpose(ps_tp[i % 2][:, od * 128:(od + 1) * 128],
                                      yacc[:, od, i * 128:(i + 1) * 128], C.const["identf"][:])
                sTP.inc(ins)

        @blk.vector
        def _(v):
            for i in range(nt2):
                sTP.wait(v, bTP + i + 1)
                sXL[i % 2].wait(v, bXL[i % 2] + (i // 2 + 1) * 16)
                if i >= 2:
                    sOS[i % 2].wait(v, bOS[i % 2] + (i // 2) * 16)
                sOA.inc(v.tensor_tensor(onew[i % 2][:], xres[i % 2][:], ps_tp[i % 2][:], ALU.add))
    done_marks.append((sOS0, sOS0.n))
    done_marks.append((sOS1, sOS1.n))


def moe_cfg(C, variant):
    nc = C.nc
    t = nc.dram_tensor
    E = 8
    if variant == "l0_ffn":
        Wfc_s = t("L0_ffn_Wfc_s", [D, 4 * D], F32, kind="ExternalInput").ap()
        Wproj_s = t("L0_ffn_Wproj_s", [4 * D, D], F32, kind="ExternalInput").ap()
        Wr = t("L0_ffn_Wr", [D, E], F32, kind="ExternalInput").ap()
        Wfc = t("L0_ffn_Wfc", [E, D, 4 * D], F32, kind="ExternalInput").ap()
        Wproj = t("L0_ffn_Wproj", [E, 4 * D, D], F32, kind="ExternalInput").ap()
        bfc_s = t("L0_ffn_bfc_s", [4 * D], F32, kind="ExternalInput").ap()
        bfc = t("L0_ffn_bfc", [E, 4 * D], F32, kind="ExternalInput").ap()
        b2all = t("L0f_b2all", [9, D], F32, kind="ExternalInput").ap()
        H2 = 2 * D
        branches = []
        for half in range(2):
            branches.append(dict(w1=Wfc_s[:, half * H2:(half + 1) * H2],
                                 w2=Wproj_s[half * H2:(half + 1) * H2, :],
                                 b1row=bfc_s[half * H2:(half + 1) * H2],
                                 gidx=None, act="gelu_tanh"))
        for e in range(E):
            for half in range(2):
                branches.append(dict(w1=Wfc[e, :, half * H2:(half + 1) * H2],
                                     w2=Wproj[e, half * H2:(half + 1) * H2, :],
                                     b1row=bfc[e, half * H2:(half + 1) * H2],
                                     gidx=e, act="gelu_tanh"))
        return dict(tag="l0f", router_w=Wr, router_b=None, branches=branches,
                    b2all=b2all, kb=9, l3_denom=False)
    if variant == "l1_ffn":
        Wg = t("L1_ffn_Wg", [D, E], F32, kind="ExternalInput").ap()
        W1 = t("L1_ffn_W1", [E, D, 2 * D], F32, kind="ExternalInput").ap()
        W2 = t("L1_ffn_W2", [E, 2 * D, D], F32, kind="ExternalInput").ap()
        b1r = t("L1f_b1r", [E, 128, 16], F32, kind="ExternalInput").ap()
        b2 = t("L1_ffn_b2", [E, D], F32, kind="ExternalInput").ap()
        branches = [dict(w1=W1[e], w2=W2[e], b1r=b1r[e], gidx=e, act="relu")
                    for e in range(E)]
        return dict(tag="l1f", router_w=Wg, router_b=None, branches=branches,
                    b2all=b2, kb=8, l3_denom=False)
    if variant == "l2_ffn":
        rw = t("L2_ffn_router_w", [D, E], F32, kind="ExternalInput").ap()
        rb = t("L2_ffn_router_b", [E], F32, kind="ExternalInput").ap()
        W1 = t("L2_ffn_W1", [E, D, 512], F32, kind="ExternalInput").ap()
        W2 = t("L2_ffn_W2", [E, 512, D], F32, kind="ExternalInput").ap()
        b1r = t("L2f_b1r", [E, 128, 4], F32, kind="ExternalInput").ap()
        b2 = t("L2_ffn_b2", [E, D], F32, kind="ExternalInput").ap()
        branches = [dict(w1=W1[e], w2=W2[e], b1r=b1r[e], gidx=e, act="gelu")
                    for e in range(E)]
        return dict(tag="l2f", router_w=rw, router_b=rb, branches=branches,
                    b2all=b2, kb=8, l3_denom=False)
    if variant == "l3_ffn":
        rwT = t("L3f_router_wT", [D, E], F32, kind="ExternalInput").ap()
        w13 = t("L3_ffn_w13", [E, D, 4 * D], F32, kind="ExternalInput").ap()
        w2 = t("L3_ffn_w2", [E, 2 * D, D], F32, kind="ExternalInput").ap()
        branches = []
        for e in range(E):
            for half in range(2):
                branches.append(dict(w1=w13[e, :, half * D:(half + 1) * D],
                                     w1u=w13[e, :, 2 * D + half * D:2 * D + (half + 1) * D],
                                     w2=w2[e, half * D:(half + 1) * D, :],
                                     b1r=None, gidx=e, act="glu_silu"))
        return dict(tag="l3f", router_w=rwT, router_b=None, branches=branches,
                    b2all=None, kb=0, l3_denom=True)
    raise ValueError(variant)


def prep_derived(inp):
    d = {}
    r = lambda b: np.ascontiguousarray(b.reshape(b.shape[0] // 128, 128).T)
    d["L0f_b2all"] = np.concatenate([inp["L0_ffn_bproj"],
                                     inp["L0_ffn_bproj_s"][None, :]], 0).astype(np.float32)
    d["L1f_b1r"] = np.stack([r(inp["L1_ffn_b1"][e]) for e in range(8)]).astype(np.float32)
    d["L2f_b1r"] = np.stack([r(inp["L2_ffn_b1"][e]) for e in range(8)]).astype(np.float32)
    d["L3f_router_wT"] = np.ascontiguousarray(inp["L3_ffn_router_w"].T).astype(np.float32)
    return d


def build_moe_program(variant, scale=1):
    C = Core(scale=scale)
    nc = C.nc
    tpc = C.tpc
    x_in = nc.dram_tensor("x", [tpc, D], F32, kind="ExternalInput").ap()
    x_out = nc.dram_tensor("xo", [tpc, D], F32, kind="ExternalOutput").ap()
    cfg = moe_cfg(C, variant)
    with C.es:
        build_consts(C)
        done = moe_ffn(C, x_in, x_out, cfg)
        with nc.Block() as blk:
            @blk.gpsimd
            def _(g):
                for s_, n_ in done:
                    s_.wait(g, n_)
    return C




def l1_attn(C, x_in, x_out, W, y1_dram, kv_src, kv_all, wait_first=None):
    nc = C.nc
    tpc, nt = C.tpc, C.nt
    H, HD = 16, 64
    sCC = C.sem("cc")
    cV, cS = C.sem("cV"), C.sem("cS")
    dW = C.sem("dW")

    with contextlib.ExitStack() as es:
        woutT = es.enter_context(nc.sbuf_tensor("l1_wo", [128, ND, D], BF16))
        lnw_row = es.enter_context(nc.sbuf_tensor("l1_lnw", [1, D], BF16))
        lnb_row = es.enter_context(nc.sbuf_tensor("l1_lnb", [1, D], BF16))
        w_bc = es.enter_context(nc.sbuf_tensor("l1_wbc", [128, D], BF16))
        b_bc = es.enter_context(nc.sbuf_tensor("l1_bbc", [128, D], BF16))
        sel_sb = es.enter_context(nc.sbuf_tensor("l1_sel", [128, 128], F32))
        q_sb = es.enter_context(nc.sbuf_tensor("l1_q", [128, nt, H, HD], BF16))
        es_qkv = contextlib.ExitStack()
        wqkvT = es_qkv.enter_context(nc.sbuf_tensor("l1_wq", [128, ND, 3 * D], BF16))
        bW = dW.n
        with nc.Block() as blk:
            @blk.gpsimd
            def _(g):
                for c in range(ND):
                    g.dma_start(out=wqkvT[:, c, :], in_=W["wqkvT"][c * 128:(c + 1) * 128, :]).then_inc(dW.h, 16)
                    g.dma_start(out=woutT[:, c, :], in_=W["woutT"][c * 128:(c + 1) * 128, :]).then_inc(dW.h, 16)
                g.dma_start(out=lnw_row[:], in_=W["ln_w"][None, :]).then_inc(dW.h, 16)
                g.dma_start(out=lnb_row[:], in_=W["ln_b"][None, :]).then_inc(dW.h, 16)
                g.dma_start(out=sel_sb[:], in_=W["sel"]).then_inc(dW.h, 16)
                dW.n += (2 * ND + 3) * 16
                dW.wait(g)
        wready = dW.n

        with contextlib.ExitStack() as es2:
            ps_b = es2.enter_context(nc.psum_tensor("l1_psb", [128, D], F32))
            blk = es2.enter_context(nc.Block())
            sBM, sBE = C.stage(2)
            bBM, bBE = sBM.n, sBE.n

            @blk.tensor
            def _(t):
                wait_consts(C, t)
                ins = None
                for j in range(D // 512):
                    ins = t.matmul(ps_b[:, j * 512:(j + 1) * 512], C.const["ones_row"][:, :],
                                   lnw_row[:, j * 512:(j + 1) * 512], start=True, stop=True)
                sBM.inc(ins)
                sBE.wait(t, bBE + 1)
                ins = None
                for j in range(D // 512):
                    ins = t.matmul(ps_b[:, j * 512:(j + 1) * 512], C.const["ones_row"][:, :],
                                   lnb_row[:, j * 512:(j + 1) * 512], start=True, stop=True)
                sBM.inc(ins)

            @blk.vector
            def _(v):
                sBM.wait(v, bBM + 1)
                sBE.inc(v.tensor_copy(w_bc[:], ps_b[:]))
                sBM.wait(v, bBM + 2)
                sBE.inc(v.tensor_copy(b_bc[:], ps_b[:]))
        bc_done = (C.stage(2)[1], C.stage(2)[1].n)

        y1_marks = ln_to_xt(C, x_in, None, "l1p", y_dram=y1_dram, wait_first=wait_first)
        xt2 = es_qkv.enter_context(nc.sbuf_tensor("l1_xt2", [128, ND, tpc], BF16))
        xt_sem, xt_n = ln_to_xt(C, y1_dram, xt2, "l1a", affine=(w_bc, b_bc),
                                wait_first=y1_marks + [bc_done])

        with contextlib.ExitStack() as es2:
            ps_qkv = [es2.enter_context(nc.psum_tensor(f"l1_pq{i}", [128, 1536], F32)) for i in range(2)]
            stg = [es2.enter_context(nc.sbuf_tensor(f"l1_st{i}", [128, 2048], BF16)) for i in range(2)]
            blk = es2.enter_context(nc.Block())
            sQM, sQE, sKS0, sKS1 = C.stage(4)
            sKS = [sKS0, sKS1]
            bQM, bQE = sQM.n, sQE.n
            bKS = [sKS0.n, sKS1.n]
            NS = 2 * nt

            @blk.tensor
            def _(t):
                xt_sem.wait(t, xt_n)
                for st in range(NS):
                    i, hf = st // 2, st % 2
                    if st >= 2:
                        sQE.wait(t, bQE + st - 1)
                    pb = ps_qkv[st % 2]
                    ins = None
                    for dc in range(ND):
                        for j in range(3):
                            ins = t.matmul(pb[:, j * 512:(j + 1) * 512],
                                           xt2[:, dc, i * 128:(i + 1) * 128],
                                           wqkvT[:, dc, hf * 1536 + j * 512:hf * 1536 + (j + 1) * 512],
                                           start=(dc == 0), stop=(dc == ND - 1))
                    sQM.inc(ins)

            @blk.scalar
            def _(s):
                for st in range(NS):
                    i, hf = st // 2, st % 2
                    sQM.wait(s, bQM + st + 1)
                    if i >= 2 and hf == 0:
                        sKS[i % 2].wait(s, bKS[i % 2] + (i // 2) * 16)
                    pv = ps_qkv[st % 2][:].rearrange("p (h c) -> p h c", h=8)
                    cS.inc(s.activation(q_sb[:, i, hf * 8:(hf + 1) * 8, :], pv[:, :, 0:HD], AF.Copy))
                    cS.inc(s.activation(stg[i % 2][:, hf * 512:(hf + 1) * 512].rearrange("p (h c) -> p h c", h=8),
                                        pv[:, :, HD:2 * HD], AF.Copy))
                    ins = s.activation(stg[i % 2][:, 1024 + hf * 512:1024 + (hf + 1) * 512].rearrange("p (h c) -> p h c", h=8),
                                       pv[:, :, 2 * HD:3 * HD], AF.Copy)
                    sQE.inc(ins)
                    cS.n += 0

            @blk.sync
            def _(sync):
                kv2d = kv_src.rearrange("(a b) -> a b", b=2 * D)
                for i in range(nt):
                    sQE.wait(sync, bQE + i * 2 + 2)
                    sync.dma_start(out=kv2d[i * 128:(i + 1) * 128, :],
                                   in_=stg[i % 2][:]).then_inc(sKS[i % 2].h, 16)
                    sKS[i % 2].n += 16
                sKS0.wait(sync)
                sKS1.wait(sync)
        kv_written = [(sKS0, sKS0.n), (sKS1, sKS1.n)]
        es_qkv.close()

        with nc.Block() as blk:
            @blk.gpsimd
            def _(g):
                for s_, n_ in kv_written:
                    s_.wait(g, n_)
                sCC.inc(g.collective_compute("AllGather", ALU.bypass, replica_groups=ALL8,
                                             ins=[kv_src.rearrange("(a b) -> a b", b=2 * D)],
                                             outs=[kv_all.rearrange("(a b) -> a b", b=2 * D)]))
        kv_ready = sCC.n

        with contextlib.ExitStack() as es2:
            kv8 = [es2.enter_context(nc.sbuf_tensor(f"l1_kv8{i}", [128, 8, 2 * D], BF16)) for i in range(2)]
            sc = es2.enter_context(nc.sbuf_tensor("l1_sc", [128, H, 8], F32))
            st2 = es2.enter_context(nc.sbuf_tensor("l1_st2", [128, H, 8], F32))
            mx = es2.enter_context(nc.sbuf_tensor("l1_mx", [128, H], F32))
            prod = es2.enter_context(nc.sbuf_tensor("l1_prod", [128, D], BF16))
            oacc = es2.enter_context(nc.sbuf_tensor("l1_oacc", [128, D], F32))
            obf = [es2.enter_context(nc.sbuf_tensor(f"l1_obf{i}", [128, D], BF16)) for i in range(2)]
            xres = [es2.enter_context(nc.sbuf_tensor(f"l1_xr{i}", [128, D], F32)) for i in range(2)]
            y1r = [es2.enter_context(nc.sbuf_tensor(f"l1_y1{i}", [128, D], F32)) for i in range(2)]
            onew = [es2.enter_context(nc.sbuf_tensor(f"l1_on{i}", [128, D], F32)) for i in range(2)]
            ps_tp = [es2.enter_context(nc.psum_tensor(f"l1_ptp{i}", [128, 1024], BF16)) for i in range(2)]
            aoT = [es2.enter_context(nc.sbuf_tensor(f"l1_aoT{i}", [128, 1024], BF16)) for i in range(2)]
            ps_o = [es2.enter_context(nc.psum_tensor(f"l1_po{i}", [128, 1024], F32)) for i in range(2)]
            blk = es2.enter_context(nc.Block())
            (sKL0, sKL1, sEXm, sOB, sTPm, sOM, sOA, sOS0, sOS1, sXL0, sXL1,
             sYL0, sYL1, sEXr, sTPe) = C.stage(15)
            sKL = [sKL0, sKL1]
            sOS = [sOS0, sOS1]
            sXL = [sXL0, sXL1]
            sYL = [sYL0, sYL1]
            bEXm, bOB, bTPm, bOM, bOA, bTPe = sEXm.n, sOB.n, sTPm.n, sOM.n, sOA.n, sTPe.n
            bEXr = sEXr.n
            bKL = [sKL0.n, sKL1.n]
            bOS = [sOS0.n, sOS1.n]
            bXL = [sXL0.n, sXL1.n]
            bYL = [sYL0.n, sYL1.n]

            def _store(sync, i):
                sOA.wait(sync, bOA + i + 1)
                sync.dma_start(out=x_out[i * 128:(i + 1) * 128, :],
                               in_=onew[i % 2][:]).then_inc(sOS[i % 2].h, 16)
                sOS[i % 2].n += 16

            @blk.sync
            def _(sync):
                sCC.wait(sync, kv_ready)
                kva = kv_all.rearrange("(a b) -> a b", b=2 * D)
                for i in range(nt):
                    if i >= 2:
                        sOB.wait(sync, bOB + i - 1)
                    for j in range(8):
                        sync.dma_start(out=kv8[i % 2][:, j, :],
                                       in_=kva[j * tpc + i * 128:j * tpc + (i + 1) * 128, :]).then_inc(sKL[i % 2].h, 16)
                        sKL[i % 2].n += 16
                    if i >= 2:
                        sOA.wait(sync, bOA + i - 1)
                    sync.dma_start(out=xres[i % 2][:],
                                   in_=x_in[i * 128:(i + 1) * 128, :]).then_inc(sXL[i % 2].h, 16)
                    sXL[i % 2].n += 16
                    sync.dma_start(out=y1r[i % 2][:],
                                   in_=y1_dram[i * 128:(i + 1) * 128, :]).then_inc(sYL[i % 2].h, 16)
                    sYL[i % 2].n += 16
                    if i >= 2:
                        _store(sync, i - 2)
                for i in range(max(0, nt - 2), nt):
                    _store(sync, i)
                sOS0.wait(sync)
                sOS1.wait(sync)

            @blk.vector
            def _(v):
                for i in range(nt):
                    sKL[i % 2].wait(v, bKL[i % 2] + (i // 2 + 1) * 128)
                    kvt = kv8[i % 2]
                    for j in range(8):
                        cV.inc(v.tensor_tensor(prod[:].rearrange("p (h e) -> p h e", h=H),
                                               q_sb[:, i, :, :],
                                               kvt[:, j, 0:D].rearrange("p (h e) -> p h e", h=H),
                                               ALU.mult))
                        cV.wait(v)
                        cV.inc(v.reduce_sum(sc[:, :, j:j + 1],
                                            prod[:].rearrange("p (h e) -> p h e", h=H), axis=AX.X))
                        cV.wait(v)
                    cV.inc(v.reduce_max(mx[:, :].rearrange("p (h o) -> p h o", o=1), sc[:, :, :],
                                        axis=AX.X, negate=True))
                    cV.wait(v)
                    cV.inc(v.tensor_tensor(st2[:], sc[:],
                                           mx[:].rearrange("p (h o) -> p h o", o=1).broadcast_to([128, H, 8]),
                                           ALU.add))
                    cV.wait(v)
                    v.engine_nop().then_inc(sEXr.h, 1)
                    sEXr.n += 1
                    sEXm.wait(v, bEXm + i + 1)
                    cV.inc(v.tensor_tensor(st2[:], st2[:], sel_sb[:].rearrange("p (h e) -> p h e", h=H), ALU.mult))
                    cV.wait(v)
                    cV.inc(v.reduce_sum(mx[:, :].rearrange("p (h o) -> p h o", o=1), st2[:], axis=AX.X))
                    cV.wait(v)
                    cV.inc(v.reciprocal(mx[:], mx[:]))
                    cV.wait(v)
                    cV.inc(v.tensor_tensor(st2[:], st2[:],
                                           mx[:].rearrange("p (h o) -> p h o", o=1).broadcast_to([128, H, 8]),
                                           ALU.mult))
                    cV.wait(v)
                    for j in range(8):
                        wj = st2[:, :, j:j + 1].broadcast_to([128, H, HD])
                        if j == 0:
                            cV.inc(v.tensor_tensor(oacc[:].rearrange("p (h e) -> p h e", h=H),
                                                   kvt[:, j, D:2 * D].rearrange("p (h e) -> p h e", h=H),
                                                   wj, ALU.mult))
                            cV.wait(v)
                        else:
                            cV.inc(v.tensor_tensor(prod[:].rearrange("p (h e) -> p h e", h=H),
                                                   kvt[:, j, D:2 * D].rearrange("p (h e) -> p h e", h=H),
                                                   wj, ALU.mult))
                            cV.wait(v)
                            cV.inc(v.tensor_tensor(oacc[:], oacc[:], prod[:], ALU.add))
                            cV.wait(v)
                    if i >= 2:
                        sTPm.wait(v, bTPm + i - 1)
                    cV.inc(v.tensor_copy(obf[i % 2][:], oacc[:]))
                    cV.wait(v)
                    v.engine_nop().then_inc(sOB.h, 1)
                    sOB.n += 1
                    sOM.wait(v, bOM + i + 1)
                    sXL[i % 2].wait(v, bXL[i % 2] + (i // 2 + 1) * 16)
                    sYL[i % 2].wait(v, bYL[i % 2] + (i // 2 + 1) * 16)
                    if i >= 2:
                        sOS[i % 2].wait(v, bOS[i % 2] + (i // 2) * 16)
                    cV.inc(v.tensor_tensor(onew[i % 2][:], xres[i % 2][:], y1r[i % 2][:], ALU.add))
                    cV.wait(v)
                    sOA.inc(v.tensor_tensor(onew[i % 2][:], onew[i % 2][:], ps_o[i % 2][:], ALU.add))

            @blk.scalar
            def _(s):
                for i in range(nt):
                    sEXr.wait(s, bEXr + i + 1)
                    ins = s.activation(st2[:], st2[:], AF.Exp, scale=1.0 / 32.0)
                    ins.then_inc(sEXm.h, 1)
                    sEXm.n += 1
                    sTPm.wait(s, bTPm + i + 1)
                    if i >= 2:
                        sOM.wait(s, bOM + i - 1)
                    ins = s.activation(aoT[i % 2][:], ps_tp[i % 2][:], AF.Copy)
                    ins.then_inc(sTPe.h, 1)
                    sTPe.n += 1

            @blk.tensor
            def _(t):
                for i in range(nt):
                    sOB.wait(t, bOB + i + 1)
                    if i >= 2:
                        sTPe.wait(t, bTPe + i - 1)
                    ins = None
                    for c in range(ND):
                        ins = t.transpose(ps_tp[i % 2][:, c * 128:(c + 1) * 128],
                                          obf[i % 2][:, c * 128:(c + 1) * 128], C.const["ident"][:])
                    sTPm.inc(ins)
                    sTPe.wait(t, bTPe + i + 1)
                    if i >= 2:
                        sOA.wait(t, bOA + i - 1)
                    ins = None
                    for dc in range(ND):
                        for j in range(D // 512):
                            ins = t.matmul(ps_o[i % 2][:, j * 512:(j + 1) * 512],
                                           aoT[i % 2][:, dc * 128:(dc + 1) * 128],
                                           woutT[:, dc, j * 512:(j + 1) * 512],
                                           start=(dc == 0), stop=(dc == ND - 1))
                    sOM.inc(ins)
        return [(sOS0, sOS0.n), (sOS1, sOS1.n)]


def build_l1_attn_program(scale=1):
    C = Core(scale=scale)
    nc = C.nc
    tpc = C.tpc
    x_in = nc.dram_tensor("x", [tpc, D], F32, kind="ExternalInput").ap()
    x_out = nc.dram_tensor("xo", [tpc, D], F32, kind="ExternalOutput").ap()
    W = {
        "wqkvT": nc.dram_tensor("L1a_wqkvT", [D, 3 * D], F32, kind="ExternalInput").ap(),
        "woutT": nc.dram_tensor("L1a_woutT", [D, D], F32, kind="ExternalInput").ap(),
        "ln_w": nc.dram_tensor("L1_attn_ln_w", [D], F32, kind="ExternalInput").ap(),
        "ln_b": nc.dram_tensor("L1_attn_ln_b", [D], F32, kind="ExternalInput").ap(),
        "sel": nc.dram_tensor("L1a_sel", [128, 128], F32, kind="ExternalInput").ap(),
    }
    y1 = nc.dram_tensor("l1_y1d", [tpc, D], F32).ap()
    kv_src = nc.dram_tensor("l1_kv_src", [tpc * 2 * D], BF16).ap()
    kv_all = nc.dram_tensor("l1_kv_all", [8 * tpc * 2 * D], BF16).ap()
    with C.es:
        build_consts(C)
        done = l1_attn(C, x_in, x_out, W, y1, kv_src, kv_all)
        with nc.Block() as blk:
            @blk.gpsimd
            def _(g):
                for s_, n_ in done:
                    s_.wait(g, n_)
    return C




def l2_attn(C, x_in, x_out, W, y1_dram, xz_src, xz_all, xconv_d, y3_d, op_d, rs_d, wait_first=None):
    nc = C.nc
    tpc, tf, nt = C.tpc, C.tf, C.nt
    DI = 2 * D
    NCC = DI // 128
    TH2 = tf // 2
    TQ4 = tf // 4
    NW4 = max(1, TQ4 // 512)
    JW4 = min(512, TQ4)
    NW2 = max(1, TH2 // 512)
    JW2 = min(512, TH2)
    sCC = C.sem("cc")
    cV, cS, cG = C.sem("cV"), C.sem("cS"), C.sem("cG")
    dW = C.sem("dW")
    dG2 = C.sem("dG2")

    with contextlib.ExitStack() as es:
        convw = es.enter_context(nc.sbuf_tensor("l2_cw", [128, 16, 4], F32))
        convb = es.enter_context(nc.sbuf_tensor("l2_cb", [128, 16], F32))
        alog = es.enter_context(nc.sbuf_tensor("l2_al", [128, 16, 8], F32))
        A_sb = es.enter_context(nc.sbuf_tensor("l2_A", [128, 16, 8], F32))
        wxb = es.enter_context(nc.sbuf_tensor("l2_wxb", [128, 16, 8], BF16))
        wxc = es.enter_context(nc.sbuf_tensor("l2_wxc", [128, 16, 8], BF16))
        wxd = es.enter_context(nc.sbuf_tensor("l2_wxd", [128, 16, 1], BF16))
        dr = es.enter_context(nc.sbuf_tensor("l2_dr", [128, 16], F32))
        selc = es.enter_context(nc.sbuf_tensor("l2_sel", [128, 1], F32))
        dsel = es.enter_context(nc.sbuf_tensor("l2_dsel", [128, 16], F32))
        g_row = es.enter_context(nc.sbuf_tensor("l2_gr", [1, D], BF16))
        b_row = es.enter_context(nc.sbuf_tensor("l2_br2", [1, D], BF16))
        w_bc = es.enter_context(nc.sbuf_tensor("l2_wbc", [128, D], BF16))
        b_bc = es.enter_context(nc.sbuf_tensor("l2_bbc", [128, D], BF16))
        with nc.Block() as blk:
            @blk.gpsimd
            def _(g):
                for name, dst in [("convw_r", convw[:, :, :]), ("convb_r", convb[:, :]),
                                  ("Alog_my", alog[:, :, :]), ("wxb", wxb[:, :, :]),
                                  ("wxc", wxc[:, :, :]), ("wxd", wxd[:, :, :]),
                                  ("Dr", dr[:, :]), ("sel", selc[:, :])]:
                    g.dma_start(out=dst, in_=W[name]).then_inc(dW.h, 16)
                    dW.n += 16
                g.dma_start(out=g_row[:], in_=W["gamma"][None, :]).then_inc(dW.h, 16)
                g.dma_start(out=b_row[:], in_=W["beta"][None, :]).then_inc(dW.h, 16)
                dW.n += 32
                dW.wait(g)
                cG.inc(g.tensor_scalar(dsel[:, :], dr[:, :], selc[:, 0:1], None, ALU.mult))
        cready = (dW.n, cG.n)

        with contextlib.ExitStack() as es2:
            ps_b = es2.enter_context(nc.psum_tensor("l2_psb", [128, D], F32))
            blk = es2.enter_context(nc.Block())
            sBM, sBE = C.stage(2)
            bBM, bBE = sBM.n, sBE.n

            @blk.scalar
            def _(s):
                cG.wait(s, cready[1])
                cS.inc(s.activation(A_sb[:, :, :], alog[:, :, :], AF.Exp))

            @blk.vector
            def _(v):
                cS.wait(v)
                cV.inc(v.tensor_scalar(A_sb[:, :, :], A_sb[:, :, :], -1.0, None, ALU.mult))
                sBM.wait(v, bBM + 1)
                sBE.inc(v.tensor_copy(w_bc[:], ps_b[:]))
                sBM.wait(v, bBM + 2)
                sBE.inc(v.tensor_copy(b_bc[:], ps_b[:]))

            @blk.tensor
            def _(t):
                wait_consts(C, t)
                dW.wait(t, cready[0])
                ins = None
                for j in range(D // 512):
                    ins = t.matmul(ps_b[:, j * 512:(j + 1) * 512], C.const["ones_row"][:, :],
                                   g_row[:, j * 512:(j + 1) * 512], start=True, stop=True)
                sBM.inc(ins)
                sBE.wait(t, bBE + 1)
                ins = None
                for j in range(D // 512):
                    ins = t.matmul(ps_b[:, j * 512:(j + 1) * 512], C.const["ones_row"][:, :],
                                   b_row[:, j * 512:(j + 1) * 512], start=True, stop=True)
                sBM.inc(ins)
            aff_marks = [(sBE, bBE + 2), (cV, cV.n)]

        y1_marks = ln_to_xt(C, x_in, None, "l2p", y_dram=y1_dram, wait_first=wait_first)
        es_w = contextlib.ExitStack()
        win = es_w.enter_context(nc.sbuf_tensor("l2_win", [128, ND, 4 * D], BF16))
        xt2 = es_w.enter_context(nc.sbuf_tensor("l2_xt2", [128, ND, tpc], BF16))
        xt_sem, xt_n = ln_to_xt(C, y1_dram, xt2, "l2a", affine=(w_bc, b_bc),
                                wait_first=y1_marks + aff_marks)

        bWi = dW.n
        with nc.Block() as blk:
            @blk.gpsimd
            def _(g):
                for dc in range(ND):
                    g.dma_start(out=win[:, dc, :], in_=W["W_in"][dc * 128:(dc + 1) * 128, :]).then_inc(dW.h, 16)
                    dW.n += 16
        win_ready = dW.n

        NWt = max(1, tpc // 512)
        JWt = min(512, tpc)
        with contextlib.ExitStack() as es2:
            ps_xz = [es2.enter_context(nc.psum_tensor(f"l2_pxz{i}", [128, tpc], F32)) for i in range(2)]
            stg = [es2.enter_context(nc.sbuf_tensor(f"l2_stg{i}", [128, tpc], BF16)) for i in range(2)]
            blk = es2.enter_context(nc.Block())
            sMM, sEV, sST0, sST1 = C.stage(4)
            sST = [sST0, sST1]
            bMM, bEV = sMM.n, sEV.n
            bST = [sST0.n, sST1.n]
            xz2d = xz_src.rearrange("(a b) -> a b", b=tpc)

            @blk.tensor
            def _(t):
                dW.wait(t, win_ready)
                xt_sem.wait(t, xt_n)
                for cc in range(4 * ND):
                    if cc >= 2:
                        sEV.wait(t, bEV + cc - 1)
                    pb = ps_xz[cc % 2]
                    ins = None
                    for dc in range(ND):
                        for j in range(NWt):
                            ins = t.matmul(pb[:, j * JWt:(j + 1) * JWt],
                                           win[:, dc, cc * 128:(cc + 1) * 128],
                                           xt2[:, dc, j * JWt:(j + 1) * JWt],
                                           start=(dc == 0), stop=(dc == ND - 1))
                    sMM.inc(ins)

            @blk.scalar
            def _(s):
                for cc in range(4 * ND):
                    sMM.wait(s, bMM + cc + 1)
                    if cc >= 2:
                        sST[cc % 2].wait(s, bST[cc % 2] + (cc // 2) * 16)
                    sEV.inc(s.activation(stg[cc % 2][:], ps_xz[cc % 2][:], AF.Copy))

            @blk.sync
            def _(sync):
                for cc in range(4 * ND):
                    sEV.wait(sync, bEV + cc + 1)
                    sync.dma_start(out=xz2d[cc * 128:(cc + 1) * 128, :],
                                   in_=stg[cc % 2][:]).then_inc(sST[cc % 2].h, 16)
                    sST[cc % 2].n += 16
                sST0.wait(sync)
                sST1.wait(sync)
        es_w.close()

        with nc.Block() as blk:
            @blk.gpsimd
            def _(g):
                sCC.inc(g.collective_compute("AllGather", ALU.bypass, replica_groups=PAIRS,
                                             ins=[xz_src.rearrange("(a b) -> a b", b=tpc)],
                                             outs=[xz_all.rearrange("(a b) -> a b", b=tpc)]))
        xz_ready = sCC.n

        es_scan = contextlib.ExitStack()
        bcB = es_scan.enter_context(nc.sbuf_tensor("l2_bcB", [128, 8, tf], BF16))
        bcC = es_scan.enter_context(nc.sbuf_tensor("l2_bcC", [128, 8, tf], BF16))
        dt_bc = es_scan.enter_context(nc.sbuf_tensor("l2_dtbc", [128, tf], F32))
        rows_d = nc.dram_tensor(f"l2_rows", [18 * tf], F32).ap().rearrange("(a b) -> a b", b=tf)
        xza = xz_all.rearrange("(a b) -> a b", b=tpc)
        xcd = xconv_d.rearrange("(a b) -> a b", b=tf)
        with contextlib.ExitStack() as es2:
            xcload = [es2.enter_context(nc.sbuf_tensor(f"l2_xcl{i}", [128, TQ4 + 3], BF16)) for i in range(2)]
            rstgB = es2.enter_context(nc.sbuf_tensor("l2_rsb", [8, TQ4], F32))
            rstgC = es2.enter_context(nc.sbuf_tensor("l2_rsc", [8, TQ4], F32))
            rstgD = es2.enter_context(nc.sbuf_tensor("l2_rsd2", [1, TQ4], F32))
            cacc = [es2.enter_context(nc.sbuf_tensor(f"l2_ca{i}", [128, TQ4], F32)) for i in range(2)]
            xcv = [es2.enter_context(nc.sbuf_tensor(f"l2_xcv{i}", [128, TQ4], BF16)) for i in range(2)]
            ps_B = es2.enter_context(nc.psum_tensor("l2_pB", [8, TQ4], F32))
            ps_Cp = es2.enter_context(nc.psum_tensor("l2_pC", [8, TQ4], F32))
            ps_d = es2.enter_context(nc.psum_tensor("l2_pd", [1, TQ4], F32))
            blk = es2.enter_context(nc.Block())
            sXL0, sXL1, sCVb, sSI, sXS0, sXS1, sXPM, sXPE, sXPS = C.stage(9)
            sXL = [sXL0, sXL1]
            sXS = [sXS0, sXS1]
            bCVb, bSI, bXPM, bXPE, bXPS = sCVb.n, sSI.n, sXPM.n, sXPE.n, sXPS.n
            bXL = [sXL0.n, sXL1.n]
            bXS = [sXS0.n, sXS1.n]
            nloads = []
            for tq in range(4):
                for cc in range(NCC):
                    g0 = tq * TQ4 - 3
                    crosses = (g0 < tpc) and ((tq + 1) * TQ4 > tpc)
                    nloads.append(2 if crosses else 1)
            cum = [[0], [0]]
            for it, nl in enumerate(nloads):
                p = it % 2
                cum[p].append(cum[p][-1] + nl * 16)
                cum[1 - p].append(cum[1 - p][-1])

            @blk.gpsimd
            def _(g):
                cG.inc(g.memset(xcload[0][:, 0:3], 0.0))
                cG.inc(g.memset(xcload[1][:, 0:3], 0.0))
            pad_mark = cG.n

            @blk.sync
            def _(sync):
                sCC.wait(sync, xz_ready)
                cG.wait(sync, pad_mark)
                def _xp_store(sync, tq):
                    sXPE.wait(sync, bXPE + tq + 1)
                    sync.dma_start(out=rows_d[0:8, tq * TQ4:(tq + 1) * TQ4],
                                   in_=rstgB[:, :]).then_inc(sXPS.h, 16)
                    sync.dma_start(out=rows_d[8:16, tq * TQ4:(tq + 1) * TQ4],
                                   in_=rstgC[:, :]).then_inc(sXPS.h, 16)
                    sync.dma_start(out=rows_d[16:17, tq * TQ4:(tq + 1) * TQ4],
                                   in_=rstgD[:, :]).then_inc(sXPS.h, 16)
                    sXPS.n += 48

                for tq in range(4):
                    if tq >= 1:
                        _xp_store(sync, tq - 1)
                    for cc in range(NCC):
                        it = tq * NCC + cc
                        if it >= 2:
                            sCVb.wait(sync, bCVb + it - 1)
                        g0 = tq * TQ4 - 3
                        gs = max(g0, 0)
                        off = gs - g0
                        for r in range(2):
                            lo = max(gs, r * tpc)
                            hi = min((tq + 1) * TQ4, (r + 1) * tpc)
                            if lo < hi:
                                src = xza[r * 4 * D + cc * 128:r * 4 * D + (cc + 1) * 128,
                                          lo - r * tpc:hi - r * tpc]
                                sync.dma_start(out=xcload[it % 2][:, off + (lo - gs):off + (hi - gs)],
                                               in_=src).then_inc(sXL[it % 2].h, 16)
                                sXL[it % 2].n += 16
                        if it >= 2:
                            it2 = it - 2
                            tq2, cc2 = it2 // NCC, it2 % NCC
                            sSI.wait(sync, bSI + it2 + 1)
                            sync.dma_start(out=xcd[cc2 * 128:(cc2 + 1) * 128, tq2 * TQ4:(tq2 + 1) * TQ4],
                                           in_=xcv[it2 % 2][:]).then_inc(sXS[it2 % 2].h, 16)
                            sXS[it2 % 2].n += 16
                for it2 in range(4 * NCC - 2, 4 * NCC):
                    tq2, cc2 = it2 // NCC, it2 % NCC
                    sSI.wait(sync, bSI + it2 + 1)
                    sync.dma_start(out=xcd[cc2 * 128:(cc2 + 1) * 128, tq2 * TQ4:(tq2 + 1) * TQ4],
                                   in_=xcv[it2 % 2][:]).then_inc(sXS[it2 % 2].h, 16)
                    sXS[it2 % 2].n += 16
                _xp_store(sync, 3)
                sXPS.wait(sync)
                sXS0.wait(sync)
                sXS1.wait(sync)

            @blk.vector
            def _(v):
                for tq in range(4):
                    for cc in range(NCC):
                        it = tq * NCC + cc
                        p = it % 2
                        sXL[p].wait(v, bXL[p] + cum[p][it + 1])
                        xl = xcload[p]
                        ca = cacc[p]
                        if it >= 2:
                            sSI.wait(v, bSI + it - 1)
                        cV.inc(v.tensor_scalar(ca[:], xl[:, 3:3 + TQ4], convw[:, cc, 3:4], None, ALU.mult))
                        cV.wait(v)
                        for tap in range(3):
                            cV.inc(v.scalar_tensor_tensor(ca[:], xl[:, tap:tap + TQ4],
                                                          convw[:, cc, tap:tap + 1], ca[:],
                                                          ALU.mult, ALU.add))
                            cV.wait(v)
                        v.engine_nop().then_inc(sCVb.h, 1)
                        sCVb.n += 1
                    sXPM.wait(v, bXPM + (tq + 1) * NCC)
                    if tq >= 1:
                        sXPS.wait(v, bXPS + tq * 3 * 16)
                    cV.inc(v.tensor_copy(rstgB[:, :], ps_B[:]))
                    cV.wait(v)
                    cV.inc(v.tensor_copy(rstgC[:, :], ps_Cp[:]))
                    cV.wait(v)
                    cV.inc(v.tensor_copy(rstgD[:, :], ps_d[:]))
                    cV.wait(v)
                    v.engine_nop().then_inc(sXPE.h, 1)
                    sXPE.n += 1

            @blk.scalar
            def _(s):
                for tq in range(4):
                    for cc in range(NCC):
                        it = tq * NCC + cc
                        sCVb.wait(s, bCVb + it + 1)
                        if it >= 2:
                            sXS[it % 2].wait(s, bXS[it % 2] + (it // 2) * 16)
                            sXPM.wait(s, bXPM + it - 1)
                        ins = s.activation(xcv[it % 2][:], cacc[it % 2][:], AF.Silu,
                                           bias=convb[:, cc:cc + 1])
                        ins.then_inc(sSI.h, 1)
                        sSI.n += 1

            @blk.tensor
            def _(t):
                for tq in range(4):
                    if tq >= 1:
                        sXPE.wait(t, bXPE + tq)
                    for cc in range(NCC):
                        it = tq * NCC + cc
                        sSI.wait(t, bSI + it + 1)
                        last = (cc == NCC - 1)
                        ins = None
                        for j in range(NW4):
                            t.matmul(ps_B[:, j * JW4:(j + 1) * JW4], wxb[:, cc, :],
                                     xcv[it % 2][:, j * JW4:(j + 1) * JW4],
                                     start=(cc == 0), stop=last)
                            t.matmul(ps_Cp[:, j * JW4:(j + 1) * JW4], wxc[:, cc, :],
                                     xcv[it % 2][:, j * JW4:(j + 1) * JW4],
                                     start=(cc == 0), stop=last)
                            ins = t.matmul(ps_d[:, j * JW4:(j + 1) * JW4], wxd[:, cc, :],
                                           xcv[it % 2][:, j * JW4:(j + 1) * JW4],
                                           start=(cc == 0), stop=last)
                        sXPM.inc(ins)
        conv_done = (C.stage(9)[8], C.stage(9)[8].n)

        with contextlib.ExitStack() as es2:
            dtr = es2.enter_context(nc.sbuf_tensor("l2_dtr", [1, tf], F32))
            dtr2 = es2.enter_context(nc.sbuf_tensor("l2_dtr2", [1, tf], F32))
            ps_bc = [es2.enter_context(nc.psum_tensor(f"l2_pbc{i}", [128, TH2], F32)) for i in range(2)]
            blk = es2.enter_context(nc.Block())
            sDL, sSP, sBM2, sBE2, sDS = C.stage(5)
            bDL, bSP, bBM2, bBE2, bDS = sDL.n, sSP.n, sBM2.n, sBE2.n, sDS.n

            @blk.sync
            def _(sync):
                conv_done[0].wait(sync, conv_done[1])
                sync.dma_start(out=dtr[:], in_=rows_d[16:17, :]).then_inc(sDL.h, 16)
                sDL.n += 16
                sSP.wait(sync, bSP + 3)
                sync.dma_start(out=rows_d[17:18, :], in_=dtr[:]).then_inc(sDS.h, 16)
                sDS.n += 16
                sDS.wait(sync)

            @blk.scalar
            def _(s):
                sSP.wait(s, bSP + 1)
                cS.inc(s.activation(dtr[:], dtr[:], AF.Abs))
                cS.wait(s)
                cS.inc(s.activation(dtr[:], dtr[:], AF.Exp, scale=-1.0))
                cS.wait(s)
                ins = s.activation(dtr[:], dtr[:], AF.Ln, bias=1.0)
                ins.then_inc(sSP.h, 1)
                sSP.n += 1

            @blk.vector
            def _(v):
                sDL.wait(v, bDL + 16)
                cV.inc(v.tensor_scalar(dtr2[:], dtr[:], 0.0, None, ALU.max))
                cV.wait(v)
                v.engine_nop().then_inc(sSP.h, 1)
                sSP.n += 1
                sSP.wait(v, bSP + 2)
                cV.inc(v.tensor_tensor(dtr[:], dtr[:], dtr2[:], ALU.add))
                cV.wait(v)
                v.engine_nop().then_inc(sSP.h, 1)
                sSP.n += 1
                for k in range(2):
                    sBM2.wait(v, bBM2 + k + 1)
                    cV.inc(v.tensor_copy(dt_bc[:, k * TH2:(k + 1) * TH2], ps_bc[k % 2][:]))
                    cV.wait(v)
                    v.engine_nop().then_inc(sBE2.h, 1)
                    sBE2.n += 1

            @blk.tensor
            def _(t):
                sSP.wait(t, bSP + 3)
                for k in range(2):
                    ins = None
                    for j in range(NW2):
                        ins = t.matmul(ps_bc[k % 2][:, j * JW2:(j + 1) * JW2],
                                       C.const["ones_row_f"][:, :],
                                       dtr[:, k * TH2 + j * JW2:k * TH2 + (j + 1) * JW2],
                                       start=True, stop=True)
                    sBM2.inc(ins)
        dt_done = (C.stage(5)[3], C.stage(5)[3].n)

        with contextlib.ExitStack() as es2:
            ps_bc = [es2.enter_context(nc.psum_tensor(f"l2_pbc2{i}", [128, TH2], F32)) for i in range(2)]
            rrow2 = [es2.enter_context(nc.sbuf_tensor(f"l2_rr2{i}", [1, tf], F32)) for i in range(2)]
            dtr_sb = es2.enter_context(nc.sbuf_tensor("l2_dts", [1, tf], F32))
            blk = es2.enter_context(nc.Block())
            sBM3, sBE3, sRM = C.stage(3)
            sRP = [C.sem("dR0"), C.sem("dR1")]
            bBM3, bBE3, bRM = sBM3.n, sBE3.n, sRM.n
            bRP = [sRP[0].n, sRP[1].n]
            sDTL = C.sem("dG2")

            @blk.gpsimd
            def _(g):
                dt_done[0].wait(g, dt_done[1])
                g.dma_start(out=dtr_sb[:], in_=rows_d[17:18, :]).then_inc(sDTL.h, 16)
                sDTL.n += 16
                sDTL.wait(g)
                cG.inc(g.engine_nop())
                for k in range(16):
                    p = k % 2
                    if k >= 2:
                        sBM3.wait(g, bBM3 + (k - 2) * 2 + 2)
                    src_row = rows_d[k:k + 1, :] if k < 8 else rows_d[k:k + 1, :]
                    g.dma_start(out=rrow2[p][:], in_=src_row).then_inc(sRP[p].h, 16)
                    sRP[p].n += 16

            @blk.vector
            def _(v):
                def _evac(idx):
                    sBM3.wait(v, bBM3 + idx + 1)
                    k2, half = idx // 2, idx % 2
                    dstt = bcB if k2 < 8 else bcC
                    nn = k2 if k2 < 8 else k2 - 8
                    cV.inc(v.tensor_copy(dstt[:, nn, half * TH2:(half + 1) * TH2],
                                         ps_bc[idx % 2][:]))
                    cV.wait(v)
                    v.engine_nop().then_inc(sBE3.h, 1)
                    sBE3.n += 1

                for k in range(16):
                    if k < 8:
                        p = k % 2
                        sRP[p].wait(v, bRP[p] + (k // 2 + 1) * 16)
                        cG.wait(v, cG.n)
                        cV.inc(v.tensor_tensor(rrow2[p][:], rrow2[p][:], dtr_sb[:], ALU.mult))
                        cV.wait(v)
                        v.engine_nop().then_inc(sRM.h, 1)
                        sRM.n += 1
                    if k >= 1:
                        _evac(2 * (k - 1))
                        _evac(2 * (k - 1) + 1)
                _evac(30)
                _evac(31)

            @blk.tensor
            def _(t):
                for k in range(16):
                    p = k % 2
                    if k < 8:
                        sRM.wait(t, bRM + k + 1)
                    else:
                        sRP[p].wait(t, bRP[p] + (k // 2 + 1) * 16)
                    for half in range(2):
                        idx = k * 2 + half
                        if idx >= 2:
                            sBE3.wait(t, bBE3 + idx - 1)
                        ins = None
                        for j in range(NW2):
                            ins = t.matmul(ps_bc[idx % 2][:, j * JW2:(j + 1) * JW2],
                                           C.const["ones_row_f"][:, :],
                                           rrow2[p][:, half * TH2 + j * JW2:half * TH2 + (j + 1) * JW2],
                                           start=True, stop=True)
                        sBM3.inc(ins)
        bc_done = (C.stage(3)[1], C.stage(3)[1].n)

        yacc = es_scan.enter_context(nc.sbuf_tensor("l2_ya", [128, TH2], F32))
        with contextlib.ExitStack() as es2:
            xcvc = [es2.enter_context(nc.sbuf_tensor(f"l2_sxc{i}", [128, tf], BF16)) for i in range(1)]
            dA = es2.enter_context(nc.sbuf_tensor("l2_dA", [128, TH2], F32))
            u_b = es2.enter_context(nc.sbuf_tensor("l2_u", [128, TH2], BF16))
            h_b = es2.enter_context(nc.sbuf_tensor("l2_h", [128, TH2], BF16))
            hC = es2.enter_context(nc.sbuf_tensor("l2_hC", [128, TH2], BF16))
            hlast = es2.enter_context(nc.sbuf_tensor("l2_hl", [128, 8], F32))
            zt = es2.enter_context(nc.sbuf_tensor("l2_zt", [128, TH2], BF16))
            szt = es2.enter_context(nc.sbuf_tensor("l2_szt", [128, TH2], BF16))
            y3b = [es2.enter_context(nc.sbuf_tensor(f"l2_y3{i}", [128, TH2], BF16)) for i in range(1)]
            blk = es2.enter_context(nc.Block())
            (sXC0, sXC1, sDA, sU, sH, sHC, sYA2, sZL0, sZL1, sSZ, sY3, sY3S0, sY3S1) = C.stage(13)
            sXC = [sXC0, sXC1]
            sZL = [sZL0, sZL1]
            sY3S = [sY3S0, sY3S1]
            (bDA, bU, bH, bHC, bYA2, bSZ, bY3) = (s.n for s in (sDA, sU, sH, sHC, sYA2, sSZ, sY3))
            bXC = [sXC0.n, sXC1.n]
            bZL = [sZL0.n, sZL1.n]
            bY3S = [sY3S0.n, sY3S1.n]
            y3d2 = y3_d.rearrange("(a b) -> a b", b=tf)

            @blk.sync
            def _(sync):
                bc_done[0].wait(sync, bc_done[1])
                for dc in range(NCC):
                    if dc >= 1:
                        sY3.wait(sync, bY3 + (dc - 1) * 2 + 2)
                    sync.dma_start(out=xcvc[0][:, :],
                                   in_=xcd[dc * 128:(dc + 1) * 128, :]).then_inc(sXC[dc % 2].h, 16)
                    sXC[dc % 2].n += 16
                    for th in range(2):
                        idx = dc * 2 + th
                        if idx >= 2:
                            sSZ.wait(sync, bSZ + idx - 1)
                        src = xza[th * 4 * D + 2 * D + dc * 128:th * 4 * D + 2 * D + (dc + 1) * 128, :]
                        sync.dma_start(out=zt[:, :] if False else zt[:],
                                       in_=src).then_inc(sZL[idx % 2].h, 16)
                        sZL[idx % 2].n += 16
                        sY3.wait(sync, bY3 + idx + 1)
                        sync.dma_start(out=y3d2[dc * 128:(dc + 1) * 128, th * TH2:(th + 1) * TH2],
                                       in_=y3b[0][:]).then_inc(sY3S[idx % 2].h, 16)
                        sY3S[idx % 2].n += 16
                sY3S0.wait(sync)
                sY3S1.wait(sync)

            @blk.scalar
            def _(s):
                for dc in range(NCC):
                    for th in range(2):
                        idx = dc * 2 + th
                        for n in range(8):
                            k = idx * 8 + n
                            if k >= 1:
                                sU.wait(s, bU + k)
                                sH.wait(s, bH + k)
                            ins = s.activation(dA[:], dt_bc[:, th * TH2:(th + 1) * TH2], AF.Exp,
                                               scale=A_sb[:, dc, n:n + 1])
                            ins.then_inc(sDA.h, 1)
                            sDA.n += 1
                        sZL[idx % 2].wait(s, bZL[idx % 2] + (idx // 2 + 1) * 16)
                        sY3.wait(s, bY3 + idx) if idx >= 1 else None
                        ins = s.activation(szt[:], zt[:], AF.Silu)
                        ins.then_inc(sSZ.h, 1)
                        sSZ.n += 1

            @blk.vector
            def _(v):
                for dc in range(NCC):
                    p = dc % 2
                    sXC[p].wait(v, bXC[p] + (dc // 2 + 1) * 16)
                    for th in range(2):
                        idx = dc * 2 + th
                        for n in range(8):
                            k = idx * 8 + n
                            sDA.wait(v, bDA + k + 1)
                            cV.inc(v.tensor_tensor(u_b[:], xcvc[0][:, th * TH2:(th + 1) * TH2],
                                                   bcB[:, n, th * TH2:(th + 1) * TH2], ALU.mult))
                            cV.wait(v)
                            v.engine_nop().then_inc(sU.h, 1)
                            sU.n += 1
                            init = 0.0 if th == 0 else hlast[:, n:n + 1]
                            cV.inc(v.tensor_tensor_scan(h_b[:], dA[:], u_b[:], init,
                                                        ALU.mult, ALU.add))
                            cV.wait(v)
                            if th == 0:
                                cV.inc(v.tensor_copy(hlast[:, n:n + 1], h_b[:, TH2 - 1:TH2]))
                                cV.wait(v)
                            v.engine_nop().then_inc(sH.h, 1)
                            sH.n += 1
                            cV.inc(v.tensor_tensor(hC[:], h_b[:],
                                                   bcC[:, n, th * TH2:(th + 1) * TH2], ALU.mult))
                            cV.wait(v)
                            if n == 0:
                                cV.inc(v.tensor_copy(yacc[:], hC[:]))
                            else:
                                cV.inc(v.tensor_tensor(yacc[:], yacc[:], hC[:], ALU.add))
                            cV.wait(v)
                        cV.inc(v.scalar_tensor_tensor(yacc[:], xcvc[0][:, th * TH2:(th + 1) * TH2],
                                                      dsel[:, dc:dc + 1], yacc[:],
                                                      ALU.mult, ALU.add))
                        cV.wait(v)
                        sSZ.wait(v, bSZ + idx + 1)
                        if idx >= 1:
                            p1 = (idx - 1) % 2
                            sY3S[p1].wait(v, bY3S[p1] + ((idx - 1) // 2 + 1) * 16)
                        cV.inc(v.tensor_tensor(y3b[0][:], yacc[:], szt[:], ALU.mult))
                        cV.wait(v)
                        v.engine_nop().then_inc(sY3.h, 1)
                        sY3.n += 1
        scan_done = (C.stage(13)[12], C.stage(13)[12].n)
        es_scan.close()

        wout = es.enter_context(nc.sbuf_tensor("l2_wo", [128, NCC, D], BF16))
        bWo = dW.n
        with nc.Block() as blk:
            @blk.gpsimd
            def _(g):
                for hc in range(NCC):
                    g.dma_start(out=wout[:, hc, :], in_=W["W_out"][hc * 128:(hc + 1) * 128, :]).then_inc(dW.h, 16)
                    dW.n += 16
        wout_ready = dW.n

        NT8 = tf // 512
        with contextlib.ExitStack() as es2:
            y3c = es2.enter_context(nc.sbuf_tensor("l2_y3c", [128, NCC, 512], BF16))
            outp = es2.enter_context(nc.sbuf_tensor("l2_outp", [128, 8, 512], BF16))
            ostg = [es2.enter_context(nc.sbuf_tensor(f"l2_os{i}", [128, 1024], BF16)) for i in range(2)]
            ps_o = [es2.enter_context(nc.psum_tensor(f"l2_po{i}", [128, 512], F32)) for i in range(4)]
            ps_tp = [es2.enter_context(nc.psum_tensor(f"l2_ptp{i}", [128, 1024], BF16)) for i in range(2)]
            blk = es2.enter_context(nc.Block())
            (sYC0, sYC1, sOM, sOE, sTPm, sTPe, sOS0, sOS1) = C.stage(8)
            sYC = [sYC0, sYC1]
            sOS = [sOS0, sOS1]
            bOM, bOE, bTPm, bTPe = sOM.n, sOE.n, sTPm.n, sTPe.n
            bYC = [sYC0.n, sYC1.n]
            bOS = [sOS0.n, sOS1.n]
            opd2 = op_d.rearrange("(a b) -> a b", b=D)
            y3d2g = y3_d.rearrange("(a b) -> a b", b=tf)
            NTT = 512 // 128

            @blk.sync
            def _(sync):
                scan_done[0].wait(sync, scan_done[1])
                for t8 in range(NT8):
                    if t8 >= 2:
                        sOM.wait(sync, bOM + (t8 - 2) * 8 + 8)
                    for hc in range(NCC):
                        sync.dma_start(out=y3c[:, hc, :],
                                       in_=y3d2g[hc * 128:(hc + 1) * 128, t8 * 512:(t8 + 1) * 512]).then_inc(sYC[t8 % 2].h, 16)
                        sYC[t8 % 2].n += 16
                    for tt in range(NTT):
                        idx = t8 * NTT + tt
                        sTPe.wait(sync, bTPe + idx + 1)
                        sync.dma_start(out=opd2[t8 * 512 + tt * 128:t8 * 512 + (tt + 1) * 128, :],
                                       in_=ostg[idx % 2][:]).then_inc(sOS[idx % 2].h, 16)
                        sOS[idx % 2].n += 16
                sOS0.wait(sync)
                sOS1.wait(sync)

            @blk.tensor
            def _(t):
                dW.wait(t, wout_ready)
                for t8 in range(NT8):
                    sYC[t8 % 2].wait(t, bYC[t8 % 2] + (t8 // 2 + 1) * NCC * 16)
                    for g2 in range(2):
                        for dmc in range(4):
                            mg = g2 * 4 + dmc
                            if t8 >= 1 or g2 >= 1:
                                sOE.wait(t, bOE + (t8 * 2 + g2 - 1) * 4 + dmc + 1)
                            ins = None
                            for hc in range(NCC):
                                ins = t.matmul(ps_o[dmc][:],
                                               wout[:, hc, mg * 128:(mg + 1) * 128],
                                               y3c[:, hc, :],
                                               start=(hc == 0), stop=(hc == NCC - 1))
                            sOM.inc(ins)
                    for tt in range(NTT):
                        idx = t8 * NTT + tt
                        sOE.wait(t, bOE + (t8 + 1) * 8)
                        if idx >= 2:
                            sTPe.wait(t, bTPe + idx - 1)
                        ins = None
                        for mg in range(8):
                            ins = t.transpose(ps_tp[idx % 2][:, mg * 128:(mg + 1) * 128],
                                              outp[:, mg, tt * 128:(tt + 1) * 128],
                                              C.const["ident"][:])
                        sTPm.inc(ins)

            @blk.scalar
            def _(s):
                for t8 in range(NT8):
                    for g2 in range(2):
                        for dmc in range(4):
                            mg = g2 * 4 + dmc
                            sOM.wait(s, bOM + t8 * 8 + g2 * 4 + dmc + 1)
                            if t8 >= 1 and g2 == 0 and dmc == 0:
                                sTPm.wait(s, bTPm + t8 * NTT)
                            ins = s.activation(outp[:, mg, :], ps_o[dmc][:], AF.Copy)
                            ins.then_inc(sOE.h, 1)
                            sOE.n += 1
                    for tt in range(NTT):
                        idx = t8 * NTT + tt
                        sTPm.wait(s, bTPm + idx + 1)
                        if idx >= 2:
                            sOS[idx % 2].wait(s, bOS[idx % 2] + (idx // 2) * 16)
                        ins = s.activation(ostg[idx % 2][:], ps_tp[idx % 2][:], AF.Copy)
                        ins.then_inc(sTPe.h, 1)
                        sTPe.n += 1

        with nc.Block() as blk:
            @blk.gpsimd
            def _(g):
                sCC.inc(g.collective_compute("ReduceScatter", ALU.add, replica_groups=PAIRS,
                                             ins=[op_d.rearrange("(a b) -> a b", b=D)],
                                             outs=[rs_d.rearrange("(a b) -> a b", b=D)]))
        rs_ready = sCC.n

        with contextlib.ExitStack() as es2:
            rsb = [es2.enter_context(nc.sbuf_tensor(f"l2_rs{i}", [128, D], BF16)) for i in range(2)]
            xres = [es2.enter_context(nc.sbuf_tensor(f"l2_fx{i}", [128, D], F32)) for i in range(2)]
            y1r = [es2.enter_context(nc.sbuf_tensor(f"l2_fy{i}", [128, D], F32)) for i in range(2)]
            onew = [es2.enter_context(nc.sbuf_tensor(f"l2_fo{i}", [128, D], F32)) for i in range(2)]
            blk = es2.enter_context(nc.Block())
            (sRL0, sRL1, sXL0b, sXL1b, sYL0, sYL1, sOA, sOSf0, sOSf1) = C.stage(9)
            sRL = [sRL0, sRL1]
            sXLb = [sXL0b, sXL1b]
            sYL = [sYL0, sYL1]
            sOSf = [sOSf0, sOSf1]
            bOA = sOA.n
            bRL = [sRL0.n, sRL1.n]
            bXLb = [sXL0b.n, sXL1b.n]
            bYL = [sYL0.n, sYL1.n]
            bOSf = [sOSf0.n, sOSf1.n]
            rsd2 = rs_d.rearrange("(a b) -> a b", b=D)

            def _store(sync, i):
                sOA.wait(sync, bOA + i + 1)
                sync.dma_start(out=x_out[i * 128:(i + 1) * 128, :],
                               in_=onew[i % 2][:]).then_inc(sOSf[i % 2].h, 16)
                sOSf[i % 2].n += 16

            @blk.sync
            def _(sync):
                sCC.wait(sync, rs_ready)
                for i in range(nt):
                    if i >= 2:
                        sOA.wait(sync, bOA + i - 1)
                    sync.dma_start(out=rsb[i % 2][:], in_=rsd2[i * 128:(i + 1) * 128, :]).then_inc(sRL[i % 2].h, 16)
                    sRL[i % 2].n += 16
                    sync.dma_start(out=xres[i % 2][:], in_=x_in[i * 128:(i + 1) * 128, :]).then_inc(sXLb[i % 2].h, 16)
                    sXLb[i % 2].n += 16
                    sync.dma_start(out=y1r[i % 2][:], in_=y1_dram[i * 128:(i + 1) * 128, :]).then_inc(sYL[i % 2].h, 16)
                    sYL[i % 2].n += 16
                    if i >= 2:
                        _store(sync, i - 2)
                for i in range(max(0, nt - 2), nt):
                    _store(sync, i)
                sOSf0.wait(sync)
                sOSf1.wait(sync)

            @blk.vector
            def _(v):
                for i in range(nt):
                    p = i % 2
                    sRL[p].wait(v, bRL[p] + (i // 2 + 1) * 16)
                    sXLb[p].wait(v, bXLb[p] + (i // 2 + 1) * 16)
                    sYL[p].wait(v, bYL[p] + (i // 2 + 1) * 16)
                    if i >= 2:
                        sOSf[p].wait(v, bOSf[p] + (i // 2) * 16)
                    cV.inc(v.tensor_tensor(onew[p][:], xres[p][:], y1r[p][:], ALU.add))
                    cV.wait(v)
                    sOA.inc(v.tensor_tensor(onew[p][:], onew[p][:], rsb[p][:], ALU.add))
        return [(sOSf0, sOSf0.n), (sOSf1, sOSf1.n)]



def build_l2_attn_program(scale=1):
    C = Core(scale=scale)
    nc = C.nc
    tpc, tf = C.tpc, C.tf
    DI = 2 * D
    x_in = nc.dram_tensor("x", [tpc, D], F32, kind="ExternalInput").ap()
    x_out = nc.dram_tensor("xo", [tpc, D], F32, kind="ExternalOutput").ap()
    W = {
        "W_in": nc.dram_tensor("L2_attn_W_in", [D, 2 * DI], F32, kind="ExternalInput").ap(),
        "W_out": nc.dram_tensor("L2_attn_W_out", [DI, D], F32, kind="ExternalInput").ap(),
        "gamma": nc.dram_tensor("L2_attn_gamma", [D], F32, kind="ExternalInput").ap(),
        "beta": nc.dram_tensor("L2_attn_beta", [D], F32, kind="ExternalInput").ap(),
        "convw_r": nc.dram_tensor("L2a_convw_r", [128, 16, 4], F32, kind="ExternalInput").ap(),
        "convb_r": nc.dram_tensor("L2a_convb_r", [128, 16], F32, kind="ExternalInput").ap(),
        "Alog_my": nc.dram_tensor("L2a_Alog_my", [128, 16, 8], F32, kind="ExternalInput").ap(),
        "wxb": nc.dram_tensor("L2a_wxb", [128, 16, 8], F32, kind="ExternalInput").ap(),
        "wxc": nc.dram_tensor("L2a_wxc", [128, 16, 8], F32, kind="ExternalInput").ap(),
        "wxd": nc.dram_tensor("L2a_wxd", [128, 16, 1], F32, kind="ExternalInput").ap(),
        "Dr": nc.dram_tensor("L2a_Dr", [128, 16], F32, kind="ExternalInput").ap(),
        "sel": nc.dram_tensor("L2a_sel", [128, 1], F32, kind="ExternalInput").ap(),
    }
    y1 = nc.dram_tensor("l2_y1d", [tpc, D], F32).ap()
    xz_src = nc.dram_tensor("l2_xz_src", [2 * DI * tpc], BF16).ap()
    xz_all = nc.dram_tensor("l2_xz_all", [2 * 2 * DI * tpc], BF16).ap()
    xconv_d = nc.dram_tensor("l2_xconv", [DI * tf], BF16).ap()
    y3_d = nc.dram_tensor("l2_y3d", [DI * tf], BF16).ap()
    op_d = nc.dram_tensor("l2_opd", [tf * D], BF16).ap()
    rs_d = nc.dram_tensor("l2_rsd", [tpc * D], BF16).ap()
    with C.es:
        build_consts(C)
        done = l2_attn(C, x_in, x_out, W, y1, xz_src, xz_all, xconv_d, y3_d, op_d, rs_d)
        with nc.Block() as blk:
            @blk.gpsimd
            def _(g):
                for s_, n_ in done:
                    s_.wait(g, n_)
    return C


def l2_derived(inp, core_id):
    DI = 2 * D
    r3 = lambda a: np.ascontiguousarray(a.reshape(16, 128, -1).transpose(1, 0, 2)).astype(np.float32)
    noff = (core_id % 2) * 8
    d = {}
    d["L2a_convw_r"] = r3(inp["L2_attn_conv_w"][:, 0, :])
    d["L2a_convb_r"] = r3(inp["L2_attn_conv_b"][:, None])[:, :, 0]
    d["L2a_Alog_my"] = r3(inp["L2_attn_A_log"])[:, :, noff:noff + 8]
    wx = inp["L2_attn_W_x"]
    d["L2a_wxb"] = r3(wx[:, noff:noff + 8])
    d["L2a_wxc"] = r3(wx[:, 16 + noff:16 + noff + 8])
    d["L2a_wxd"] = r3(wx[:, 32:33])
    d["L2a_Dr"] = r3(inp["L2_attn_D_skip"][:, None])[:, :, 0]
    d["L2a_sel"] = np.full((128, 1), 1.0 if core_id % 2 == 0 else 0.0, np.float32)
    return d




def l3_attn(C, x_in, x_out, W, kv_src, kv_all, wait_first=None):
    nc = C.nc
    tpc, tf, nt = C.tpc, C.tf, C.nt
    H, HD = 16, 64
    NKC = tf // 128
    QW = min(512, tpc)
    NQH = tpc // QW
    sCC = C.sem("cc")
    cV, cS, cG = C.sem("cV"), C.sem("cS"), C.sem("cG")
    dW = C.sem("dW")

    with contextlib.ExitStack() as es:
        gl = es.enter_context(nc.sbuf_tensor("l3_gl", [128, 8], F32))
        bl = es.enter_context(nc.sbuf_tensor("l3_bl", [128, 8], F32))
        lamc = es.enter_context(nc.sbuf_tensor("l3_lam", [128, 1], F32))
        nlam = es.enter_context(nc.sbuf_tensor("l3_nlam", [128, 1], F32))
        oml = es.enter_context(nc.sbuf_tensor("l3_oml", [128, 1], F32))
        ones_cb = es.enter_context(nc.sbuf_tensor("l3_onec", [128, 1], BF16))
        q1T = es.enter_context(nc.sbuf_tensor("l3_q1", [128, ND, tpc], BF16))
        q2T = es.enter_context(nc.sbuf_tensor("l3_q2", [128, ND, tpc], BF16))
        with nc.Block() as blk:
            @blk.gpsimd
            def _(g):
                g.dma_start(out=gl[:], in_=W["gam_r"]).then_inc(dW.h, 16)
                g.dma_start(out=bl[:], in_=W["bet_r"]).then_inc(dW.h, 16)
                g.dma_start(out=lamc[:], in_=W["lam_col"]).then_inc(dW.h, 16)
                dW.n += 48
                dW.wait(g)
                g.memset(ones_cb[:], 1.0)
                cG.inc(g.engine_nop())

            @blk.vector
            def _(v):
                cG.wait(v)
                cV.inc(v.tensor_scalar(nlam[:], lamc[:], -1.0, None, ALU.mult))
                cV.wait(v)
                cV.inc(v.tensor_scalar(oml[:], lamc[:], -1.0, 1.0, ALU.mult, ALU.add))
                cV.wait(v)
                cV.inc(v.tensor_scalar(gl[:], gl[:], oml[:, 0:1], None, ALU.mult))
                cV.wait(v)
                cV.inc(v.tensor_scalar(bl[:], bl[:], oml[:, 0:1], None, ALU.mult))
                cV.wait(v)
        pre_done = (cV, cV.n, cG, cG.n)
        _sq = nc.enter_named_scope("l3_qkv", False)[0]

        es_qkv = contextlib.ExitStack()
        w5 = es_qkv.enter_context(nc.sbuf_tensor("l3_w5", [128, ND, 5 * D], BF16))
        xt = es_qkv.enter_context(nc.sbuf_tensor("l3_xt", [128, ND, tpc], BF16))
        bW5 = dW.n
        with nc.Block() as blk:
            @blk.gpsimd
            def _(g):
                for wi, wn in enumerate(["wq1c", "wk1c", "wq2c", "wk2c", "wvc"]):
                    for dc in range(ND):
                        g.dma_start(out=w5[:, dc, wi * D:(wi + 1) * D],
                                    in_=W[wn][dc * 128:(dc + 1) * 128, :]).then_inc(dW.h, 16)
                        dW.n += 16
        w5_ready = dW.n

        xt_sem, xt_n = ln_to_xt(C, x_in, xt, "l3a", wait_first=wait_first)

        NH2 = tpc // QW
        NQK = 4 * ND * NH2
        with contextlib.ExitStack() as es2:
            ps_qk = [es2.enter_context(nc.psum_tensor(f"l3pqk{i}", [128, QW], F32)) for i in range(2)]
            ps_v = [es2.enter_context(nc.psum_tensor(f"l3pv{i}", [128, min(D, 1024)], F32)) for i in range(2)]
            stage = [es2.enter_context(nc.sbuf_tensor(f"l3st{i}", [128, max(QW, D)], BF16)) for i in range(2)]
            blk = es2.enter_context(nc.Block())
            sMM, sEVq, sSTa, sSTb = C.stage(4)
            sST = [sSTa, sSTb]
            bMM, bEVq = sMM.n, sEVq.n
            bST = [sSTa.n, sSTb.n]

            @blk.tensor
            def _(t):
                dW.wait(t, w5_ready)
                xt_sem.wait(t, xt_n)
                for st in range(NQK):
                    cc, hf = st // NH2, st % NH2
                    if st >= 2:
                        sEVq.wait(t, bEVq + st - 1)
                    pb = ps_qk[st % 2]
                    ins = None
                    for dc in range(ND):
                        ins = t.matmul(pb[:, 0:QW],
                                       w5[:, dc, cc * 128:(cc + 1) * 128],
                                       xt[:, dc, hf * QW:(hf + 1) * QW],
                                       start=(dc == 0), stop=(dc == ND - 1))
                    sMM.inc(ins)
                for i in range(nt):
                    if NQK + i >= 2:
                        sEVq.wait(t, bEVq + NQK + i - 1)
                    pb = ps_v[i % 2]
                    ins = None
                    for dc in range(ND):
                        for j in range(D // 512):
                            ins = t.matmul(pb[:, j * 512:(j + 1) * 512],
                                           xt[:, dc, i * 128:(i + 1) * 128],
                                           w5[:, dc, 4 * D + j * 512:4 * D + (j + 1) * 512],
                                           start=(dc == 0), stop=(dc == ND - 1))
                    sMM.inc(ins)

            @blk.scalar
            def _(s):
                for st in range(NQK):
                    cc, hf = st // NH2, st % NH2
                    wsel = cc // ND
                    ccl = cc % ND
                    sMM.wait(s, bMM + st + 1)
                    if wsel == 0:
                        ins = s.activation(q1T[:, ccl, hf * QW:(hf + 1) * QW],
                                           ps_qk[st % 2][:], AF.Copy)
                    elif wsel == 2:
                        ins = s.activation(q2T[:, ccl, hf * QW:(hf + 1) * QW],
                                           ps_qk[st % 2][:], AF.Copy)
                    else:
                        sidx = _l3_sidx(st, NH2)
                        if sidx >= 2:
                            q = sidx % 2
                            sST[q].wait(s, bST[q] + (sidx // 2) * 16)
                        ins = s.activation(stage[st % 2][:, :QW], ps_qk[st % 2][:], AF.Copy)
                    sEVq.inc(ins)
                for i in range(nt):
                    sMM.wait(s, bMM + NQK + i + 1)
                    sidx = 2 * ND * NH2 + i
                    if sidx >= 2:
                        q = sidx % 2
                        sST[q].wait(s, bST[q] + (sidx // 2) * 16)
                    ins = s.activation(stage[i % 2][:, :D], ps_v[i % 2][:], AF.Copy)
                    sEVq.inc(ins)

            @blk.sync
            def _(sync):
                k1v = kv_src[0:D * tpc].rearrange("(p t) -> p t", t=tpc)
                k2v = kv_src[D * tpc:2 * D * tpc].rearrange("(p t) -> p t", t=tpc)
                vv = kv_src[2 * D * tpc:3 * D * tpc].rearrange("(p t) -> p t", t=D)
                for st in range(NQK):
                    cc, hf = st // NH2, st % NH2
                    wsel = cc // ND
                    ccl = cc % ND
                    if wsel not in (1, 3):
                        continue
                    sidx = _l3_sidx(st, NH2)
                    tgt = k1v if wsel == 1 else k2v
                    sEVq.wait(sync, bEVq + st + 1)
                    sync.dma_start(out=tgt[ccl * 128:(ccl + 1) * 128, hf * QW:(hf + 1) * QW],
                                   in_=stage[st % 2][:, :QW]).then_inc(sST[sidx % 2].h, 16)
                    sST[sidx % 2].n += 16
                for i in range(nt):
                    sidx = 2 * ND * NH2 + i
                    sEVq.wait(sync, bEVq + NQK + i + 1)
                    sync.dma_start(out=vv[i * 128:(i + 1) * 128, :],
                                   in_=stage[i % 2][:, :D]).then_inc(sST[sidx % 2].h, 16)
                    sST[sidx % 2].n += 16
                sSTa.wait(sync)
                sSTb.wait(sync)
        kv_written = [(sSTa, sSTa.n), (sSTb, sSTb.n)]
        es_qkv.close()
        nc.leave_named_scope("l3_qkv", _sq, False)
        attnT = es.enter_context(nc.sbuf_tensor("l3_at", [128, ND, tpc], BF16))
        _sg = nc.enter_named_scope("l3_ag", False)[0]

        with nc.Block() as blk:
            @blk.gpsimd
            def _(g):
                for s_, n_ in kv_written:
                    s_.wait(g, n_)
                sCC.inc(g.collective_compute("AllGather", ALU.bypass, replica_groups=PAIRS,
                                             ins=[kv_src.rearrange("(a b) -> a b", b=D)],
                                             outs=[kv_all.rearrange("(a b) -> a b", b=D)]))
        kv_ready = sCC.n
        nc.leave_named_scope("l3_ag", _sg, False)
        _sa = nc.enter_named_scope("l3_at", False)[0]

        v65 = es.enter_context(nc.sbuf_tensor("l3_v65", [128, tf // 128, H, HD + 1], BF16))
        NVT = tf // 128
        half_t = NVT // 2
        BL3 = 3 * tpc * D
        with contextlib.ExitStack() as es2:
            vstage = [es2.enter_context(nc.sbuf_tensor(f"l3vst{i}", [128, D], BF16)) for i in range(2)]
            kth1 = es2.enter_context(nc.sbuf_tensor("l3kt1", [128, tf], BF16))
            kth2 = es2.enter_context(nc.sbuf_tensor("l3kt2", [128, tf], BF16))
            probs = [es2.enter_context(nc.sbuf_tensor(f"l3pr{i}", [128, 2 * QW], BF16)) for i in range(2)]
            usb = es2.enter_context(nc.sbuf_tensor("l3usb", [64, QW], F32))
            u2sb = es2.enter_context(nc.sbuf_tensor("l3u2sb", [64, QW], F32))
            rrow = es2.enter_context(nc.sbuf_tensor("l3rr", [1, QW], F32))
            attnT64 = es2.enter_context(nc.sbuf_tensor("l3at64", [64, 2, tpc], BF16))
            ps_sc = [es2.enter_context(nc.psum_tensor(f"l3psc{i}", [128, 2 * QW], F32)) for i in range(2)]
            ps_u1 = es2.enter_context(nc.psum_tensor("l3psu1", [65, QW], F32))
            ps_u2 = es2.enter_context(nc.psum_tensor("l3psu2", [65, QW], F32))
            ps_rb = es2.enter_context(nc.psum_tensor("l3psrb", [64, QW], F32))
            blk = es2.enter_context(nc.Block())
            (sVL0, sVL1, sVC, sVO, sKL0, sKL1, sSC, sEX, sAV, sRC, sRB, sUE, sAT) = C.stage(13)
            sVL = [sVL0, sVL1]
            sKL = [sKL0, sKL1]
            sMG = [C.sem("dM0"), C.sem("dM1")]
            (bVC, bVO, bSC, bEX, bAV, bRC, bRB, bUE, bAT) = (
                s.n for s in (sVC, sVO, sSC, sEX, sAV, sRC, sRB, sUE, sAT))
            bVL = [sVL0.n, sVL1.n]
            bKL = [sKL0.n, sKL1.n]
            bMG = [sMG[0].n, sMG[1].n]

            @blk.sync
            def _(sync):
                sCC.wait(sync, kv_ready)
                for i in range(NVT):
                    if i >= 2:
                        sVC.wait(sync, bVC + i - 1)
                    r, ii = (0, i) if i < half_t else (1, i - half_t)
                    src = kv_all[r * BL3 + 2 * D * tpc + ii * 128 * D:
                                 r * BL3 + 2 * D * tpc + (ii + 1) * 128 * D]
                    sync.dma_start(out=vstage[i % 2][:],
                                   in_=src.rearrange("(p t) -> p t", t=D)).then_inc(sVL[i % 2].h, 16)
                    sVL[i % 2].n += 16
                KP = NKC // 2
                for h in range(H):
                    if h >= 2:
                        sSC.wait(sync, bSC + (h - 1) * NQH * 2 * KP)
                    for ki, kth in ((0, kth1), (1, kth2)):
                        for r in range(2):
                            src = kv_all[r * BL3 + ki * D * tpc + h * HD * tpc:
                                         r * BL3 + ki * D * tpc + (h + 1) * HD * tpc]
                            sync.dma_start(out=kth[(h % 2) * 64:(h % 2) * 64 + 64, r * tpc:(r + 1) * tpc],
                                           in_=src.rearrange("(p t) -> p t", t=tpc)).then_inc(sKL[h % 2].h, 16)
                            sKL[h % 2].n += 16

            @blk.gpsimd
            def _(g):
                sVO.inc(g.memset(v65[:, :, :, HD:HD + 1], 1.0))
                for h in range(H):
                    sAT.wait(g, bAT + (h + 1) * NQH)
                    g.dma_start(out=attnT[(h % 2) * 64:(h % 2) * 64 + 64, h // 2, :],
                                in_=attnT64[:, h % 2, :]).then_inc(sMG[h % 2].h, 16)
                    sMG[h % 2].n += 16
                sMG[0].wait(g)
                sMG[1].wait(g)

            @blk.vector
            def _(v):
                for i in range(NVT):
                    sVL[i % 2].wait(v, bVL[i % 2] + (i // 2 + 1) * 16)
                    sVC.inc(v.tensor_copy(v65[:, i, :, 0:HD],
                                          vstage[i % 2][:].rearrange("p (h e) -> p h e", h=H)))
                for h in range(H):
                    for qh in range(NQH):
                        hq = h * NQH + qh
                        sAV.wait(v, bAV + (hq + 1) * 2 * (NKC // 2))
                        if hq >= 1:
                            sRB.wait(v, bRB + 2 * hq)
                        sRC.inc(v.reciprocal(rrow[:], ps_u1[64:65, :]))
                        sUE.wait(v, bUE + 2 * hq + 1)
                        sRB.wait(v, bRB + 2 * hq + 1)
                        cV.inc(v.tensor_tensor(usb[:], usb[:], ps_rb[:], ALU.mult))
                        cV.wait(v)
                        sRC.inc(v.reciprocal(rrow[:], ps_u2[64:65, :]))
                        sUE.wait(v, bUE + 2 * hq + 2)
                        sRB.wait(v, bRB + 2 * hq + 2)
                        cV.inc(v.tensor_tensor(u2sb[:], u2sb[:], ps_rb[:], ALU.mult))
                        cV.wait(v)
                        if qh == 0 and h >= 2:
                            sMG[h % 2].wait(v, bMG[h % 2] + (h // 2) * 16)
                        dst = attnT64[:, h % 2, qh * QW:(qh + 1) * QW]
                        sAT.inc(v.scalar_tensor_tensor(dst, u2sb[:], nlam[0:64, 0:1], usb[:],
                                                       ALU.mult, ALU.add))

            @blk.tensor
            def _(t):
                for s_, n_ in [pre_done[:2], pre_done[2:]]:
                    s_.wait(t, n_)
                sVC.wait(t, bVC + NVT)
                sVO.wait(t, bVO + 1)
                KP = NKC // 2

                def _av(t, pgp, pkp, h, psu, hq, pass2):
                    sEX.wait(t, bEX + pgp + 1)
                    if pkp == 0 and hq >= 1:
                        sRC.wait(t, bRC + (hq - 1) * 2 + 1 + pass2)
                        sUE.wait(t, bUE + hq * 2 + pass2)
                    ins = None
                    for half in range(2):
                        kc = 2 * pkp + half
                        ins = t.matmul(psu[:, 0:QW],
                                       v65[:, kc, h, :],
                                       probs[pgp % 2][:, half * QW:(half + 1) * QW],
                                       start=(kc == 0), stop=(kc == NKC - 1))
                    sAV.inc(ins)

                gp = 0
                for h in range(H):
                    sKL[h % 2].wait(t, bKL[h % 2] + (h // 2 + 1) * 64)
                    for qh in range(NQH):
                        hq = h * NQH + qh
                        for pass2, (kth, qT, psu) in enumerate([(kth1, q1T, ps_u1), (kth2, q2T, ps_u2)]):
                            qrows = qT[(h % 2) * 64:(h % 2) * 64 + 64, h // 2, :]
                            for kp in range(KP):
                                if gp >= 2:
                                    sEX.wait(t, bEX + gp - 1)
                                pb = ps_sc[gp % 2]
                                ins = None
                                for half in range(2):
                                    kc = 2 * kp + half
                                    ins = t.matmul(pb[:, half * QW:(half + 1) * QW],
                                                   kth[(h % 2) * 64:(h % 2) * 64 + 64, kc * 128:(kc + 1) * 128],
                                                   qrows[:, qh * QW:(qh + 1) * QW],
                                                   start=True, stop=True)
                                sSC.inc(ins)
                                if kp >= 1:
                                    _av(t, gp - 1, kp - 1, h, psu, hq, pass2)
                                gp += 1
                            _av(t, gp - 1, KP - 1, h, psu, hq, pass2)
                        for pass2 in range(2):
                            sRC.wait(t, bRC + hq * 2 + pass2 + 1)
                            if pass2 == 0 and hq >= 1:
                                sAT.wait(t, bAT + hq)
                            ins = t.matmul(ps_rb[:, 0:QW],
                                           C.const["ones_row_f"][:, 0:64],
                                           rrow[:, 0:QW], start=True, stop=True)
                            sRB.inc(ins)

            @blk.scalar
            def _(s):
                KP = NKC // 2
                gp = 0
                for h in range(H):
                    for qh in range(NQH):
                        hq = h * NQH + qh
                        for pass2 in range(2):
                            for kp in range(KP):
                                sSC.wait(s, bSC + gp + 1)
                                if gp >= 2:
                                    sAV.wait(s, bAV + gp - 1)
                                sEX.inc(s.activation(probs[gp % 2][:], ps_sc[gp % 2][:],
                                                     AF.Exp, bias=-SHIFT, scale=1.0 / 8.0))
                                gp += 1
                            sAV.wait(s, bAV + gp)
                            if hq * 2 + pass2 >= 1:
                                sAT.wait(s, bAT + hq)
                            tgt = usb if pass2 == 0 else u2sb
                            psu = ps_u1 if pass2 == 0 else ps_u2
                            sUE.inc(s.activation(tgt[:], psu[0:64, :], AF.Copy))
        attn_done = [(sMG[0], sMG[0].n), (sMG[1], sMG[1].n)]

        nc.leave_named_scope("l3_at", _sa, False)
        _sl = nc.enter_named_scope("l3_ln", False)[0]
        JW3 = min(512, tpc)
        NJ = tpc // JW3
        NTT3 = JW3 // 128
        with contextlib.ExitStack() as es2:
            sq_sb = es2.enter_context(nc.sbuf_tensor("l3_sq", [128, JW3], BF16))
            rowst = es2.enter_context(nc.sbuf_tensor("l3_rst", [1, 2 * JW3], F32))
            rbr = es2.enter_context(nc.sbuf_tensor("l3_rbr", [128, JW3], F32))
            rbmu = es2.enter_context(nc.sbuf_tensor("l3_rbmu", [128, JW3], F32))
            tbuf = es2.enter_context(nc.sbuf_tensor("l3_tb", [128, JW3], F32))
            obf = es2.enter_context(nc.sbuf_tensor("l3_obf", [128, 8, JW3], BF16))
            xres = [es2.enter_context(nc.sbuf_tensor(f"l3_fx{i}", [128, D], F32)) for i in range(2)]
            onew = [es2.enter_context(nc.sbuf_tensor(f"l3_fo{i}", [128, D], F32)) for i in range(2)]
            ps_s1 = es2.enter_context(nc.psum_tensor("l3_ps1", [1, JW3], F32))
            ps_s2 = es2.enter_context(nc.psum_tensor("l3_ps2", [1, JW3], F32))
            ps_rb2 = [es2.enter_context(nc.psum_tensor(f"l3_prb{i}", [128, JW3], F32)) for i in range(2)]
            ps_tp = [es2.enter_context(nc.psum_tensor(f"l3_ptp{i}", [128, 1024], BF16)) for i in range(2)]
            blk = es2.enter_context(nc.Block())
            (sSQ, sSM, sRW, sRBM, sRBE, sAP, sTPm, sOA, sXLa, sXLb, sOSa, sOSb) = C.stage(12)
            sXL = [sXLa, sXLb]
            sOS = [sOSa, sOSb]
            (bSQ, bSM, bRW, bRBM, bRBE, bAP, bTPm, bOA) = (
                s.n for s in (sSQ, sSM, sRW, sRBM, sRBE, sAP, sTPm, sOA))
            bXL = [sXLa.n, sXLb.n]
            bOS = [sOSa.n, sOSb.n]

            @blk.scalar
            def _(s):
                for j in range(NJ):
                    for dc in range(ND):
                        k = j * ND + dc
                        for s_, n_ in attn_done:
                            s_.wait(s, n_)
                        if k >= 1:
                            sSM.wait(s, bSM + 2 * k)
                        ins = s.activation(sq_sb[:], attnT[:, dc, j * JW3:(j + 1) * JW3], AF.Square)
                        ins.then_inc(sSQ.h, 1)
                        sSQ.n += 1
                    sSM.wait(s, bSM + 2 * (j * ND + ND))
                    sRW.wait(s, bRW + j * 3 + 1)
                    cS.inc(s.activation(rowst[:, 0:JW3], rowst[:, 0:JW3], AF.Ln, bias=LN_EPS))
                    cS.wait(s)
                    ins = s.activation(rowst[:, 0:JW3], rowst[:, 0:JW3], AF.Exp, scale=-0.5)
                    ins.then_inc(sRW.h, 1)
                    sRW.n += 1

            @blk.tensor
            def _(t):
                for j in range(NJ):
                    if j >= 1:
                        sRW.wait(t, bRW + (j - 1) * 3 + 1)
                    for dc in range(ND):
                        k = j * ND + dc
                        sSQ.wait(t, bSQ + k + 1)
                        ins = t.matmul(ps_s1[0:1, :],
                                       ones_cb[:, 0:1],
                                       attnT[:, dc, j * JW3:(j + 1) * JW3],
                                       start=(dc == 0), stop=(dc == ND - 1))
                        sSM.inc(ins)
                        ins = t.matmul(ps_s2[0:1, :],
                                       ones_cb[:, 0:1],
                                       sq_sb[:],
                                       start=(dc == 0), stop=(dc == ND - 1))
                        sSM.inc(ins)
                    sRW.wait(t, bRW + j * 3 + 3)
                    if j >= 1:
                        sRBE.wait(t, bRBE + j * 2)
                    ins = t.matmul(ps_rb2[0][:, 0:JW3], C.const["ones_row_f"][:, :],
                                   rowst[:, 0:JW3], start=True, stop=True)
                    sRBM.inc(ins)
                    ins = t.matmul(ps_rb2[1][:, 0:JW3], C.const["ones_row_f"][:, :],
                                   rowst[:, JW3:2 * JW3], start=True, stop=True)
                    sRBM.inc(ins)
                    for tt in range(NTT3):
                        i = j * NTT3 + tt
                        sAP.wait(t, bAP + j * ND + ND) if tt == 0 else None
                        if i >= 2:
                            sOA.wait(t, bOA + i - 1)
                        ins = None
                        for dc in range(ND):
                            ins = t.transpose(ps_tp[i % 2][:, dc * 128:(dc + 1) * 128],
                                              obf[:, dc, tt * 128:(tt + 1) * 128],
                                              C.const["ident"][:])
                        sTPm.inc(ins)

            @blk.vector
            def _(v):
                for j in range(NJ):
                    sSM.wait(v, bSM + 2 * (j * ND + ND))
                    cV.inc(v.tensor_scalar(rowst[:, JW3:2 * JW3], ps_s1[0:1, :], 1.0 / D, None, ALU.mult))
                    cV.wait(v)
                    cV.inc(v.tensor_scalar(tbuf[0:1, 0:JW3], rowst[:, JW3:2 * JW3], None, None, ALU.bypass)) if False else None
                    cV.inc(v.tensor_scalar(rowst[:, 0:JW3], ps_s2[0:1, :], 1.0 / D, None, ALU.mult))
                    cV.wait(v)
                    cV.inc(v.tensor_tensor(tbuf[0:1, 0:JW3], rowst[:, JW3:2 * JW3], rowst[:, JW3:2 * JW3], ALU.mult))
                    cV.wait(v)
                    cV.inc(v.tensor_tensor(rowst[:, 0:JW3], rowst[:, 0:JW3], tbuf[0:1, 0:JW3], ALU.subtract))
                    cV.wait(v)
                    v.engine_nop().then_inc(sRW.h, 1)
                    sRW.n += 1
                    sRW.wait(v, bRW + j * 3 + 2)
                    cV.inc(v.tensor_tensor(rowst[:, JW3:2 * JW3], rowst[:, JW3:2 * JW3], rowst[:, 0:JW3], ALU.mult))
                    cV.wait(v)
                    v.engine_nop().then_inc(sRW.h, 1)
                    sRW.n += 1
                    sRBM.wait(v, bRBM + j * 2 + 2)
                    cV.inc(v.tensor_copy(rbr[:], ps_rb2[0][:]))
                    cV.wait(v)
                    cV.inc(v.tensor_copy(rbmu[:], ps_rb2[1][:]))
                    cV.wait(v)
                    v.engine_nop().then_inc(sRBE.h, 2)
                    sRBE.n += 2
                    for dc in range(ND):
                        k = j * ND + dc
                        cV.inc(v.tensor_tensor(tbuf[:], attnT[:, dc, j * JW3:(j + 1) * JW3], rbr[:], ALU.mult))
                        cV.wait(v)
                        cV.inc(v.tensor_tensor(tbuf[:], tbuf[:], rbmu[:], ALU.subtract))
                        cV.wait(v)
                        cV.inc(v.tensor_scalar(obf[:, dc, :], tbuf[:], gl[:, dc:dc + 1], bl[:, dc:dc + 1],
                                               ALU.mult, ALU.add))
                        cV.wait(v)
                        v.engine_nop().then_inc(sAP.h, 1)
                        sAP.n += 1
                    for tt in range(NTT3):
                        i = j * NTT3 + tt
                        sTPm.wait(v, bTPm + i + 1)
                        sXL[i % 2].wait(v, bXL[i % 2] + (i // 2 + 1) * 16)
                        if i >= 2:
                            sOS[i % 2].wait(v, bOS[i % 2] + (i // 2) * 16)
                        sOA.inc(v.tensor_tensor(onew[i % 2][:], xres[i % 2][:], ps_tp[i % 2][:], ALU.add))

            @blk.sync
            def _(sync):
                def _store(sync, i):
                    sOA.wait(sync, bOA + i + 1)
                    sync.dma_start(out=x_out[i * 128:(i + 1) * 128, :],
                                   in_=onew[i % 2][:]).then_inc(sOS[i % 2].h, 16)
                    sOS[i % 2].n += 16
                for i in range(nt):
                    if i >= 2:
                        sOA.wait(sync, bOA + i - 1)
                    sync.dma_start(out=xres[i % 2][:],
                                   in_=x_in[i * 128:(i + 1) * 128, :]).then_inc(sXL[i % 2].h, 16)
                    sXL[i % 2].n += 16
                    if i >= 2:
                        _store(sync, i - 2)
                for i in range(max(0, nt - 2), nt):
                    _store(sync, i)
                sOSa.wait(sync)
                sOSb.wait(sync)
        nc.leave_named_scope("l3_ln", _sl, False)
        return [(sOSa, sOSa.n), (sOSb, sOSb.n)]


def _l3_sidx(st, NH2):
    cc, hf = st // NH2, st % NH2
    wsel = cc // ND
    ccl = cc % ND
    if wsel == 1:
        return (ccl * NH2 + hf)
    if wsel == 3:
        return ND * NH2 + (ccl * NH2 + hf)
    return -1


def build_l3_attn_program(scale=1):
    C = Core(scale=scale)
    nc = C.nc
    tpc = C.tpc
    x_in = nc.dram_tensor("x", [tpc, D], F32, kind="ExternalInput").ap()
    x_out = nc.dram_tensor("xo", [tpc, D], F32, kind="ExternalOutput").ap()
    W = {k: nc.dram_tensor(f"L3a_{k}", [D, D], F32, kind="ExternalInput").ap()
         for k in ["wq1c", "wk1c", "wq2c", "wk2c", "wvc"]}
    W["gam_r"] = nc.dram_tensor("L3a_gam_r", [128, 8], F32, kind="ExternalInput").ap()
    W["bet_r"] = nc.dram_tensor("L3a_bet_r", [128, 8], F32, kind="ExternalInput").ap()
    W["lam_col"] = nc.dram_tensor("L3a_lam_col", [128, 1], F32, kind="ExternalInput").ap()
    kv_src = nc.dram_tensor("l3_kv_src", [3 * tpc * D], BF16).ap()
    kv_all = nc.dram_tensor("l3_kv_all", [2 * 3 * tpc * D], BF16).ap()
    with C.es:
        build_consts(C)
        done = l3_attn(C, x_in, x_out, W, kv_src, kv_all)
        with nc.Block() as blk:
            @blk.gpsimd
            def _(g):
                for s_, n_ in done:
                    s_.wait(g, n_)
    return C


def l3_derived(inp):
    d = {}
    for k, nm in [("wq1c", "L3_attn_wq1"), ("wk1c", "L3_attn_wk1"),
                  ("wq2c", "L3_attn_wq2"), ("wk2c", "L3_attn_wk2"),
                  ("wvc", "L3_attn_wv")]:
        d["L3a_" + k] = np.ascontiguousarray(
            inp[nm].transpose(1, 0, 2).reshape(D, D)).astype(np.float32)
    r = lambda b: np.ascontiguousarray(b.reshape(8, 128).T)
    d["L3a_gam_r"] = r(inp["L3_attn_ln_gamma"]).astype(np.float32)
    d["L3a_bet_r"] = r(inp["L3_attn_ln_beta"]).astype(np.float32)
    d["L3a_lam_col"] = np.full((128, 1), float(np.asarray(inp["L3_attn_lamb"])), np.float32)
    return d


def build_full_program(scale=1):
    C = Core(scale=scale)
    nc = C.nc
    tpc, tf = C.tpc, C.tf
    DI = 2 * D
    t = nc.dram_tensor
    x_in = t("x", [tpc, D], F32, kind="ExternalInput").ap()
    x_out = t("xo", [tpc, D], F32, kind="ExternalOutput").ap()
    xh = [t(f"xh{i}", [tpc, D], F32).ap() for i in range(7)]

    W0 = {
        "w_qkv": t("L0_attn_w_qkv", [D, 3 * D], F32, kind="ExternalInput").ap(),
        "w_out": t("L0_attn_w_out", [D, D], F32, kind="ExternalInput").ap(),
        "b_out": t("L0_attn_b_out", [D], F32, kind="ExternalInput").ap(),
    }
    W1 = {
        "wqkvT": t("L1a_wqkvT", [D, 3 * D], F32, kind="ExternalInput").ap(),
        "woutT": t("L1a_woutT", [D, D], F32, kind="ExternalInput").ap(),
        "ln_w": t("L1_attn_ln_w", [D], F32, kind="ExternalInput").ap(),
        "ln_b": t("L1_attn_ln_b", [D], F32, kind="ExternalInput").ap(),
        "sel": t("L1a_sel", [128, 128], F32, kind="ExternalInput").ap(),
    }
    W2 = {
        "W_in": t("L2_attn_W_in", [D, 2 * DI], F32, kind="ExternalInput").ap(),
        "W_out": t("L2_attn_W_out", [DI, D], F32, kind="ExternalInput").ap(),
        "gamma": t("L2_attn_gamma", [D], F32, kind="ExternalInput").ap(),
        "beta": t("L2_attn_beta", [D], F32, kind="ExternalInput").ap(),
        "convw_r": t("L2a_convw_r", [128, 16, 4], F32, kind="ExternalInput").ap(),
        "convb_r": t("L2a_convb_r", [128, 16], F32, kind="ExternalInput").ap(),
        "Alog_my": t("L2a_Alog_my", [128, 16, 8], F32, kind="ExternalInput").ap(),
        "wxb": t("L2a_wxb", [128, 16, 8], F32, kind="ExternalInput").ap(),
        "wxc": t("L2a_wxc", [128, 16, 8], F32, kind="ExternalInput").ap(),
        "wxd": t("L2a_wxd", [128, 16, 1], F32, kind="ExternalInput").ap(),
        "Dr": t("L2a_Dr", [128, 16], F32, kind="ExternalInput").ap(),
        "sel": t("L2a_sel", [128, 1], F32, kind="ExternalInput").ap(),
    }
    W3 = {k: t(f"L3a_{k}", [D, D], F32, kind="ExternalInput").ap()
          for k in ["wq1c", "wk1c", "wq2c", "wk2c", "wvc"]}
    W3["gam_r"] = t("L3a_gam_r", [128, 8], F32, kind="ExternalInput").ap()
    W3["bet_r"] = t("L3a_bet_r", [128, 8], F32, kind="ExternalInput").ap()
    W3["lam_col"] = t("L3a_lam_col", [128, 1], F32, kind="ExternalInput").ap()

    kv0_src = t("l0_kv_src", [2 * tpc * D], BF16).ap()
    kv0_all = t("l0_kv_all", [2 * 2 * tpc * D], BF16).ap()
    y1_1 = t("l1_y1d", [tpc, D], F32).ap()
    kv1_src = t("l1_kv_src", [tpc * 2 * D], BF16).ap()
    kv1_all = t("l1_kv_all", [8 * tpc * 2 * D], BF16).ap()
    y1_2 = t("l2_y1d", [tpc, D], F32).ap()
    xz_src = t("l2_xz_src", [2 * DI * tpc], BF16).ap()
    xz_all = t("l2_xz_all", [2 * 2 * DI * tpc], BF16).ap()
    xconv_d = t("l2_xconv", [DI * tf], BF16).ap()
    y3_d = t("l2_y3d", [DI * tf], BF16).ap()
    op_d = t("l2_opd", [tf * D], BF16).ap()
    rs_d = t("l2_rsd", [tpc * D], BF16).ap()
    kv3_src = t("l3_kv_src", [3 * tpc * D], BF16).ap()
    kv3_all = t("l3_kv_all", [2 * 3 * tpc * D], BF16).ap()

    with C.es:
        build_consts(C)
        cfg0 = moe_cfg(C, "l0_ffn")
        cfg1 = moe_cfg(C, "l1_ffn")
        cfg2 = moe_cfg(C, "l2_ffn")
        cfg3 = moe_cfg(C, "l3_ffn")
        with nc.named_scope("b_l0a"):
            d = l0_attn(C, x_in, xh[0], W0, kv0_src, kv0_all)
        with nc.named_scope("b_l0f"):
            d = moe_ffn(C, xh[0], xh[1], cfg0, wait_first=list(d))
        with nc.named_scope("b_l1a"):
            d = l1_attn(C, xh[1], xh[2], W1, y1_1, kv1_src, kv1_all, wait_first=list(d))
        with nc.named_scope("b_l1f"):
            d = moe_ffn(C, xh[2], xh[3], cfg1, wait_first=list(d))
        with nc.named_scope("b_l2a"):
            d = l2_attn(C, xh[3], xh[4], W2, y1_2, xz_src, xz_all, xconv_d, y3_d,
                        op_d, rs_d, wait_first=list(d))
        with nc.named_scope("b_l2f"):
            d = moe_ffn(C, xh[4], xh[5], cfg2, wait_first=list(d))
        with nc.named_scope("b_l3a"):
            d = l3_attn(C, xh[5], xh[6], W3, kv3_src, kv3_all, wait_first=list(d))
        with nc.named_scope("b_l3f"):
            d = moe_ffn(C, xh[6], x_out, cfg3, wait_first=list(d))
        with nc.Block() as blk:
            @blk.gpsimd
            def _(g):
                for s_, n_ in d:
                    s_.wait(g, n_)
    return C


def _ext_names(C):
    import concourse.mybir as mb
    names = set()
    for alloc in C.nc.m.functions[0].allocations:
        if isinstance(alloc, mb.MemoryLocationSet) and alloc.kind == "ExternalInput":
            for mem in alloc.memorylocations:
                names.add(mem.name)
    return names


def _l1a_sel(hf):
    sel = np.zeros((128, 128), np.float32)
    for h in range(16):
        for j in range(8):
            sel[:, h * 8 + j] = 1.0 if (j % 2) == hf else 0.0
    return sel


def kernel(**inputs):
    from concourse.bass_utils import run_bass_kernel_spmd
    import ml_dtypes
    x = np.asarray(inputs["L0_attn_x"], np.float32)
    pool = {k: np.asarray(v) for k, v in inputs.items()}
    pool.update(prep_derived(inputs))
    pool.update(l3_derived(inputs))
    pool["c_ident"] = np.eye(128, dtype=ml_dtypes.bfloat16)
    pool["c_identf"] = np.eye(128, dtype=np.float32)
    pool["L1a_wqkvT"] = np.ascontiguousarray(np.asarray(inputs["L1_attn_w_qkv"]).T).astype(np.float32)
    pool["L1a_woutT"] = np.ascontiguousarray(np.asarray(inputs["L1_attn_w_out"]).T).astype(np.float32)
    percore = []
    for c in range(8):
        d = l2_derived(inputs, c)
        d["L1a_sel"] = _l1a_sel(c % 2)
        percore.append(d)
    C = build_full_program()
    names = _ext_names(C)
    in_maps = []
    for c in range(8):
        m = {"x": np.ascontiguousarray(x[c // 2, (c % 2) * TPC:(c % 2 + 1) * TPC])}
        for k in names:
            if k == "x" or k == "partition_id":
                continue
            m[k] = percore[c][k] if k in percore[c] else pool[k]
        in_maps.append(m)
    res = run_bass_kernel_spmd(C.nc, in_maps, core_ids=list(range(8)))
    out = np.zeros((4, 4096, 1024), np.float32)
    for c in range(8):
        out[c // 2, (c % 2) * TPC:(c % 2 + 1) * TPC] = np.asarray(res.results[c]["xo"])
    return out


if __name__ == "__main__":
    C = build_l0_attn_program(scale=16)
    print("built ok")
